# version 14
# baseline (speedup 1.0000x reference)
import contextlib
import numpy as np
import ml_dtypes
import concourse.bass as bass
import concourse.mybir as mybir
from concourse.bass import IndirectOffsetOnAxis
from concourse.bass_utils import run_bass_kernel_spmd

F32 = mybir.dt.float32
BF16 = mybir.dt.bfloat16
I32 = mybir.dt.int32
U8 = mybir.dt.uint8
I8 = mybir.dt.int8
AF = mybir.ActivationFunctionType
ALU = mybir.AluOpType
AX = mybir.AxisListType
BF = ml_dtypes.bfloat16

ENGINES = ("sync", "tensor", "vector", "scalar", "gpsimd")
NCORES = 8
TOK = 1024
D = 4096
NFC = D // 128

CV_EPS, CV_SHIFT01, CV_SHIFT3 = 0, 1, 2
FDBG = False
DBG = None
SOFTMAX_SHIFT01 = -14.0
SOFTMAX_SHIFT3 = -10.0


class Sem:
    __slots__ = ("h", "count", "name")

    def __init__(self, h, name):
        self.h = h
        self.name = name
        self.count = 0


class Ctx:
    def __init__(self):
        self.nc = bass.Bass("TRN2", target_bir_lowering=False)
        self._stack = contextlib.ExitStack()
        self.thunks = {e: [] for e in ENGINES}

    def sem(self, name):
        return Sem(self._stack.enter_context(self.nc.semaphore(name)), name)

    def sbuf(self, name, shape, dtype):
        return self._stack.enter_context(self.nc.sbuf_tensor(name, shape, dtype))

    def sbuf_c(self, name, shape, dtype):
        if not hasattr(self, "_sbufc"):
            self._sbufc = {}
        if name not in self._sbufc:
            self._sbufc[name] = self.sbuf(name, shape, dtype)
        return self._sbufc[name]

    def psum(self, name, shape, dtype=F32):
        return self._stack.enter_context(self.nc.psum_tensor(name, shape, dtype))

    def dram(self, name, shape, dtype, kind="Internal", **kw):
        return self.nc.dram_tensor(name, shape, dtype, kind=kind, **kw).ap()

    def emit(self, eng, fn):
        self.thunks[eng].append(fn)

    def wait(self, eng, sem, val=None):
        v = sem.count if val is None else val
        if v > 0:
            self.emit(eng, lambda e, s=sem.h, v=v: e.wait_ge(s, v))

    def dma(self, queue, out, in_, sem, **kw):
        def f(e, out=out, in_=in_, kw=kw, s=sem.h):
            e.dma_start(out, in_, **kw).then_inc(s, 16)
        self.emit(queue, f)
        sem.count += 16
        return sem.count

    def op(self, eng, build, sem=None, n=1):
        if sem is None:
            self.emit(eng, build)
            return None
        if isinstance(sem, list):
            pairs = sem
        else:
            pairs = [(sem, n)]
        def f(e, build=build, pairs=tuple((s.h, k) for s, k in pairs)):
            ins = build(e)
            for h, k in pairs:
                ins = ins.then_inc(h, k)
        self.emit(eng, f)
        for s, k in pairs:
            s.count += k
        return pairs[0][0].count

    def barrier_from(self, eng, *sems):
        for s in sems:
            self.wait(eng, s, s.count)

    def chan(self, queue, name):
        return DmaChan(self, queue, self.sem(name))

    def finalize(self):
        nc = self.nc
        with nc.Block() as block:
            def runner(thunks):
                def run(eng):
                    for t in thunks:
                        t(eng)
                return run
            for name, meth in (("sync", block.sync), ("tensor", block.tensor),
                               ("vector", block.vector), ("scalar", block.scalar),
                               ("gpsimd", block.gpsimd)):
                if self.thunks[name]:
                    meth(runner(self.thunks[name]))
        self._stack.close()
        return nc


class DmaChan:

    def __init__(self, ctx, queue, sem):
        self.ctx = ctx
        self.queue = queue
        self.sem = sem

    def go(self, out, in_, **kw):
        if self.sem.count > 0:
            self.ctx.wait(self.queue, self.sem, self.sem.count)
        return self.ctx.dma(self.queue, out, in_, self.sem, **kw)


def mk_sems(ctx, tag):
    return {k: ctx.sem(f"{tag}_{k}") for k in ("wl", "pe", "peog", "ev", "tr")}


def emit_mm(ctx, name, wt, n_in, n_out, rhs_fn, evac_fn, sems, *,
            pp, goc=3, icb=8, transform=None, ntok=TOK, extra_passes=None,
            wbufs=None, pre_og=None, tiled=False, wchans=None):
    ogs = [list(range(g, min(g + goc, n_out))) for g in range(0, n_out, goc)]
    nhalf = (ntok + 511) // 512
    maxcols = goc * 128 * icb

    if wbufs is not None:
        fbufs, bufs = wbufs
    elif transform is None:
        bufs = [ctx.sbuf(f"{name}_wb{i}", [128, maxcols], BF16) for i in range(2)]
        fbufs = None
    else:
        fbufs = [ctx.sbuf(f"{name}_wf{i}", [128, maxcols], F32) for i in range(2)]
        bufs = [ctx.sbuf(f"{name}_wb{i}", [128, maxcols], BF16) for i in range(2)]
    pss = [p[:, 0:ntok] for p in pp[:min(goc, n_out)]]

    s_pe, s_peog, s_ev, s_tr = (sems["pe"], sems["peog"],
                                sems["ev"], sems["tr"])
    if wchans is None:
        wlc = [ctx.chan("gpsimd", f"{name}_w0c"),
               ctx.chan("gpsimd", f"{name}_w1c")]
    else:
        wlc = wchans
    wch_i = [0]
    load_idx = 0
    pe_release = {}
    ev_targets = []

    for gi, og in enumerate(ogs):
        if pre_og is not None:
            pre_og(gi, og)
        col0 = og[0] * 128
        ncols = len(og) * 128
        icbs = [list(range(i, min(i + icb, n_in))) for i in range(0, n_in, icb)]
        for bi, ics in enumerate(icbs):
            slot = load_idx % 2
            if load_idx >= 2:
                rs, rv = pe_release[load_idx - 2]
                ctx.wait("gpsimd", rs, rv)
            nic = len(ics)
            dstt = (bufs if transform is None else fbufs)[slot]
            wl_pairs = []

            def wgo(dst_ap, src_ap):
                ch = wlc[wch_i[0] % len(wlc)]
                wch_i[0] += 1
                ch.go(dst_ap, src_ap)
                wl_pairs.append((ch.sem, ch.sem.count))

            if tiled:
                for oi, oc in enumerate(og):
                    c0 = oc * n_in * 128 + ics[0] * 128
                    half = (nic + 1) // 2
                    for h0 in range(0, nic, half):
                        h1 = min(nic, h0 + half)
                        wgo(dstt[:, (oi * nic + h0) * 128:
                                 (oi * nic + h1) * 128],
                            wt[:, c0 + h0 * 128:c0 + h1 * 128])
            elif wchans is not None:
                half = (nic + 1) // 2
                for h0 in range(0, nic, half):
                    h1 = min(nic, h0 + half)
                    srca = wt[(ics[0] + h0) * 128:(ics[0] + h1) * 128,
                              col0:col0 + ncols]
                    srca = srca.rearrange("(i p) c -> p i c", p=128)
                    dsta = dstt[:, h0 * ncols:h1 * ncols].rearrange(
                        "p (i c) -> p i c", i=h1 - h0)
                    wgo(dsta, srca)
            else:
                srca = wt[ics[0] * 128:(ics[-1] + 1) * 128, col0:col0 + ncols]
                srca = srca.rearrange("(i p) c -> p i c", p=128)
                dst = dstt[:, 0:nic * ncols].rearrange("p (i c) -> p i c",
                                                       i=nic)
                wgo(dst, srca)
            if transform == "sign":
                for ws, wv in wl_pairs:
                    ctx.wait("scalar", ws, wv)
                ctx.op("scalar", lambda e, s=slot, n=nic * ncols:
                       e.activation(bufs[s][:, 0:n], fbufs[s][:, 0:n], AF.Sign),
                       s_tr)
                ctx.wait("tensor", s_tr, s_tr.count)
            else:
                for ws, wv in wl_pairs:
                    ctx.wait("tensor", ws, wv)
            if bi == 0 and gi > 0:
                ctx.wait("tensor", s_ev, ev_targets[gi - 1])
            any_extra = (extra_passes is not None
                         and any(extra_passes(oc) for oc in og))
            for j, ic in enumerate(ics):
                first = (bi == 0 and j == 0)
                last_ic = (bi == len(icbs) - 1 and j == len(ics) - 1)
                for oi, oc in enumerate(og):
                    if tiled:
                        lhsT = bufs[slot][:, oi * nic * 128 + j * 128:
                                          oi * nic * 128 + (j + 1) * 128]
                    else:
                        lhsT = bufs[slot][:, j * ncols + oi * 128:
                                          j * ncols + (oi + 1) * 128]
                    rhs = rhs_fn(ic)
                    for h in range(nhalf):
                        t0, t1 = h * 512, min(ntok, (h + 1) * 512)
                        is_last = (last_ic
                                   and (extra_passes is None or not extra_passes(oc)))
                        grp_last = (j == len(ics) - 1 and oi == len(og) - 1
                                    and h == nhalf - 1)
                        og_last = (grp_last and bi == len(icbs) - 1
                                   and not any_extra)
                        sem1 = s_peog if og_last else (s_pe if grp_last else None)
                        ctx.op("tensor", lambda e, o=pss[oi][:, t0:t1], a=lhsT,
                               b=rhs[:, t0:t1], fs=first, l=is_last:
                               e.matmul(o, a, b, start=fs, stop=l), sem1)
                        if grp_last:
                            pe_release[load_idx] = (sem1, sem1.count)
            load_idx += 1
        if any_extra:
            eplist = [(oc, extra_passes(oc)) for oc in og]
            n_ep = sum(len(ep) for _, ep in eplist)
            cnt = 0
            for oi, (oc, ep) in enumerate(eplist):
                for k, (lhsT_ap, rhs_ap) in enumerate(ep):
                    lastk = (k == len(ep) - 1)
                    for h in range(nhalf):
                        t0, t1 = h * 512, min(ntok, (h + 1) * 512)
                        cnt_last = (oi == len(eplist) - 1 and k == len(ep) - 1
                                    and h == nhalf - 1)
                        ctx.op("tensor", lambda e, o=pss[oi][:, t0:t1], a=lhsT_ap,
                               b=rhs_ap[:, t0:t1], l=lastk:
                               e.matmul(o, a, b, start=False, stop=l),
                               [(s_peog, 1)] if cnt_last else None)
        og_tgt = s_peog.count
        for oi, oc in enumerate(og):
            evac_fn(oc, pss[oi], og_tgt)
        ev_targets.append(s_ev.count)
    return ev_targets[-1] if ev_targets else 0


def emit_ln(ctx, name, xsrc, xb, aux, sems, ntok=TOK):
    s_xl, s_pe, s_ve, s_tr, s_ac = (sems["xl"], sems["pe"], sems["ve"],
                                    sems["tr"], sems["ac"])
    pp = aux["pp"]
    nhalf = ntok // 512
    if "ln_shared" not in aux:
        aux["ln_shared"] = dict(
            sq=[ctx.sbuf(f"lnsh_sq{i}", [128, ntok], BF16) for i in range(2)],
            rows={k: ctx.sbuf(f"lnsh_{k}", [1, ntok], F32)
                  for k in ("m", "msq", "var", "std", "rstd", "bneg")})
    sq = aux["ln_shared"]["sq"]
    rows = aux["ln_shared"]["rows"]
    ps_sx = pp[0][0:1, 0:ntok]
    ps_sq = pp[1][0:1, 0:ntok]
    ps_ba = pp[0][:, 0:ntok]
    ps_bb = pp[1][:, 0:ntok]

    for fc in range(NFC):
        ctx.dma("gpsimd", xb[:, fc * ntok:(fc + 1) * ntok],
                xsrc[fc * 128:(fc + 1) * 128, :], s_xl)
    xl_all = s_xl.count
    ctx.wait("scalar", s_xl, xl_all)
    ctx.wait("tensor", s_xl, xl_all)
    sq_release = {}
    for fc in range(NFC):
        slot = fc % 2
        if fc >= 2:
            ctx.wait("scalar", s_pe, sq_release[fc - 2])
        ctx.op("scalar", lambda e, s=slot, fc=fc:
               e.activation(sq[s][:, :], xb[:, fc * ntok:(fc + 1) * ntok], AF.Square),
               s_tr)
        tr_tgt = s_tr.count
        ctx.wait("tensor", s_tr, tr_tgt)
        for h in range(nhalf):
            t0, t1 = h * 512, (h + 1) * 512
            ctx.op("tensor", lambda e, fc=fc, t0=t0, t1=t1:
                   e.matmul(ps_sx[:, t0:t1], aux["onesb"][:, :],
                            xb[:, fc * ntok + t0:fc * ntok + t1],
                            start=(fc == 0), stop=(fc == NFC - 1)))
            ctx.op("tensor", lambda e, s=slot, t0=t0, t1=t1, fc=fc:
                   e.matmul(ps_sq[:, t0:t1], aux["onesb"][:, :],
                            sq[s][:, t0:t1], start=(fc == 0), stop=(fc == NFC - 1)),
                   s_pe if h == nhalf - 1 else None)
        sq_release[fc] = s_pe.count
    pe_sums = s_pe.count
    ctx.wait("vector", s_pe, pe_sums)
    ctx.op("vector", lambda e: e.tensor_scalar(rows["m"][:, :], ps_sx[:, :],
                                               1.0 / D, None, ALU.mult), s_ve)
    ctx.wait("vector", s_ve, s_ve.count)
    ctx.op("vector", lambda e: e.tensor_mul(rows["msq"][:, :], rows["m"][:, :],
                                            rows["m"][:, :]), s_ve)
    ctx.wait("vector", s_ve, s_ve.count)
    ctx.op("vector", lambda e: e.scalar_tensor_tensor(
        rows["var"][:, :], ps_sq[:, :], 1.0 / D, rows["msq"][:, :],
        ALU.mult, ALU.subtract), s_ve)
    ctx.wait("scalar", s_ve, s_ve.count)
    ctx.op("scalar", lambda e: e.activation(rows["std"][:, :], rows["var"][:, :],
                                            AF.Sqrt,
                                            bias=aux["cv"][0:1, CV_EPS:CV_EPS + 1],
                                            scale=1.0), s_tr)
    ctx.wait("vector", s_tr, s_tr.count)
    ctx.op("vector", lambda e: e.reciprocal(rows["rstd"][:, :], rows["std"][:, :]), s_ve)
    ctx.wait("vector", s_ve, s_ve.count)
    ctx.op("vector", lambda e: e.scalar_tensor_tensor(
        rows["bneg"][:, :], rows["m"][:, :], -1.0, rows["rstd"][:, :],
        ALU.mult, ALU.mult), s_ve)
    ctx.wait("tensor", s_ve, s_ve.count)
    for h in range(nhalf):
        t0, t1 = h * 512, (h + 1) * 512
        ctx.op("tensor", lambda e, t0=t0, t1=t1:
               e.matmul(ps_ba[:, t0:t1], aux["onesf"][:, :],
                        rows["rstd"][:, t0:t1], start=True, stop=True))
        ctx.op("tensor", lambda e, t0=t0, t1=t1:
               e.matmul(ps_bb[:, t0:t1], aux["onesf"][:, :],
                        rows["bneg"][:, t0:t1], start=True, stop=True),
               s_pe if h == nhalf - 1 else None)
    ctx.wait("vector", s_pe, s_pe.count)
    for fc in range(NFC):
        a = xb[:, fc * ntok:(fc + 1) * ntok]
        ctx.op("vector", lambda e, a=a: e.tensor_tensor(a, a, ps_ba[:, :], ALU.mult))
        ctx.op("vector", lambda e, a=a: e.tensor_tensor(a, a, ps_bb[:, :], ALU.add))
    ctx.op("vector", lambda e: e.engine_nop(), s_ac)
    return s_ac.count


def build_A01(lay):
    ctx = Ctx()
    if lay == 0:
        xsrc = ctx.dram("xT", [D, TOK], F32, kind="ExternalInput")
    else:
        xAp = ctx.dram("xAp", [D, TOK], F32, kind="ExternalInput")
        aoT = ctx.dram("aoT", [D, TOK], BF16, kind="ExternalInput")
        woT = ctx.dram("woT", [D, D], F32, kind="ExternalInput")
        xsrc = ctx.dram("x", [D, TOK], F32)
    wqkvT = ctx.dram("wqkvT", [D, 6144], F32, kind="ExternalInput")
    ctab = ctx.dram("ctab", [128, TOK], F32, kind="ExternalInput")
    stab = ctx.dram("stab", [128, TOK], F32, kind="ExternalInput")
    cvec = ctx.dram("cvec", [128, 8], F32, kind="ExternalInput")
    ident = ctx.dram("ident", [128, 128], F32, kind="ExternalInput")
    qkvT = ctx.dram("qkvT", [6144, TOK], BF16, kind="ExternalOutput")
    xA = ctx.dram("xA", [D, TOK], F32, kind="ExternalOutput")
    dbg = ctx.dram("dbg", [D, TOK], F32, kind="ExternalOutput") if DBG else None
    if lay == 0:
        w0T = ctx.dram("w0T", [D, D], F32, kind="ExternalInput")
        bias0 = ctx.dram("bias0", [128, NFC], F32, kind="ExternalInput")
        gwT = ctx.dram("gwT", [D, 8], F32, kind="ExternalInput")
        ics = ctx.dram("ics", [8, D], F32, kind="ExternalInput")
        ocs = ctx.dram("ocs", [8, D], F32, kind="ExternalInput")
    else:
        pwT = ctx.dram("pwT", [2048, D], U8, kind="ExternalInput")
        sclT = ctx.dram("sclT", [32, D], F32, kind="ExternalInput")
        sel2 = ctx.dram("sel2", [2, 128], F32, kind="ExternalInput")
        w1T = ctx.dram("w1T", [D, D], BF16)

    S = {k: ctx.sem(k) for k in
         ("wl", "xl", "pe", "peog", "ev", "tr", "ve", "ac", "st", "pe2",
          "tb", "ao", "xc", "xac")}
    acc2 = [ctx.chan("gpsimd", "acc0"), ctx.chan("gpsimd", "acc1")]
    stc = [ctx.chan("sync", "stc0"), ctx.chan("sync", "stc1")]
    pp = [ctx.psum(f"pp{i}", [128, TOK], F32) for i in range(3)]
    pp3 = ctx.psum("pp3", [128, 512], F32)
    pp4 = ctx.psum("pp4", [128, TOK], BF16)

    big = ctx.sbuf("big", [128, NFC * TOK], BF16)
    onesb = ctx.sbuf("onesb", [128, 1], BF16)
    onesf = ctx.sbuf("onesf", [1, 128], F32)
    cv = ctx.sbuf("cv", [128, 8], F32)
    ctb = ctx.sbuf("ctb", [128, TOK], BF16)
    stb = ctx.sbuf("stb", [128, TOK], BF16)
    stage = [ctx.sbuf(f"stage{i}", [128, TOK], BF16) for i in range(2)]
    fstage = [ctx.sbuf(f"fstage{i}", [128, TOK], F32) for i in range(2)]
    rot = ctx.sbuf("rot", [128, TOK], BF16)
    tmpb = ctx.sbuf("tmpb", [128, TOK], BF16)
    identb = ctx.sbuf("identb", [128, 128], BF16)
    aux = dict(onesb=onesb, onesf=onesf, cv=cv, pp=pp, identb=identb)

    ctx.op("gpsimd", lambda e: e.memset(onesb[:, :], 1.0))
    ctx.op("gpsimd", lambda e: e.memset(onesf[:, :], 1.0), S["pe2"], 1)
    ctx.wait("tensor", S["pe2"], 1)
    ctx.dma("gpsimd", cv[:, :], cvec, S["tb"])
    ctx.dma("gpsimd", ctb[:, :], ctab, S["tb"])
    ctx.dma("gpsimd", stb[:, :], stab, S["tb"])
    ctx.dma("gpsimd", identb[:, :], ident, S["tb"])
    tabs_tgt = S["tb"].count
    ctx.wait("tensor", S["tb"], tabs_tgt)
    ctx.wait("vector", S["tb"], tabs_tgt)
    ctx.wait("scalar", S["tb"], tabs_tgt)

    if lay == 1:
        ctx.dma("gpsimd", xsrc, xAp, S["xc"])
        xcopy_tgt = S["xc"].count
        for fc in range(NFC):
            ctx.dma("sync", big[:, fc * TOK:(fc + 1) * TOK],
                    aoT[fc * 128:(fc + 1) * 128, :], S["ao"])
        ao_tgt = S["ao"].count
        ctx.wait("tensor", S["ao"], ao_tgt)

        def wo_evac(oc, ps, og_tgt):
            ctx.wait("vector", S["peog"], og_tgt)
            slot = oc % 2
            if oc >= 2:
                ctx.wait("vector", acc2[slot].sem, wo_acc[oc - 2])
            ctx.op("vector", lambda e, s=slot, p=ps:
                   e.tensor_copy(fstage[s][:, :], p[:, :]), S["ev"])
            ctx.wait("gpsimd", S["ev"], S["ev"].count)
            ctx.wait("gpsimd", S["xc"], xcopy_tgt)
            acc2[slot].go(xsrc[oc * 128:(oc + 1) * 128, :],
                          fstage[slot][:, :], accum_op=ALU.add)
            wo_acc[oc] = acc2[slot].sem.count
        wo_acc = {}
        emit_mm(ctx, "wo", woT, NFC, NFC,
                lambda ic: big[:, ic * TOK:(ic + 1) * TOK], wo_evac,
                S, pp=pp, goc=3, icb=8)
        for _c in acc2:
            ctx.wait("gpsimd", _c.sem, _c.sem.count)

    sems_ln = dict(xl=S["xl"], pe=S["pe"], ve=S["ve"], tr=S["tr"], ac=S["ve"])
    ctx.barrier_from("tensor", S["ev"], S["ve"])
    xn_tgt = emit_ln(ctx, "ln", xsrc, big, aux, sems_ln)

    ctx.wait("gpsimd", S["ve"], xn_tgt)
    ctx.dma("gpsimd", xA, xsrc, S["xac"])
    xA_copy = S["xac"].count

    ctx.barrier_from("tensor", S["ve"])
    qkv_store = {}

    def qkv_evac(oc, ps, og_tgt):
        slot = oc % 2
        ctx.wait("scalar", S["peog"], og_tgt)
        if oc >= 2:
            ctx.wait("scalar", stc[slot].sem, qkv_store[oc - 2])
        ctx.op("scalar", lambda e, s=slot, p=ps:
               e.activation(stage[s][:, :], p[:, :], AF.Identity, bias=0.0, scale=1.0),
               S["ev"])
        ev_tgt = S["ev"].count
        if oc < 40:
            ctx.wait("vector", S["ev"], ev_tgt)
            ctx.op("vector", lambda e, s=slot: e.tensor_copy(rot[0:64, :], stage[s][64:128, :]))
            ctx.op("vector", lambda e, s=slot: e.tensor_copy(rot[64:128, :], stage[s][0:64, :]))
            ctx.op("vector", lambda e, s=slot: e.tensor_tensor(tmpb[:, :], stage[s][:, :], ctb[:, :], ALU.mult))
            ctx.op("vector", lambda e: e.tensor_tensor(rot[:, :], rot[:, :], stb[:, :], ALU.mult))
            ctx.op("vector", lambda e, s=slot: e.tensor_tensor(stage[s][:, :], tmpb[:, :], rot[:, :], ALU.add),
                   S["ve"])
            ctx.wait("sync", S["ve"], S["ve"].count)
        else:
            ctx.wait("sync", S["ev"], ev_tgt)
        stc[slot].go(qkvT[oc * 128:(oc + 1) * 128, :], stage[slot][:, :])
        qkv_store[oc] = stc[slot].sem.count

    emit_mm(ctx, "qkv",
            wqkvT, NFC, 48, lambda ic: big[:, ic * TOK:(ic + 1) * TOK],
            qkv_evac, S, pp=pp, goc=3, icb=8)

    aux2 = dict(aux, pp3=pp3, pp4=pp4, fstage=fstage, acc2=acc2, dbg=dbg)
    if lay == 0:
        emit_ffn0(ctx, big, xA, aux2, S,
                  dict(w0T=w0T, bias0=bias0, gwT=gwT, ics=ics, ocs=ocs),
                  xA_copy)
    else:
        emit_ffn1(ctx, big, xA, aux2, S, dict(pwT=pwT, sclT=sclT, w1T=w1T,
                                              sel2=sel2), xA_copy)

    for ch in acc2 + stc:
        ctx.wait("gpsimd", ch.sem, ch.sem.count)
    ctx.barrier_from("gpsimd", S["xac"], S["st"])
    return ctx


def emit_ffn0(ctx, big, xA, aux, S, W, xA_copy, loaded=False, wbufs=None,
              tiled=False, wchans=None):
    pp, pp3, pp4, fstage = aux["pp"], aux["pp3"], aux["pp4"], aux["fstage"]
    gw = aux.get("f0_gw") if "f0_gw" in aux else ctx.sbuf_c(
        "f0_gw", [128, NFC * 8], BF16)
    icsb = aux.get("f0_icsb") if "f0_icsb" in aux else ctx.sbuf_c(
        "f0_ics", [8, D], BF16)
    ocsb = aux.get("f0_ocsb") if "f0_ocsb" in aux else ctx.sbuf_c(
        "f0_ocs", [8, D], BF16)
    biasb = ctx.sbuf_c("f0_bias", [128, NFC], F32)
    rtT = aux.get("f0_rtT") if "f0_rtT" in aux else ctx.sbuf_c(
        "f0_rtT", [8, TOK], BF16)
    pexp = ctx.sbuf_c("f0_pexp", [128, 8], BF16)
    rm = ctx.sbuf_c("f0_rm", [128, 1], F32)
    rsum = ctx.sbuf_c("f0_rsum", [128, 1], F32)
    rrcp = ctx.sbuf_c("f0_rrcp", [128, 1], F32)
    rtile = ctx.sbuf_c("f0_rtile", [128, 8], BF16)
    osb = ctx.sbuf_c("f0_osb", [128, 3 * TOK], F32)

    if not loaded:
        ctx.dma("gpsimd", gw[:, :].rearrange("p (c e) -> p c e", e=8),
                W["gwT"].rearrange("(c p) e -> p c e", p=128), S["wl"])
        ctx.dma("gpsimd", icsb[:, :], W["ics"], S["wl"])
        ctx.dma("gpsimd", ocsb[:, :], W["ocs"], S["wl"])
        ctx.dma("gpsimd", biasb[:, :], W["bias0"], S["wl"])
    lw = S["wl"].count

    ctx.barrier_from("tensor", S["ev"], S["ve"], S["tr"])
    ctx.wait("tensor", S["wl"], lw)
    ctx.wait("vector", S["wl"], lw)
    ctx.wait("scalar", S["wl"], lw)

    for tt in range(8):
        psl = pp3[:, 0:8]
        for ic in range(NFC):
            ctx.op("tensor", lambda e, ic=ic, tt=tt, psl=psl:
                   e.matmul(psl, big[:, ic * TOK + tt * 128: ic * TOK + (tt + 1) * 128],
                            gw[:, ic * 8:(ic + 1) * 8],
                            start=(ic == 0), stop=(ic == NFC - 1)),
                   S["peog"] if ic == NFC - 1 else None)
        ctx.wait("vector", S["peog"], S["peog"].count)
        ctx.op("vector", lambda e, psl=psl:
               e.reduce_max(rm[:, :], psl, axis=AX.X, negate=True), S["ve"])
        ctx.wait("scalar", S["ve"], S["ve"].count)
        ctx.op("scalar", lambda e, psl=psl:
               e.activation(pexp[:, :], psl, AF.Exp, bias=rm[:, :], scale=1.0),
               S["tr"])
        ctx.wait("vector", S["tr"], S["tr"].count)
        ctx.op("vector", lambda e: e.reduce_sum(rsum[:, :], pexp[:, :], axis=AX.X),
               S["ve"])
        ctx.wait("vector", S["ve"], S["ve"].count)
        ctx.op("vector", lambda e: e.reciprocal(rrcp[:, :], rsum[:, :]), S["ve"])
        ctx.wait("vector", S["ve"], S["ve"].count)
        ctx.op("vector", lambda e:
               e.tensor_scalar(rtile[:, :], pexp[:, :], rrcp[:, :], None, ALU.mult),
               S["ve"])
        ctx.wait("tensor", S["ve"], S["ve"].count)
        ctx.op("tensor", lambda e: e.transpose(pp4[0:8, 0:128], rtile[:, :],
                                               aux["identb"][:, :]), S["peog"])
        ctx.wait("vector", S["peog"], S["peog"].count)
        ctx.op("vector", lambda e, tt=tt:
               e.tensor_copy(rtT[:, tt * 128:(tt + 1) * 128], pp4[0:8, 0:128]),
               S["ve"])
        ctx.wait("tensor", S["ve"], S["ve"].count)

    for fc in range(NFC):
        for h in range(2):
            t0, t1 = h * 512, (h + 1) * 512
            ctx.op("tensor", lambda e, fc=fc, t0=t0, t1=t1:
                   e.matmul(pp3[:, 0:512], icsb[:, fc * 128:(fc + 1) * 128],
                            rtT[:, t0:t1], start=True, stop=True), S["peog"])
            ctx.wait("vector", S["peog"], S["peog"].count)
            a = big[:, fc * TOK + t0: fc * TOK + t1]
            ctx.op("vector", lambda e, a=a:
                   e.tensor_tensor(a, a, pp3[:, 0:512], ALU.mult), S["ve"])
            ctx.wait("tensor", S["ve"], S["ve"].count)
    z_done = S["ve"].count
    if DBG == "rt":
        dbgch = ctx.chan("gpsimd", "dbgch")
        ctx.wait("gpsimd", S["ve"], z_done)
        dbgch.go(aux["dbg"][0:8, :], rtT[:, :])
        ctx.wait("gpsimd", dbgch.sem, dbgch.sem.count)
    if DBG == "z":
        dbgch = ctx.chan("gpsimd", "dbgch")
        ctx.wait("gpsimd", S["ve"], z_done)
        for fc in range(NFC):
            dbgch.go(aux["dbg"][fc * 128:(fc + 1) * 128, :],
                     big[:, fc * TOK:(fc + 1) * TOK])
        ctx.wait("gpsimd", dbgch.sem, dbgch.sem.count)

    os_evt = {}

    def pre_og(gi, og):
        for oi, oc in enumerate(og):
            for h in range(2):
                t0, t1 = h * 512, (h + 1) * 512
                ctx.op("tensor", lambda e, oc=oc, t0=t0, t1=t1:
                       e.matmul(pp3[:, 0:512], ocsb[:, oc * 128:(oc + 1) * 128],
                                rtT[:, t0:t1], start=True, stop=True), S["peog"])
                ctx.wait("vector", S["peog"], S["peog"].count)
                ctx.op("vector", lambda e, oi=oi, t0=t0, t1=t1:
                       e.tensor_copy(osb[:, oi * TOK + t0: oi * TOK + t1],
                                     pp3[:, 0:512]), S["ve"])
                ctx.wait("tensor", S["ve"], S["ve"].count)

    acc_t = {}

    acc2 = aux["acc2"]
    ctx.barrier_from("vector", *[c.sem for c in acc2])

    def evac(oc, ps, og_tgt):
        oi = oc % 3
        slot = oc % 2
        ctx.wait("vector", S["peog"], og_tgt)
        if oc >= 2:
            ctx.wait("vector", acc2[slot].sem, acc_t[oc - 2])
        ctx.op("vector", lambda e, s=slot, p=ps, oi=oi:
               e.tensor_tensor(fstage[s][:, :], p[:, :], osb[:, oi * TOK:(oi + 1) * TOK],
                               ALU.mult))
        ctx.op("vector", lambda e, s=slot, oc=oc:
               e.tensor_scalar(fstage[s][:, :], fstage[s][:, :],
                               biasb[:, oc:oc + 1], None, ALU.add), S["ev"])
        ctx.wait("gpsimd", S["ev"], S["ev"].count)
        ctx.wait("gpsimd", S["xac"], xA_copy)
        acc2[slot].go(xA[oc * 128:(oc + 1) * 128, :], fstage[slot][:, :],
                      accum_op=ALU.add)
        if DBG == "ffn":
            acc2[slot].go(aux["dbg"][oc * 128:(oc + 1) * 128, :], fstage[slot][:, :])
        acc_t[oc] = acc2[slot].sem.count

    emit_mm(ctx, f"f0{aux.get('tag','')}", W["w0T"], NFC, NFC,
            lambda ic: big[:, ic * TOK:(ic + 1) * TOK], evac, S,
            pp=pp, goc=3, icb=4, transform="sign", pre_og=pre_og, wbufs=wbufs,
            tiled=tiled, wchans=wchans)


def emit_ffn1(ctx, big, xA, aux, S, W, xA_copy, loaded=False, wbufs=None,
              wchans=None):
    pp, pp3, fstage = aux["pp"], aux["pp3"], aux["fstage"]
    pk = ctx.sbuf_c("f1_pk", [128, D], U8)
    hi8 = ctx.sbuf_c("f1_hi8", [128, TOK], U8)
    lo8 = ctx.sbuf_c("f1_lo8", [128, TOK], U8)
    hi32 = aux.get("f1_hi32") if "f1_hi32" in aux else ctx.sbuf_c(
        "f1_hi", [128, TOK], BF16)
    lo32 = aux.get("f1_lo32") if "f1_lo32" in aux else ctx.sbuf_c(
        "f1_lo", [128, TOK], BF16)
    scl2 = aux.get("f1_scl2") if "f1_scl2" in aux else ctx.sbuf_c(
        "f1_scl2", [2, D], BF16)
    sel2b = ctx.sbuf_c("f1_sel2", [2, 128], BF16)
    bsc = aux.get("f1_bsc") if "f1_bsc" in aux else ctx.sbuf_c(
        "f1_bsc", [128, TOK], BF16)

    acc2 = aux["acc2"]
    ctx.barrier_from("tensor", S["ev"], S["ve"], S["tr"])
    if not loaded:
        pkc = ctx.chan("gpsimd", "f1_pkc")
        dqc = ctx.chan("sync", "f1_dqc")
        ctx.dma("gpsimd", sel2b[:, :], W["sel2"], S["wl"])
        ctx.wait("tensor", S["wl"], S["wl"].count)
        dq_store = 0
        prev_ve = 0
        prev_pe = 0
        for cp in range(16):
            if cp > 0:
                ctx.wait("gpsimd", S["ve"], prev_ve)
                ctx.wait("gpsimd", S["peog"], prev_pe)
            pkc.go(pk[:, :], W["pwT"][cp * 128:(cp + 1) * 128, :])
            pkc.go(scl2[:, :], W["sclT"][2 * cp:2 * cp + 2, :])
            lw = pkc.sem.count
            ctx.wait("tensor", pkc.sem, lw)
            ctx.wait("vector", pkc.sem, lw)
            for pc in range(4):
                c0 = pc * TOK
                if dq_store > 0:
                    ctx.wait("vector", dqc.sem, dq_store)
                for h in range(2):
                    t0 = c0 + h * 512
                    ctx.op("tensor", lambda e, t0=t0:
                           e.matmul(pp[0][:, (t0 % TOK):(t0 % TOK) + 512],
                                    sel2b[:, :], scl2[:, t0:t0 + 512],
                                    start=True, stop=True),
                           S["peog"] if h == 1 else None)
                ctx.wait("vector", S["peog"], S["peog"].count)
                ctx.op("vector", lambda e: e.tensor_copy(bsc[:, :], pp[0][:, 0:TOK]))
                ctx.op("vector", lambda e, c0=c0:
                       e.tensor_scalar(hi8[:, :], pk[:, c0:c0 + TOK], 4, None,
                                       ALU.logical_shift_right))
                ctx.op("vector", lambda e, c0=c0:
                       e.tensor_scalar(lo8[:, :], pk[:, c0:c0 + TOK], 15, None,
                                       ALU.bitwise_and))
                ctx.op("vector", lambda e:
                       e.scalar_tensor_tensor(hi32[:, :], hi8[:, :], -8.0, bsc[:, :],
                                              ALU.add, ALU.mult))
                ctx.op("vector", lambda e:
                       e.scalar_tensor_tensor(lo32[:, :], lo8[:, :], -8.0, bsc[:, :],
                                              ALU.add, ALU.mult), S["ve"])
                ve_tgt = S["ve"].count
                ctx.wait("tensor", S["ve"], ve_tgt)
                ctx.wait("sync", S["ve"], ve_tgt)
                for par, bufv in ((0, hi32), (1, lo32)):
                    for half in range(2):
                        blk = 2 * cp + half
                        dst = W["w1T"][blk * 128:(blk + 1) * 128, pc * TOK:(pc + 1) * TOK]
                        dst = dst.rearrange("(r two) c -> two r c", two=2)[par]
                        dqc.go(dst, bufv[half * 64:(half + 1) * 64, :])
                dq_store = dqc.sem.count
            prev_ve = S["ve"].count
            prev_pe = S["peog"].count
        ctx.wait("gpsimd", dqc.sem, dq_store)

    acc_t = {}
    ctx.barrier_from("vector", *[c.sem for c in acc2])

    def evac(oc, ps, og_tgt):
        slot = oc % 2
        ctx.wait("vector", S["peog"], og_tgt)
        if oc >= 2:
            ctx.wait("vector", acc2[slot].sem, acc_t[oc - 2])
        ctx.op("vector", lambda e, s=slot, p=ps:
               e.tensor_copy(fstage[s][:, :], p[:, :]), S["ev"])
        ctx.wait("gpsimd", S["ev"], S["ev"].count)
        ctx.wait("gpsimd", S["xac"], xA_copy)
        acc2[slot].go(xA[oc * 128:(oc + 1) * 128, :], fstage[slot][:, :],
                      accum_op=ALU.add)
        acc_t[oc] = acc2[slot].sem.count

    emit_mm(ctx, f"f1{aux.get('tag','')}", W["w1T"], NFC, NFC,
            lambda ic: big[:, ic * TOK:(ic + 1) * TOK], evac, S,
            pp=pp, goc=3, icb=4, wbufs=wbufs, wchans=wchans)


ROPE_THETA = 10000.0


def host_rope_tables(core):
    inv = 1.0 / ROPE_THETA ** (np.arange(0, 128, 2, dtype=np.float64) / 128.0)
    pos = np.arange(512 * core, 512 * core + 512, dtype=np.float64)
    fr = pos[None, :] * np.concatenate([inv, inv])[:, None]
    c = np.cos(fr)
    s = np.sin(fr)
    sgn = np.where(np.arange(128) < 64, -1.0, 1.0)[:, None]
    ct = np.concatenate([c, c], axis=1).astype(np.float32)
    st = (np.concatenate([s, s], axis=1) * sgn).astype(np.float32)
    return np.ascontiguousarray(ct), np.ascontiguousarray(st)


def host_xT(x, core):
    sl = np.concatenate([x[0, 512 * core:512 * core + 512],
                         x[1, 512 * core:512 * core + 512]], axis=0)
    return np.ascontiguousarray(sl.T.astype(np.float32, copy=False))


def host_unxT(xTs):
    out = np.empty((2, 4096, 4096), np.float32)
    for c, xT in enumerate(xTs):
        t = np.asarray(xT).T
        out[0, 512 * c:512 * c + 512] = t[0:512]
        out[1, 512 * c:512 * c + 512] = t[512:1024]
    return out


def host_cvec():
    v = np.zeros((128, 8), np.float32)
    v[:, CV_EPS] = 1e-5
    v[:, CV_SHIFT01] = SOFTMAX_SHIFT01
    v[:, CV_SHIFT3] = SOFTMAX_SHIFT3
    return v


def host_A0_inputs(ins, core, xT):
    wqkvT = np.ascontiguousarray(np.concatenate(
        [np.asarray(ins["L0_attn_Wq"]).T, np.asarray(ins["L0_attn_Wk"]).T,
         np.asarray(ins["L0_attn_Wv"]).T], axis=1).astype(np.float32, copy=False))
    ct, st = host_rope_tables(core)
    return dict(
        xT=xT, wqkvT=wqkvT, ctab=ct, stab=st, cvec=host_cvec(),
        ident=np.eye(128, dtype=np.float32),
        w0T=np.ascontiguousarray(np.asarray(ins["L0_ffn_weight"]).T),
        bias0=np.ascontiguousarray(np.asarray(ins["L0_ffn_bias"]).reshape(NFC, 128).T),
        gwT=np.ascontiguousarray(np.asarray(ins["L0_ffn_gate_w"]).T),
        ics=np.ascontiguousarray(np.asarray(ins["L0_ffn_in_channel_scale"])),
        ocs=np.ascontiguousarray(np.asarray(ins["L0_ffn_out_channel_scale"])),
    )


def host_A1_inputs(ins, core, xAp, aoT):
    wqkvT = np.ascontiguousarray(np.concatenate(
        [np.asarray(ins["L1_attn_Wq"]).T, np.asarray(ins["L1_attn_Wk"]).T,
         np.asarray(ins["L1_attn_Wv"]).T], axis=1).astype(np.float32, copy=False))
    ct, st = host_rope_tables(core)
    pw = np.asarray(ins["L1_ffn_packed_weight"])
    pb = np.ascontiguousarray(pw).view(np.int8)[..., 0::4].astype(np.int8, copy=True)
    pb = pb.view(np.uint8)
    pwT = np.ascontiguousarray(pb.transpose(1, 2, 0)).reshape(2048, 4096)
    scl = np.asarray(ins["L1_ffn_scales"]).reshape(4096, 32)
    sclT = np.ascontiguousarray(scl.T.astype(np.float32, copy=False))
    sel2 = np.zeros((2, 128), np.float32)
    sel2[0, 0:64] = 1.0
    sel2[1, 64:128] = 1.0
    return dict(
        xAp=xAp, aoT=aoT,
        woT=np.ascontiguousarray(np.asarray(ins["L0_attn_Wo"]).T),
        wqkvT=wqkvT, ctab=ct, stab=st, cvec=host_cvec(),
        ident=np.eye(128, dtype=np.float32),
        pwT=pwT, sclT=sclT, sel2=sel2,
    )


def host_reshard_qkv(qkvs, core):
    allq = np.concatenate([np.asarray(q)[None] for q in qkvs], axis=0)
    qm = allq[:, 512 * core:512 * core + 512, :]
    qm = np.ascontiguousarray(qm.transpose(1, 0, 2).reshape(512, 8192))
    km = allq[:, 4096 + 128 * core:4096 + 128 * (core + 1), :]
    km = np.ascontiguousarray(km.transpose(1, 0, 2).reshape(128, 8192))
    vm = allq[:, 5120 + 128 * core:5120 + 128 * (core + 1), :]
    vm = vm.transpose(1, 0, 2).reshape(128, 8192)
    vtok = np.ascontiguousarray(vm.T)
    return qm, km, vtok


def build_B01():
    ctx = Ctx()
    qmine = ctx.dram("qmine", [512, 8192], BF16, kind="ExternalInput")
    kmine = ctx.dram("kmine", [128, 8192], BF16, kind="ExternalInput")
    vtok = ctx.dram("vtok", [8192, 128], BF16, kind="ExternalInput")
    trimask = ctx.dram("trimask", [128, 896], F32, kind="ExternalInput")
    cvec = ctx.dram("cvec", [128, 8], F32, kind="ExternalInput")
    aomine = ctx.dram("aomine", [512, 8192], BF16, kind="ExternalOutput")

    S = {k: ctx.sem(k) for k in ("pe", "tr", "ve", "kv")}
    qc = [ctx.chan("gpsimd", "qc0"), ctx.chan("gpsimd", "qc1")]
    stc = [ctx.chan("sync", "stc0"), ctx.chan("sync", "stc1")]

    kb_s = ctx.sbuf("kb_s", [128, 8192], BF16)
    vt_s = ctx.sbuf("vt_s", [128, 8192], BF16)
    qh = [ctx.sbuf(f"qh{i}", [128, 8192], BF16) for i in range(2)]
    tri = ctx.sbuf("tri", [128, 896], BF16)
    cv = ctx.sbuf("cv", [128, 8], F32)
    onesb = ctx.sbuf("onesb", [128, 1], BF16)
    onesf = ctx.sbuf("onesf", [1, 128], F32)
    ebuf = [ctx.sbuf(f"ebuf{i}", [128, 1536], BF16) for i in range(2)]
    esum = ctx.sbuf("esum", [128, 512], BF16)
    rd_row = ctx.sbuf("rd_row", [1, 512], F32)
    bc_sb = ctx.sbuf("bc_sb", [128, 512], F32)
    aost = [ctx.sbuf(f"aost{i}", [128, 512], BF16) for i in range(2)]

    ps_sc = [ctx.psum(f"ps_sc{i}", [128, 1536], F32) for i in range(2)]
    ps_av = ctx.psum("ps_av", [128, 512], F32)
    ps_m = ctx.psum("ps_m", [128, 512], F32)

    ctx.op("gpsimd", lambda e: e.memset(onesb[:, :], 1.0))
    ctx.op("gpsimd", lambda e: e.memset(onesf[:, :], 1.0))
    ctx.dma("gpsimd", kb_s[:, :], kmine, S["kv"])
    ctx.dma("gpsimd", vt_s[:, :].rearrange("p (kb d) -> p kb d", d=128),
            vtok.rearrange("(kb p) d -> p kb d", p=128), S["kv"])
    ctx.dma("gpsimd", tri[:, :], trimask, S["kv"])
    ctx.dma("gpsimd", cv[:, :], cvec, S["kv"])
    kv_tgt = S["kv"].count
    for eng in ("tensor", "scalar", "vector"):
        ctx.wait(eng, S["kv"], kv_tgt)

    qload = {}
    for h in range(4):
        qslot = h % 2
        if h >= 2:
            rs, rv = qload[h - 2]
            ctx.wait("gpsimd", rs, rv)
        qc[qslot].go(qh[qslot][:, :], qmine[h * 128:(h + 1) * 128, :])
        q_tgt = qc[qslot].sem.count
        ctx.wait("tensor", qc[qslot].sem, q_tgt)
        for b in range(2):
            for s in range(8):
                qcol = s * 1024 + b * 512
                qtile = qh[qslot][:, qcol:qcol + 512]
                nblk = 4 * (s + 1)
                kcols = [(j // 4) * 1024 + b * 512 + (j % 4) * 128
                         for j in range(nblk)]
                groups = [list(range(g, min(g + 3, nblk))) for g in range(0, nblk, 3)]
                grp_tr = {}
                grp_ve = {}
                for gi, grp in enumerate(groups):
                    scp = ps_sc[gi % 2]
                    if gi >= 2:
                        ctx.wait("tensor", S["tr"], grp_tr[gi - 2])
                    for i, j in enumerate(grp):
                        ctx.op("tensor", lambda e, scp=scp, i=i, kc=kcols[j], qtile=qtile:
                               e.matmul(scp[:, i * 512:(i + 1) * 512],
                                        kb_s[:, kc:kc + 128], qtile,
                                        start=True, stop=True),
                               S["pe"] if i == len(grp) - 1 else None)
                    ctx.wait("scalar", S["pe"], S["pe"].count)
                    if gi >= 2:
                        ctx.wait("scalar", S["ve"], grp_ve[gi - 2])
                    eb = ebuf[gi % 2]
                    ng = len(grp)
                    ctx.op("scalar", lambda e, scp=scp, eb=eb, ng=ng:
                           e.activation(eb[:, 0:ng * 512], scp[:, 0:ng * 512],
                                        AF.Exp, bias=cv[:, CV_SHIFT01:CV_SHIFT01 + 1],
                                        scale=0.08838834764831845), S["tr"])
                    grp_tr[gi] = S["tr"].count
                    ctx.wait("vector", S["tr"], S["tr"].count)
                    for i, j in enumerate(grp):
                        esl = eb[:, i * 512:(i + 1) * 512]
                        jp = j - (nblk - 4)
                        if jp >= 0:
                            ctx.op("vector", lambda e, esl=esl, jp=jp:
                                   e.tensor_tensor(esl, esl,
                                                   tri[:, (3 - jp) * 128:(3 - jp) * 128 + 512],
                                                   ALU.mult))
                        if j == 0:
                            ctx.op("vector", lambda e, esl=esl:
                                   e.tensor_copy(esum[:, :], esl))
                        else:
                            ctx.op("vector", lambda e, esl=esl:
                                   e.tensor_tensor(esum[:, :], esum[:, :], esl, ALU.add))
                    ctx.op("vector", lambda e: e.engine_nop(), S["ve"])
                    grp_ve[gi] = S["ve"].count
                    ctx.wait("tensor", S["ve"], grp_ve[gi])
                    for i, j in enumerate(grp):
                        kbv = kcols[j] // 128
                        ctx.op("tensor", lambda e, i=i, eb=eb, kbv=kbv,
                               fs=(j == 0), l=(j == nblk - 1):
                               e.matmul(ps_av[:, :], vt_s[:, kbv * 128:(kbv + 1) * 128],
                                        eb[:, i * 512:(i + 1) * 512],
                                        start=fs, stop=l),
                               S["pe"] if j == nblk - 1 else None)
                av_done = S["pe"].count
                ctx.op("tensor", lambda e: e.matmul(ps_m[0:1, :], onesb[:, :],
                                                    esum[:, :], start=True, stop=True),
                       S["pe"])
                ctx.wait("vector", S["pe"], S["pe"].count)
                ctx.op("vector", lambda e: e.reciprocal(rd_row[:, :], ps_m[0:1, :]),
                       S["ve"])
                ctx.wait("tensor", S["ve"], S["ve"].count)
                ctx.op("tensor", lambda e: e.matmul(ps_m[:, :], onesf[:, :],
                                                    rd_row[:, :], start=True, stop=True),
                       S["pe"])
                ctx.wait("vector", S["pe"], S["pe"].count)
                aslot = (h * 16 + b * 8 + s) % 2
                ctx.wait("vector", stc[aslot].sem, stc[aslot].sem.count)
                ctx.op("vector", lambda e: e.tensor_copy(bc_sb[:, :], ps_m[:, :]))
                ctx.op("vector", lambda e, aslot=aslot:
                       e.tensor_tensor(aost[aslot][:, :], ps_av[:, :], bc_sb[:, :],
                                       ALU.mult), S["ve"])
                ev_t = S["ve"].count
                ctx.wait("tensor", S["ve"], ev_t)
                ctx.wait("sync", S["ve"], ev_t)
                stc[aslot].go(aomine[h * 128:(h + 1) * 128, qcol:qcol + 512],
                              aost[aslot][:, :])
        qload[h] = (S["pe"], S["pe"].count)
    for chn in stc:
        ctx.wait("gpsimd", chn.sem, chn.sem.count)
    return ctx


def host_trimask():
    t = (np.arange(128)[:, None] <= np.arange(896)[None, :] - 384)
    return np.ascontiguousarray(t.astype(np.float32))


def host_B_inputs(qkvs, core):
    qm, km, vtokm = host_reshard_qkv(qkvs, core)
    return dict(qmine=qm, kmine=km, vtok=vtokm, trimask=host_trimask(),
                cvec=host_cvec())


def host_reshard_ao(aos, core):
    cols = np.concatenate([np.arange(core * 1024, core * 1024 + 512),
                           np.arange(core * 1024 + 512, core * 1024 + 1024)])
    out = np.concatenate([np.asarray(a)[:, cols] for a in aos], axis=0)
    return np.ascontiguousarray(out)


def build_C2A3():
    ctx = Ctx()
    xAp = ctx.dram("xAp", [D, TOK], F32, kind="ExternalInput")
    aoT = ctx.dram("aoT", [D, TOK], BF16, kind="ExternalInput")
    woT = ctx.dram("woT", [D, D], F32, kind="ExternalInput")
    cvec = ctx.dram("cvec", [128, 8], F32, kind="ExternalInput")
    ident = ctx.dram("ident", [128, 128], F32, kind="ExternalInput")
    mask8 = ctx.dram("mask8", [D, D], U8, kind="ExternalInput")
    w2aT = ctx.dram("w2aT", [D, D], F32, kind="ExternalInput")
    bias2 = ctx.dram("bias2", [128, NFC], F32, kind="ExternalInput")
    indT = ctx.dram("indT", [128, D], F32, kind="ExternalInput")
    rep4 = ctx.dram("rep4", [4, 128], F32, kind="ExternalInput")
    g4 = ctx.dram("g4", [128, 4], F32, kind="ExternalInput")
    w1qT = ctx.dram("w1qT", [128, 86 * 1024], F32, kind="ExternalInput")
    w3qT = ctx.dram("w3qT", [128, 86 * 1024], F32, kind="ExternalInput")
    w2qT = ctx.dram("w2qT", [2752, D], F32, kind="ExternalInput")
    a1T = ctx.dram("a1T", [D, 2], F32, kind="ExternalInput")
    a3T = ctx.dram("a3T", [D, 2], F32, kind="ExternalInput")
    a2Tp = ctx.dram("a2Tp", [128, 172], F32, kind="ExternalInput")
    b1T = ctx.dram("b1T", [2, 11008], F32, kind="ExternalInput")
    b3T = ctx.dram("b3T", [2, 11008], F32, kind="ExternalInput")
    b2T = ctx.dram("b2T", [2, D], F32, kind="ExternalInput")
    wqkv3 = ctx.dram("wqkv3", [D, 3072], F32, kind="ExternalInput")
    bqkv3 = ctx.dram("bqkv3", [128, 24], F32, kind="ExternalInput")
    kern8 = ctx.dram("kern8", [512, D], I8, kind="ExternalInput")
    qkv3T = ctx.dram("qkv3T", [3072, TOK], BF16, kind="ExternalOutput")
    xA3 = ctx.dram("xA3", [D, TOK], F32, kind="ExternalOutput")
    xsrc = ctx.dram("x", [D, TOK], F32)
    x2 = ctx.dram("x2", [D, TOK], F32)
    bmd = ctx.dram("bmd", [128, 128], F32)
    bxd = ctx.dram("bxd", [128, D], BF16)

    S = {k: ctx.sem(k) for k in
         ("wl", "xl", "pe", "peog", "ev", "tr", "ve", "ac", "st", "pe2",
          "tb", "ao", "xc", "xac", "ms", "ve2")}
    acc2 = [ctx.chan("gpsimd", "acc0"), ctx.chan("gpsimd", "acc1")]
    stc = [ctx.chan("sync", "stc0"), ctx.chan("sync", "stc1")]
    mbch = ctx.chan("gpsimd", "mbch")
    bmch = ctx.chan("sync", "bmch")
    b4ch = ctx.chan("gpsimd", "b4ch")
    lwch = [ctx.chan("gpsimd", "lw0"), ctx.chan("gpsimd", "lw1")]

    pp = [ctx.psum(f"pp{i}", [128, TOK], F32) for i in range(3)]
    pp3 = ctx.psum("pp3", [128, 512], F32)
    pm2a = pp3[0:2, 0:512]
    pm2b = pp[2][0:2, 0:512]

    big = ctx.sbuf("big", [128, NFC * TOK], BF16)
    hf = big[:, 0:22 * TOK]
    onesb = ctx.sbuf("onesb", [128, 1], BF16)
    onesf = ctx.sbuf("onesf", [1, 128], F32)
    cv = ctx.sbuf("cv", [128, 8], F32)
    identb = ctx.sbuf("identb", [128, 128], BF16)
    fstage = [ctx.sbuf(f"fstage{i}", [128, TOK], F32) for i in range(2)]
    stage = [ctx.sbuf(f"stage{i}", [128, TOK], BF16) for i in range(2)]
    aux = dict(onesb=onesb, onesf=onesf, cv=cv, pp=pp, identb=identb)

    mb = ctx.sbuf("mb", [128, D], BF16)
    cpl = ctx.sbuf("cpl", [128, 128], F32)
    rtmp = ctx.sbuf("rtmp", [4, 128], F32)
    bmf = ctx.sbuf("bmf", [128, 128], F32)
    binb = ctx.sbuf("binb", [128, 128], BF16)
    indsb = ctx.sbuf("indsb", [128, D], BF16)
    rep4b = ctx.sbuf("rep4b", [4, 128], BF16)
    g4b = ctx.sbuf("g4b", [128, 4], BF16)
    b4sb = ctx.sbuf("b4sb", [4, D], BF16)
    wmb = [ctx.sbuf(f"wmb{i}", [128, 384], BF16) for i in range(2)]
    biasb = ctx.sbuf("c2_bias", [128, NFC], F32)

    xf4 = ctx.sbuf("xf4", [128, 8 * TOK], BF16)
    a1sb = ctx.sbuf("a1sb", [128, 64], BF16)
    a3sb = ctx.sbuf("a3sb", [128, 64], BF16)
    a2sb = ctx.sbuf("a2sb", [128, 172], BF16)
    b1sb = ctx.sbuf("b1sb", [2, 11008 // 64], BF16)
    b3sb = ctx.sbuf("b3sb", [2, 11008 // 64], BF16)
    b2sb = ctx.sbuf("b2sb", [2, D], BF16)
    mid1b = ctx.sbuf("mid1b", [2, TOK], BF16)
    mid3b = ctx.sbuf("mid3b", [2, TOK], BF16)
    mid2b = ctx.sbuf("mid2b", [2, TOK], BF16)
    gsb = ctx.sbuf("gsb", [128, TOK], BF16)
    usb = ctx.sbuf("usb", [128, TOK], BF16)
    hsb = ctx.sbuf("hsb", [128, TOK], BF16)

    xf8 = ctx.sbuf("xf8", [128, 4 * TOK], BF16)
    bq3 = ctx.sbuf("bq3", [128, 24], F32)

    ctx.op("gpsimd", lambda e: e.memset(onesb[:, :], 1.0))
    ctx.op("gpsimd", lambda e: e.memset(b4sb[:, :], 0.0))
    ctx.op("gpsimd", lambda e: e.memset(onesf[:, :], 1.0), S["pe2"], 1)
    ctx.wait("tensor", S["pe2"], 1)
    for dst, srcp in ((cv, cvec), (identb, ident), (indsb, indT), (rep4b, rep4),
                     (g4b, g4), (a1sb, None), (a3sb, None), (a2sb, a2Tp),
                     (b2sb, b2T), (biasb, bias2), (bq3, bqkv3)):
        if dst is a1sb:
            ctx.dma("gpsimd", a1sb[:, :].rearrange("p (c r) -> p c r", r=2),
                    a1T.rearrange("(c p) r -> p c r", p=128), S["tb"])
        elif dst is a3sb:
            ctx.dma("gpsimd", a3sb[:, :].rearrange("p (c r) -> p c r", r=2),
                    a3T.rearrange("(c p) r -> p c r", p=128), S["tb"])
        else:
            ctx.dma("gpsimd", dst[:, :], srcp, S["tb"])
    tabs = S["tb"].count
    for eng in ("tensor", "vector", "scalar"):
        ctx.wait(eng, S["tb"], tabs)

    ctx.dma("gpsimd", xsrc, xAp, S["xc"])
    xcopy_tgt = S["xc"].count
    for fc in range(NFC):
        ctx.dma("sync", big[:, fc * TOK:(fc + 1) * TOK],
                aoT[fc * 128:(fc + 1) * 128, :], S["ao"])
    ctx.wait("tensor", S["ao"], S["ao"].count)

    wo_acc = {}

    def wo_evac(oc, ps, og_tgt):
        ctx.wait("vector", S["peog"], og_tgt)
        slot = oc % 2
        if oc >= 2:
            ctx.wait("vector", acc2[slot].sem, wo_acc[oc - 2])
        ctx.op("vector", lambda e, s=slot, p=ps:
               e.tensor_copy(fstage[s][:, :], p[:, :]), S["ev"])
        ctx.wait("gpsimd", S["ev"], S["ev"].count)
        ctx.wait("gpsimd", S["xc"], xcopy_tgt)
        acc2[slot].go(xsrc[oc * 128:(oc + 1) * 128, :],
                      fstage[slot][:, :], accum_op=ALU.add)
        wo_acc[oc] = acc2[slot].sem.count
    emit_mm(ctx, "wo", woT, NFC, NFC,
            lambda ic: big[:, ic * TOK:(ic + 1) * TOK], wo_evac,
            S, pp=pp, goc=3, icb=8)
    for _c in acc2:
        ctx.wait("gpsimd", _c.sem, _c.sem.count)

    sems_ln = dict(xl=S["xl"], pe=S["pe"], ve=S["ve"], tr=S["tr"], ac=S["ve"])
    ctx.barrier_from("tensor", S["ev"], S["ve"])
    emit_ln(ctx, "ln2", xsrc, big, aux, sems_ln)
    ctx.wait("gpsimd", S["ve"], S["ve"].count)
    ctx.dma("gpsimd", x2, xsrc, S["xac"])
    x2copy = S["xac"].count

    ctx.barrier_from("tensor", S["ve"])
    ctx.barrier_from("scalar", S["ve"])
    mb_dve = 0
    for rc in range(NFC):
        if rc > 0:
            ctx.wait("gpsimd", S["ve2"], mb_dve)
        mbch.go(mb[:, :], mask8[rc * 128:(rc + 1) * 128, :])
        ctx.wait("vector", mbch.sem, mbch.sem.count)
        ctx.op("vector", lambda e: e.tensor_reduce(
            cpl[:, :].rearrange("p (c o) -> p c o", o=1),
            mb[:, :].rearrange("p (c w) -> p c w", w=32),
            axis=AX.X, op=ALU.add), S["ve2"])
        mb_dve = S["ve2"].count
        ctx.wait("tensor", S["ve2"], mb_dve)
        ctx.op("vector", lambda e: e.tensor_copy(mb[:, 0:128], cpl[:, :]), S["ve2"])
        ctx.wait("tensor", S["ve2"], S["ve2"].count)
        ctx.op("tensor", lambda e: e.matmul(pp3[0:4, 0:512], g4b[:, :],
                                            mb[:, 0:512], start=True, stop=True),
               S["peog"])
        ctx.wait("vector", S["peog"], S["peog"].count)
        if rc > 0:
            ctx.wait("vector", bmch.sem, bmch.sem.count)
        ctx.op("vector", lambda e: e.tensor_copy(rtmp[:, :], pp3[0:4, 0:128]),
               S["ve2"])
        ctx.wait("sync", S["ve2"], S["ve2"].count)
        bmch.go(bmd[4 * rc:4 * rc + 4, :], rtmp[:, :])
    ctx.wait("gpsimd", bmch.sem, bmch.sem.count)
    ctx.dma("gpsimd", bmf[:, :], bmd, S["ms"])
    ctx.wait("scalar", S["ms"], S["ms"].count)
    ctx.op("scalar", lambda e: e.activation(binb[:, :], bmf[:, :], AF.Sign), S["tr"])
    bin_tr = S["tr"].count
    ctx.wait("tensor", S["tr"], bin_tr)
    for hblk in range(8):
        ctx.op("tensor", lambda e, hblk=hblk: e.matmul(
            pp3[:, 0:512], binb[:, :], indsb[:, hblk * 512:(hblk + 1) * 512],
            start=True, stop=True), S["peog"])
        ctx.wait("scalar", S["peog"], S["peog"].count)
        ctx.op("scalar", lambda e, hblk=hblk: e.activation(
            stage[hblk % 2][:, 0:512], pp3[:, 0:512], AF.Identity,
            bias=0.0, scale=1.0), S["tr"])
        ctx.wait("sync", S["tr"], S["tr"].count)
        stc[hblk % 2].go(bxd[:, hblk * 512:(hblk + 1) * 512],
                         stage[hblk % 2][:, 0:512])
        ctx.wait("tensor", stc[hblk % 2].sem, stc[hblk % 2].sem.count)
    for chn in stc:
        ctx.wait("gpsimd", chn.sem, chn.sem.count)

    n_out = NFC
    ogs = [list(range(g, min(g + 3, n_out))) for g in range(0, n_out, 3)]
    ctx.barrier_from("vector", *[c.sem for c in acc2])
    at_acc = {}
    lidx = 0
    lrel = {}
    ev_prev = []
    for gi, og in enumerate(ogs):
        ncols = len(og) * 128
        col0 = og[0] * 128
        for ic in range(NFC):
            slot = lidx % 2
            if lidx >= 2:
                rs, rv = lrel[lidx - 2]
                ctx.wait("gpsimd", rs, rv)
            lwch[slot].go(stage[slot][:, 0:ncols],
                          w2aT[ic * 128:(ic + 1) * 128, col0:col0 + ncols])
            wl_tgt = lwch[slot].sem.count
            b4ch.go(b4sb[:, 0:ncols], bxd[4 * ic:4 * ic + 4, col0:col0 + ncols])
            b4t = b4ch.sem.count
            ctx.wait("tensor", b4ch.sem, b4t)
            if gi > 0 and ic == 0:
                ctx.wait("tensor", S["ev"], ev_prev[gi - 1])
            ctx.op("tensor", lambda e: e.matmul(
                pp3[:, 0:512], rep4b[:, :], b4sb[:, 0:512], start=True, stop=True),
                S["peog"])
            ctx.wait("vector", S["peog"], S["peog"].count)
            ctx.wait("vector", lwch[slot].sem, wl_tgt)
            ctx.op("vector", lambda e, s=slot, n=ncols: e.tensor_tensor(
                wmb[s][:, 0:n], stage[s][:, 0:n], pp3[:, 0:n], ALU.mult),
                S["ve"])
            wm_t = S["ve"].count
            ctx.wait("tensor", S["ve"], wm_t)
            for oi, oc in enumerate(og):
                for h in range(2):
                    t0, t1 = h * 512, (h + 1) * 512
                    last = (ic == NFC - 1)
                    sem1 = None
                    if oi == len(og) - 1 and h == 1:
                        sem1 = S["peog"] if last else S["pe"]
                    ctx.op("tensor", lambda e, oi=oi, s=slot, o0=oi * 128,
                           t0=t0, t1=t1, fs=(ic == 0), l=last:
                           e.matmul(pp[oi][:, t0:t1], wmb[s][:, o0:o0 + 128],
                                    big[:, ic * TOK + t0:ic * TOK + t1],
                                    start=fs, stop=l), sem1)
            if (True):
                lrel[lidx] = (S["peog"] if ic == NFC - 1 else S["pe"],
                              (S["peog"] if ic == NFC - 1 else S["pe"]).count)
            lidx += 1
        og_tgt = S["peog"].count
        for oi, oc in enumerate(og):
            slot2 = oc % 2
            ctx.wait("vector", S["peog"], og_tgt)
            if oc >= 2:
                ctx.wait("vector", acc2[slot2].sem, at_acc[oc - 2])
            ctx.op("vector", lambda e, s=slot2, oi=oi:
                   e.tensor_copy(fstage[s][:, :], pp[oi][:, :]))
            ctx.op("vector", lambda e, s=slot2, oc=oc:
                   e.tensor_scalar(fstage[s][:, :], fstage[s][:, :],
                                   biasb[:, oc:oc + 1], None, ALU.add), S["ev"])
            ctx.wait("gpsimd", S["ev"], S["ev"].count)
            ctx.wait("gpsimd", S["xac"], x2copy)
            acc2[slot2].go(x2[oc * 128:(oc + 1) * 128, :], fstage[slot2][:, :],
                           accum_op=ALU.add)
            at_acc[oc] = acc2[slot2].sem.count
        ev_prev.append(S["ev"].count)
    attn2_done_pe = S["peog"].count
    attn2_done_ve = S["ev"].count
    return _c2a3_part2(ctx, locals())


def _c2a3_part2(ctx, L):
    S = L["S"]
    acc2, stc, lwch, b4ch = L["acc2"], L["stc"], L["lwch"], L["b4ch"]
    pp, pp3, pm2a, pm2b = L["pp"], L["pp3"], L["pm2a"], L["pm2b"]
    big, hf, xf4, xf8 = L["big"], L["hf"], L["xf4"], L["xf8"]
    fstage, stage = L["fstage"], L["stage"]
    a1sb, a3sb, a2sb = L["a1sb"], L["a3sb"], L["a2sb"]
    b1sb, b3sb, b2sb = L["b1sb"], L["b3sb"], L["b2sb"]
    mid1b, mid3b, mid2b = L["mid1b"], L["mid3b"], L["mid2b"]
    gsb, usb, hsb = L["gsb"], L["usb"], L["hsb"]
    aux, cv, bq3 = L["aux"], L["cv"], L["bq3"]
    w1qT, w3qT, w2qT = L["w1qT"], L["w3qT"], L["w2qT"]
    b1T, b3T = L["b1T"], L["b3T"]
    wqkv3, kern8 = L["wqkv3"], L["kern8"]
    qkv3T, xA3, x2 = L["qkv3T"], L["xA3"], L["x2"]
    x2copy = L["x2copy"]

    ctx.barrier_from("vector", S["peog"])
    for f in range(8):
        d = xf4[:, f * TOK:(f + 1) * TOK]
        ctx.op("vector", lambda e, d=d, f=f: e.tensor_tensor(
            d, big[:, f * TOK:(f + 1) * TOK], big[:, (f + 8) * TOK:(f + 9) * TOK],
            ALU.add))
        ctx.op("vector", lambda e, d=d, f=f: e.tensor_tensor(
            d, d, big[:, (f + 16) * TOK:(f + 17) * TOK], ALU.add))
        ctx.op("vector", lambda e, d=d, f=f: e.tensor_tensor(
            d, d, big[:, (f + 24) * TOK:(f + 25) * TOK], ALU.add),
            S["ve"] if f == 7 else None)
    xf4_done = S["ve"].count

    ctx.barrier_from("tensor", S["ev"], S["ve"])
    for mb_, aT, ppx in ((mid1b, a1sb, pp[0]), (mid3b, a3sb, pp[1])):
        for ic in range(NFC):
            for h2 in range(2):
                t0, t1 = h2 * 512, (h2 + 1) * 512
                ctx.op("tensor", lambda e, aT=aT, ic=ic, t0=t0, t1=t1, ppx=ppx:
                       e.matmul(ppx[0:2, t0:t1], aT[:, 2 * ic:2 * ic + 2],
                                big[:, ic * TOK + t0:ic * TOK + t1],
                                start=(ic == 0), stop=(ic == NFC - 1)),
                       S["peog"] if (ic == NFC - 1 and h2 == 1) else None)
        ctx.wait("vector", S["peog"], S["peog"].count)
        ctx.op("vector", lambda e, mb_=mb_, ppx=ppx: e.tensor_scalar(
            mb_[:, :], ppx[0:2, :], 0.5, None, ALU.mult), S["ve"])
    mids_done = S["ve"].count
    ctx.wait("tensor", S["ve"], mids_done)

    w1b = [ctx.sbuf(f"w1b{i}", [128, 1024], BF16) for i in range(2)]
    w3b = [ctx.sbuf(f"w3b{i}", [128, 1024], BF16) for i in range(2)]
    hstmp = L["mb"][:, 0:TOK]
    NOC = 86
    hw_written = set()
    yrel = {}
    hfold_ve = {}
    for oc in range(NOC):
        slot = oc % 2
        if oc >= 2:
            rs, rv = yrel[oc - 2]
            ctx.wait("gpsimd", rs, rv)
        lwch[slot].go(w1b[slot][:, :].rearrange("p (i c) -> p i c", i=8),
                      w1qT[:, oc * 128:(oc + 1) * 128]
                      .rearrange("(i p) c -> p i c", p=128))
        lwch[slot].go(w3b[slot][:, :].rearrange("p (i c) -> p i c", i=8),
                      w3qT[:, oc * 128:(oc + 1) * 128]
                      .rearrange("(i p) c -> p i c", p=128))
        w_tgt = lwch[slot].sem.count
        b4ch.go(b1sb[:, 0:128], b1T[:, oc * 128:(oc + 1) * 128])
        b4ch.go(b3sb[:, 0:128], b3T[:, oc * 128:(oc + 1) * 128])
        b_tgt = b4ch.sem.count
        ctx.wait("tensor", lwch[slot].sem, w_tgt)
        ctx.wait("tensor", b4ch.sem, b_tgt)
        if oc >= 1:
            ctx.wait("tensor", S["tr"], sil_t[0])
            ctx.wait("tensor", S["ve"], sil_t[1])
        for wbuf, ppx, midb, bsb in ((w1b[slot], pp[0], mid1b, b1sb),
                                     (w3b[slot], pp[1], mid3b, b3sb)):
            for ic in range(8):
                for h2 in range(2):
                    t0, t1 = h2 * 512, (h2 + 1) * 512
                    ctx.op("tensor", lambda e, w=wbuf, ic=ic, t0=t0, t1=t1, ppx=ppx:
                           e.matmul(ppx[:, t0:t1], w[:, ic * 128:(ic + 1) * 128],
                                    xf4[:, ic * TOK + t0:ic * TOK + t1],
                                    start=(ic == 0), stop=False),
                           S["pe"] if (ic == 7 and h2 == 1) else None)
            for h2 in range(2):
                t0, t1 = h2 * 512, (h2 + 1) * 512
                ctx.op("tensor", lambda e, b=bsb, m=midb, t0=t0, t1=t1, ppx=ppx:
                       e.matmul(ppx[:, t0:t1], b[:, 0:128], m[:, t0:t1],
                                start=False, stop=True),
                       S["peog"] if h2 == 1 else None)
        yrel[oc] = (S["pe"], S["pe"].count)
        y_done = S["peog"].count
        ctx.wait("scalar", S["peog"], y_done)
        ctx.wait("vector", S["peog"], y_done)
        ctx.op("scalar", lambda e: e.activation(gsb[:, :], pp[0][:, :], AF.Silu),
               S["tr"])
        sil_a = S["tr"].count
        ctx.op("vector", lambda e: e.tensor_copy(usb[:, :], pp[1][:, :]), S["ve"])
        sil_b = S["ve"].count
        sil_t = (sil_a, sil_b)
        ctx.wait("vector", S["tr"], sil_a)
        ctx.op("vector", lambda e: e.tensor_tensor(hsb[:, :], gsb[:, :], usb[:, :],
                                                   ALU.mult), S["ve"])
        h_t = S["ve"].count
        ctx.wait("tensor", S["ve"], h_t)
        for h2 in range(2):
            t0, t1 = h2 * 512, (h2 + 1) * 512
            pm = pm2a if h2 == 0 else pm2b
            ctx.op("tensor", lambda e, pm=pm, t0=t0, t1=t1, oc=oc:
                   e.matmul(pm[:, :], a2sb[:, 2 * oc:2 * oc + 2], hsb[:, t0:t1],
                            start=(oc == 0), stop=(oc == NOC - 1),
                            skip_group_check=True),
                   S["pe2"] if (oc == NOC - 1 and h2 == 1) else None)
        t0g = (128 * oc) % 2752
        pieces = []
        src0 = 0
        rem = 128
        cur = t0g
        while rem > 0:
            tile = cur // 128
            off = cur % 128
            ln = min(rem, 128 - off, 2752 - cur)
            pieces.append((tile, off, src0, ln))
            src0 += ln
            cur = (cur + ln) % 2752
            rem -= ln
        for tile, off, s0, ln in pieces:
            dstap = hf[off:off + ln, tile * TOK:(tile + 1) * TOK]
            srcap = hsb[s0:s0 + ln, :]
            covered = any(k[0] == tile and k[1] <= off and k[1] + k[2] >= off + ln
                          for k in hw_written)
            if covered:
                if off != s0:
                    ctx.op("vector", lambda e, s=srcap, off=off, ln=ln:
                           e.tensor_copy(hstmp[off:off + ln, :], s))
                    srcap = hstmp[off:off + ln, :]
                ctx.op("vector", lambda e, d=dstap, s=srcap:
                       e.tensor_tensor(d, d, s, ALU.add),
                       S["ve"])
            else:
                ctx.op("vector", lambda e, d=dstap, s=srcap: e.tensor_copy(d, s),
                       S["ve"])
                hw_written.add((tile, off, ln))
        hfold_ve[oc] = S["ve"].count
    hf_done = S["ve"].count
    ctx.wait("vector", S["pe2"], S["pe2"].count)
    ctx.op("vector", lambda e: e.tensor_scalar(mid2b[:, 0:512], pm2a[:, :], 0.5,
                                               None, ALU.mult))
    ctx.op("vector", lambda e: e.tensor_scalar(mid2b[:, 512:1024], pm2b[:, :], 0.5,
                                               None, ALU.mult), S["ve"])
    ctx.wait("tensor", S["ve"], S["ve"].count)

    ctx.barrier_from("vector", *[c.sem for c in acc2])
    dn_acc = {}
    lidx = 0
    lrel = {}
    ev_prev2 = []
    ogs = [list(range(g, min(g + 3, NFC), 1)) for g in range(0, NFC, 3)]
    for gi, og in enumerate(ogs):
        ncols = len(og) * 128
        col0 = og[0] * 128
        for ic in range(22):
            rows = 128 if ic < 21 else 64
            slot = lidx % 2
            if lidx >= 2:
                rs, rv = lrel[lidx - 2]
                ctx.wait("gpsimd", rs, rv)
            lwch[slot].go(stage[slot][0:rows, 0:ncols],
                          w2qT[ic * 128:ic * 128 + rows, col0:col0 + ncols])
            w_tgt = lwch[slot].sem.count
            ctx.wait("tensor", lwch[slot].sem, w_tgt)
            if gi > 0 and ic == 0:
                ctx.wait("tensor", S["ev"], ev_prev2[gi - 1])
            for oi, oc in enumerate(og):
                for h2 in range(2):
                    t0, t1 = h2 * 512, (h2 + 1) * 512
                    sem1 = None
                    if oi == len(og) - 1 and h2 == 1:
                        sem1 = S["pe"]
                    ctx.op("tensor", lambda e, s=slot, oi=oi, o0=oi * 128, r=rows,
                           ic=ic, t0=t0, t1=t1:
                           e.matmul(pp[oi][:, t0:t1],
                                    stage[s][0:r, o0:o0 + 128],
                                    hf[0:r, ic * TOK + t0:ic * TOK + t1],
                                    start=(ic == 0), stop=False), sem1)
            lrel[lidx] = (S["pe"], S["pe"].count)
            lidx += 1
        for oi, oc in enumerate(og):
            for h2 in range(2):
                t0, t1 = h2 * 512, (h2 + 1) * 512
                ctx.op("tensor", lambda e, oi=oi, oc=oc, t0=t0, t1=t1:
                       e.matmul(pp[oi][:, t0:t1], b2sb[:, oc * 128:oc * 128 + 128],
                                mid2b[:, t0:t1], start=False, stop=True),
                       S["peog"] if (oi == len(og) - 1 and h2 == 1) else None)
        og_tgt = S["peog"].count
        for oi, oc in enumerate(og):
            slot2 = oc % 2
            ctx.wait("vector", S["peog"], og_tgt)
            if oc >= 2:
                ctx.wait("vector", acc2[slot2].sem, dn_acc.get(oc - 2, 0))
            ctx.op("vector", lambda e, s=slot2, oi=oi:
                   e.tensor_copy(fstage[s][:, :], pp[oi][:, :]), S["ev"])
            ctx.wait("gpsimd", S["ev"], S["ev"].count)
            acc2[slot2].go(x2[oc * 128:(oc + 1) * 128, :], fstage[slot2][:, :],
                           accum_op=ALU.add)
            dn_acc[oc] = acc2[slot2].sem.count
        ev_prev2.append(S["ev"].count)

    for _c in acc2:
        ctx.wait("gpsimd", _c.sem, _c.sem.count)
    ctx.dma("gpsimd", xA3, x2, S["xac"])
    x3copy = S["xac"].count
    sems_ln = dict(xl=S["xl"], pe=S["pe"], ve=S["ve"], tr=S["tr"], ac=S["ve"])
    ctx.barrier_from("tensor", S["ev"], S["ve"])
    emit_ln(ctx, "ln3", x2, big, aux, sems_ln)

    ctx.barrier_from("tensor", S["ve"], S["ev"])
    identb = L["identb"]
    for i in range(3):
        for h2 in range(2):
            t0, t1 = h2 * 512, (h2 + 1) * 512
            ctx.op("tensor", lambda e, i=i, t0=t0, t1=t1:
                   e.matmul(pp[i][:, t0:t1], identb[:, :], big[:, 0:512],
                            start=True, stop=True),
                   S["peog"] if (i == 2 and h2 == 1) else None)
    ctx.wait("tensor", S["peog"], S["peog"].count)
    q3_store = {}

    def qkv3_evac(oc, ps, og_tgt):
        slot = oc % 2
        ctx.wait("scalar", S["peog"], og_tgt)
        if oc >= 2:
            ctx.wait("scalar", stc[slot].sem, q3_store[oc - 2])
        ctx.op("scalar", lambda e, s=slot, p=ps, oc=oc:
               e.activation(stage[s][:, 0:TOK], p[:, :], AF.Identity,
                            bias=bq3[:, oc:oc + 1], scale=1.0), S["ev"])
        ctx.wait("sync", S["ev"], S["ev"].count)
        stc[slot].go(qkv3T[oc * 128:(oc + 1) * 128, :], stage[slot][:, 0:TOK])
        q3_store[oc] = stc[slot].sem.count

    emit_mm(ctx, "q3", wqkv3, NFC, 24,
            lambda ic: big[:, ic * TOK:(ic + 1) * TOK], qkv3_evac,
            S, pp=pp, goc=3, icb=4)

    for f in range(4):
        d = xf8[:, f * TOK:(f + 1) * TOK]
        ctx.op("vector", lambda e, d=d, f=f: e.tensor_tensor(
            d, big[:, f * TOK:(f + 1) * TOK], big[:, (f + 4) * TOK:(f + 5) * TOK],
            ALU.add))
        for r in range(2, 8):
            ctx.op("vector", lambda e, d=d, f=f, r=r: e.tensor_tensor(
                d, d, big[:, (f + 4 * r) * TOK:(f + 4 * r + 1) * TOK], ALU.add),
                S["ve"] if (f == 3 and r == 7) else None)
    ctx.wait("tensor", S["ve"], S["ve"].count)
    kmb = [ctx.sbuf(f"kmb{i}", [128, 1024], BF16) for i in range(2)]
    ctx.barrier_from("tensor", S["ev"])
    f3_acc = {}
    lidx = 0
    lrel2 = {}
    ev_prev3 = []
    ctx.barrier_from("vector", *[c.sem for c in acc2])
    ogs3 = [list(range(g, min(g + 2, NFC))) for g in range(0, NFC, 2)]
    for gi, og in enumerate(ogs3):
        ncols = len(og) * 128
        col0 = og[0] * 128
        slot = gi % 2
        if gi >= 2:
            rs, rv = lrel2[gi - 2]
            ctx.wait("gpsimd", rs, rv)
        lwch[slot].go(kmb[slot][:, 0:4 * ncols].rearrange("p (i c) -> p i c", i=4),
                      kern8[:, col0:col0 + ncols].rearrange("(i p) c -> p i c", p=128))
        w_tgt = lwch[slot].sem.count
        ctx.wait("tensor", lwch[slot].sem, w_tgt)
        if gi > 0:
            ctx.wait("tensor", S["ev"], ev_prev3[gi - 1])
        for ic in range(4):
            for oi, oc in enumerate(og):
                for h2 in range(2):
                    t0, t1 = h2 * 512, (h2 + 1) * 512
                    sem1 = S["peog"] if (ic == 3 and oi == len(og) - 1 and h2 == 1) else None
                    ctx.op("tensor", lambda e, s=slot, ic=ic, oi=oi, o0=ic * ncols + oi * 128,
                           t0=t0, t1=t1, l=(ic == 3):
                           e.matmul(pp[oi][:, t0:t1], kmb[s][:, o0:o0 + 128],
                                    xf8[:, ic * TOK + t0:ic * TOK + t1],
                                    start=(ic == 0), stop=l), sem1)
        lrel2[gi] = (S["peog"], S["peog"].count)
        og_tgt = S["peog"].count
        for oi, oc in enumerate(og):
            slot2 = oc % 2
            ctx.wait("vector", S["peog"], og_tgt)
            if oc >= 2:
                ctx.wait("vector", acc2[slot2].sem, f3_acc.get(oc - 2, 0))
            ctx.op("vector", lambda e, s=slot2, oi=oi:
                   e.tensor_copy(fstage[s][:, :], pp[oi][:, :]), S["ev"])
            ctx.wait("gpsimd", S["ev"], S["ev"].count)
            ctx.wait("gpsimd", S["xac"], x3copy)
            acc2[slot2].go(xA3[oc * 128:(oc + 1) * 128, :], fstage[slot2][:, :],
                           accum_op=ALU.add)
            f3_acc[oc] = acc2[slot2].sem.count
        ev_prev3.append(S["ev"].count)

    for ch in acc2 + stc:
        ctx.wait("gpsimd", ch.sem, ch.sem.count)
    ctx.barrier_from("gpsimd", S["xac"])
    return ctx


def build_B3():
    ctx = Ctx()
    q3 = ctx.dram("q3", [1024, TOK], BF16, kind="ExternalInput")
    k3all = ctx.dram("k3all", [1024, 8192], BF16, kind="ExternalInput")
    v3tok = ctx.dram("v3tok", [8192, 1024], BF16, kind="ExternalInput")
    xA3 = ctx.dram("xA3", [D, TOK], F32, kind="ExternalInput")
    wo3 = ctx.dram("wo3", [1024, D], F32, kind="ExternalInput")
    bo3 = ctx.dram("bo3", [128, NFC], F32, kind="ExternalInput")
    cvec = ctx.dram("cvec", [128, 8], F32, kind="ExternalInput")
    xout = ctx.dram("xout", [D, TOK], F32, kind="ExternalOutput")

    S = {k: ctx.sem(k) for k in ("pe", "peog", "ev", "tr", "ve", "xac", "q")}
    kch = ctx.chan("gpsimd", "kch")
    vch = [ctx.chan("gpsimd", "vch0"), ctx.chan("gpsimd", "vch1")]
    acc2 = [ctx.chan("gpsimd", "acc0"), ctx.chan("gpsimd", "acc1")]

    q3sb = ctx.sbuf("q3sb", [128, 8 * TOK], BF16)
    k3sb = ctx.sbuf("k3sb", [128, 8 * 4096], BF16)
    efull = ctx.sbuf("efull", [128, 32 * 512], BF16)
    esum = ctx.sbuf("esum", [128, 512], BF16)
    v3sb = [ctx.sbuf(f"v3sb{i}", [128, 32 * 128], BF16) for i in range(2)]
    ao3 = ctx.sbuf("ao3", [128, 8 * TOK], BF16)
    rd_row = ctx.sbuf("rd_row", [1, 512], F32)
    bc_sb = ctx.sbuf("bc_sb", [128, 512], F32)
    onesb = ctx.sbuf("onesb", [128, 1], BF16)
    onesf = ctx.sbuf("onesf", [1, 128], F32)
    cv = ctx.sbuf("cv", [128, 8], F32)
    fstage = [ctx.sbuf(f"fstage{i}", [128, TOK], F32) for i in range(2)]

    ps_sc = [ctx.psum(f"ps_sc{i}", [128, 1536], F32) for i in range(2)]
    ps_av = ctx.psum("ps_av", [128, 512], F32)
    ps_m = ctx.psum("ps_m", [128, 512], F32)

    ctx.op("gpsimd", lambda e: e.memset(onesb[:, :], 1.0))
    ctx.op("gpsimd", lambda e: e.memset(onesf[:, :], 1.0))
    ctx.dma("gpsimd", cv[:, :], cvec, S["q"])
    ctx.dma("gpsimd", q3sb[:, :].rearrange("p (dc t) -> p dc t", t=TOK),
            q3.rearrange("(dc p) t -> p dc t", p=128), S["q"])
    ctx.dma("gpsimd", xout, xA3, S["xac"])
    x_copy = S["xac"].count
    q_tgt = S["q"].count
    for eng in ("tensor", "vector", "scalar"):
        ctx.wait(eng, S["q"], q_tgt)

    av_last = 0
    for b in range(2):
        kch.go(k3sb[:, :].rearrange("p (dc t) -> p dc t", t=4096),
               k3all[:, b * 4096:(b + 1) * 4096].rearrange("(dc p) t -> p dc t", p=128))
        k_tgt = kch.sem.count
        ctx.wait("tensor", kch.sem, k_tgt)
        if b == 1:
            ctx.wait("scalar", S["pe"], av_last)
        groups = [list(range(g, min(g + 3, 32))) for g in range(0, 32, 3)]
        grp_tr = {}
        for gi, grp in enumerate(groups):
            scp = ps_sc[gi % 2]
            if gi >= 2:
                ctx.wait("tensor", S["tr"], grp_tr[gi - 2])
            for i, kb in enumerate(grp):
                for dc in range(8):
                    ctx.op("tensor", lambda e, scp=scp, i=i, kb=kb, dc=dc, b=b:
                           e.matmul(scp[:, i * 512:(i + 1) * 512],
                                    k3sb[:, dc * 4096 + kb * 128: dc * 4096 + kb * 128 + 128],
                                    q3sb[:, dc * TOK + b * 512: dc * TOK + b * 512 + 512],
                                    start=(dc == 0), stop=(dc == 7)),
                           S["pe"] if (i == len(grp) - 1 and dc == 7) else None)
            ctx.wait("scalar", S["pe"], S["pe"].count)
            ng = len(grp)
            ctx.op("scalar", lambda e, scp=scp, ng=ng, c0=grp[0] * 512:
                   e.activation(efull[:, c0:c0 + ng * 512], scp[:, 0:ng * 512],
                                AF.Exp, bias=cv[:, CV_SHIFT3:CV_SHIFT3 + 1],
                                scale=0.03125), S["tr"])
            grp_tr[gi] = S["tr"].count
            ctx.wait("vector", S["tr"], S["tr"].count)
            for i, kb in enumerate(grp):
                esl = efull[:, kb * 512:(kb + 1) * 512]
                if kb == 0:
                    ctx.op("vector", lambda e, esl=esl: e.tensor_copy(esum[:, :], esl))
                else:
                    ctx.op("vector", lambda e, esl=esl:
                           e.tensor_tensor(esum[:, :], esum[:, :], esl, ALU.add))
            ctx.op("vector", lambda e: e.engine_nop(), S["ve"])
        ctx.wait("tensor", S["ve"], S["ve"].count)
        ctx.op("tensor", lambda e: e.matmul(ps_m[0:1, :], onesb[:, :], esum[:, :],
                                            start=True, stop=True), S["pe"])
        ctx.wait("vector", S["pe"], S["pe"].count)
        ctx.op("vector", lambda e: e.reciprocal(rd_row[:, :], ps_m[0:1, :]), S["ve"])
        ctx.wait("tensor", S["ve"], S["ve"].count)
        ctx.op("tensor", lambda e: e.matmul(ps_m[:, :], onesf[:, :], rd_row[:, :],
                                            start=True, stop=True), S["pe"])
        ctx.wait("vector", S["pe"], S["pe"].count)
        ctx.op("vector", lambda e: e.tensor_copy(bc_sb[:, :], ps_m[:, :]), S["ve"])
        bc_t = S["ve"].count
        if b == 0:
            vrel = {}
        for dc in range(8):
            gdc = b * 8 + dc
            vslot = dc % 2
            if gdc >= 2:
                ctx.wait("gpsimd", S["pe"], vrel[gdc - 2])
            vch[vslot].go(v3sb[vslot][:, :].rearrange("p (kb d) -> p kb d", d=128),
                          v3tok[b * 4096:(b + 1) * 4096, dc * 128:(dc + 1) * 128]
                          .rearrange("(kb p) d -> p kb d", p=128))
            v_tgt = vch[vslot].sem.count
            ctx.wait("tensor", vch[vslot].sem, v_tgt)
            if b == 0 and dc == 0:
                pass
            else:
                ctx.wait("tensor", S["ve"], ev_prev_av)
            for kb in range(32):
                ctx.op("tensor", lambda e, vs=vslot, kb=kb:
                       e.matmul(ps_av[:, :], v3sb[vs][:, kb * 128:(kb + 1) * 128],
                                efull[:, kb * 512:(kb + 1) * 512],
                                start=(kb == 0), stop=(kb == 31)),
                       S["pe"] if kb == 31 else None)
            vrel[gdc] = S["pe"].count
            ctx.wait("vector", S["pe"], S["pe"].count)
            ctx.wait("vector", S["ve"], bc_t)
            ctx.op("vector", lambda e, dc=dc, b=b:
                   e.tensor_tensor(ao3[:, dc * TOK + b * 512: dc * TOK + b * 512 + 512],
                                   ps_av[:, :], bc_sb[:, :], ALU.mult), S["ve"])
            ev_prev_av = S["ve"].count
        av_last = S["pe"].count

    ctx.barrier_from("tensor", S["ve"])
    bo3sb = ctx.sbuf("bo3sb", [128, NFC], F32)
    ctx.dma("gpsimd", bo3sb[:, :], bo3, S["q"])
    ctx.wait("vector", S["q"], S["q"].count)
    w_acc = {}
    ctx.barrier_from("vector", *[c.sem for c in acc2])

    def wo3_evac(oc, ps, og_tgt):
        slot = oc % 2
        ctx.wait("vector", S["peog"], og_tgt)
        if oc >= 2:
            ctx.wait("vector", acc2[slot].sem, w_acc[oc - 2])
        ctx.op("vector", lambda e, s=slot, p=ps:
               e.tensor_copy(fstage[s][:, :], p[:, :]))
        ctx.op("vector", lambda e, s=slot, oc=oc:
               e.tensor_scalar(fstage[s][:, :], fstage[s][:, :],
                               bo3sb[:, oc:oc + 1], None, ALU.add), S["ev"])
        ctx.wait("gpsimd", S["ev"], S["ev"].count)
        ctx.wait("gpsimd", S["xac"], x_copy)
        acc2[slot].go(xout[oc * 128:(oc + 1) * 128, :], fstage[slot][:, :],
                      accum_op=ALU.add)
        w_acc[oc] = acc2[slot].sem.count

    emit_mm(ctx, "wo3", wo3, 8, NFC,
            lambda ic: ao3[:, ic * TOK:(ic + 1) * TOK], wo3_evac,
            S, pp=[ps_sc[0][:, 0:TOK], ps_sc[1][:, 0:TOK]], goc=2, icb=8)
    for ch in acc2:
        ctx.wait("gpsimd", ch.sem, ch.sem.count)
    ctx.barrier_from("gpsimd", S["xac"])
    return ctx


def host_C2A3_inputs(ins, core, xAp, aoT):
    def T_(a):
        return np.ascontiguousarray(np.asarray(a).T.astype(np.float32, copy=False))
    rb = np.arange(128)
    indT = (np.arange(D)[None, :] // 32 == rb[:, None]).astype(np.float32)
    rep4 = (np.arange(128)[None, :] // 32 == np.arange(4)[:, None]).astype(np.float32)
    g4 = (np.arange(128)[:, None] // 32 == np.arange(4)[None, :]).astype(np.float32)
    mask = np.ascontiguousarray(np.asarray(ins["L2_attn_mask"]))
    mask8 = np.ascontiguousarray(mask.view(np.int8)[:, 0::4]).view(np.uint8)
    a2 = np.asarray(ins["L2_ffn_a2"])
    a2Tp = np.ascontiguousarray(
        a2.T.reshape(86, 128, 2).transpose(1, 0, 2).reshape(128, 172)
        .astype(np.float32))
    kern = np.asarray(ins["L3_ffn_kernel"]).reshape(512, D)
    return dict(
        xAp=xAp, aoT=aoT, woT=T_(ins["L1_attn_Wo"]), cvec=host_cvec(),
        ident=np.eye(128, dtype=np.float32),
        mask8=mask8, w2aT=T_(ins["L2_attn_weight"]),
        bias2=np.ascontiguousarray(np.asarray(ins["L2_attn_bias"]).reshape(NFC, 128).T),
        indT=indT, rep4=rep4, g4=g4,
        w1qT=T_(ins["L2_ffn_w1q"]), w3qT=T_(ins["L2_ffn_w3q"]),
        w2qT=T_(ins["L2_ffn_w2q"]),
        a1T=T_(ins["L2_ffn_a1"]), a3T=T_(ins["L2_ffn_a3"]), a2Tp=a2Tp,
        b1T=T_(ins["L2_ffn_b1"]), b3T=T_(ins["L2_ffn_b3"]), b2T=T_(ins["L2_ffn_b2"]),
        wqkv3=np.ascontiguousarray(np.asarray(ins["L3_attn_Wqkv"]).astype(np.float32)),
        bqkv3=np.ascontiguousarray(np.asarray(ins["L3_attn_bqkv"]).reshape(24, 128).T),
        kern8=np.ascontiguousarray(kern),
    )


def host_B3_inputs(ins, qkv3s, core, xA3c):
    k3 = np.empty((1024, 8192), BF)
    v3 = np.empty((8192, 1024), BF)
    for r in range(8):
        qr = np.asarray(qkv3s[r])
        for b in range(2):
            k3[:, b * 4096 + 512 * r: b * 4096 + 512 * (r + 1)] = \
                qr[1024:2048, b * 512:(b + 1) * 512]
            v3[b * 4096 + 512 * r: b * 4096 + 512 * (r + 1), :] = \
                qr[2048:3072, b * 512:(b + 1) * 512].T
    return dict(
        q3=np.ascontiguousarray(np.asarray(qkv3s[core])[0:1024]),
        k3all=np.ascontiguousarray(k3), v3tok=np.ascontiguousarray(v3),
        xA3=xA3c, wo3=np.ascontiguousarray(np.asarray(ins["L3_attn_Wo"]).astype(np.float32)),
        bo3=np.ascontiguousarray(np.asarray(ins["L3_attn_bo"]).reshape(NFC, 128).T),
        cvec=host_cvec(),
    )


FUSED = True


def kernel(**ins):
    if FUSED:
        return kernel_fused(ins)
    x0 = np.asarray(ins["L0_attn_hidden_states"], np.float32)

    def run(nc, in_maps):
        return run_bass_kernel_spmd(nc, in_maps, core_ids=list(range(NCORES))).results

    nc = build_A01(0).finalize()
    r = run(nc, [host_A0_inputs(ins, c, host_xT(x0, c)) for c in range(NCORES)])
    qkv0 = [r[c]["qkvT"] for c in range(NCORES)]
    xA0 = [np.asarray(r[c]["xA"]) for c in range(NCORES)]
    ncb = build_B01().finalize()
    rb = run(ncb, [host_B_inputs(qkv0, c) for c in range(NCORES)])
    ao0 = [rb[c]["aomine"] for c in range(NCORES)]
    nca1 = build_A01(1).finalize()
    r1 = run(nca1, [host_A1_inputs(ins, c, xA0[c], host_reshard_ao(ao0, c))
                    for c in range(NCORES)])
    qkv1 = [r1[c]["qkvT"] for c in range(NCORES)]
    xA1 = [np.asarray(r1[c]["xA"]) for c in range(NCORES)]
    rb1 = run(ncb, [host_B_inputs(qkv1, c) for c in range(NCORES)])
    ao1 = [rb1[c]["aomine"] for c in range(NCORES)]
    ncc = build_C2A3().finalize()
    r2 = run(ncc, [host_C2A3_inputs(ins, c, xA1[c], host_reshard_ao(ao1, c))
                   for c in range(NCORES)])
    qkv3 = [r2[c]["qkv3T"] for c in range(NCORES)]
    xA3 = [np.asarray(r2[c]["xA3"]) for c in range(NCORES)]
    ncb3 = build_B3().finalize()
    r3 = run(ncb3, [host_B3_inputs(ins, qkv3, c, xA3[c]) for c in range(NCORES)])
    return host_unxT([np.asarray(r3[c]["xout"]) for c in range(NCORES)])


class PairComm:

    NROUNDS = 8

    def __init__(self, ctx):
        self.ctx = ctx
        self.sflag = [ctx.dram(f"sflag{r}", [4, 64], I32, addr_space="Shared")
                      for r in range(self.NROUNDS)]
        self.fidx = ctx.dram("fidx", [2, 1], I32, kind="ExternalInput")
        self.fvals = ctx.dram("fvals", [2, 64 * self.NROUNDS], I32,
                              kind="ExternalInput")
        self.s_fidx = ctx.sbuf("s_fidx", [2, 1], I32)
        self.s_fval = ctx.sbuf("s_fval", [2, 64 * self.NROUNDS], I32)
        self.s_flg = ctx.sbuf("s_flg", [2, 64], I32)
        self.sem = ctx.sem("pcflag")
        self.round = 0
        ctx.dma("gpsimd", self.s_fidx[:, :], self.fidx, self.sem)
        ctx.dma("gpsimd", self.s_fval[:, :], self.fvals, self.sem)
        ctx.wait("gpsimd", self.sem, self.sem.count)

    def publish(self, after_sems):
        ctx = self.ctx
        r = self.round
        self.round += 1
        assert r < self.NROUNDS
        for s, v in after_sems:
            ctx.wait("gpsimd", s, v)
        def f(e, r=r):
            e.indirect_dma_start(
                self.sflag[r][0:2, :],
                IndirectOffsetOnAxis(self.s_fidx[:, :], 0),
                self.s_fval[:, 64 * r:64 * (r + 1)], None).then_inc(self.sem.h, 16)
        ctx.emit("gpsimd", f)
        self.sem.count += 16
        ctx.wait("gpsimd", self.sem, self.sem.count)
        return r

    def poll(self, r):
        ctx = self.ctx
        psem = ctx.sem(f"pcpoll{r}")

        def f(e, r=r, psem=psem):
            with (e.register(f"it{r}") as it, e.register(f"nr{r}") as nr,
                  e.register(f"r0_{r}") as r0, e.register(f"r1_{r}") as r1,
                  e.register(f"nn{r}") as nn,
                  e.register(f"a{r}") as a, e.register(f"b{r}") as b):
                e.reg_mov(it, 0)
                e.reg_mov(nr, 1)
                e.reg_load(nn, self.s_fval[0:1, 64 * r:64 * r + 1])
                with e.While(nr):
                    e.dma_start(self.s_flg[:, :],
                                self.sflag[r][0:2, :]).then_inc(psem.h, 16)
                    e.reg_add(it, it, 16)
                    e.wait_ge(psem.h, it)
                    e.reg_load(r0, self.s_flg[0:1, 0:1])
                    e.reg_load(r1, self.s_flg[1:2, 0:1])
                    e.reg_alu(a, r0, nn, ALU.is_equal)
                    e.reg_alu(b, r1, nn, ALU.is_equal)
                    e.reg_alu(a, a, b, ALU.mult)
                    e.reg_alu(nr, a, 1, ALU.is_lt)
        ctx.emit("gpsimd", f)

    def barrier_engines(self, engines=("tensor", "vector", "scalar", "sync")):
        ctx = self.ctx
        if not hasattr(self, "bar"):
            self.bar = ctx.sem("pcbar")
            self.scr = ctx.sbuf("pc_scr", [1, 8], I32)
        ctx.op("gpsimd", lambda e: e.memset(self.scr[:, :], 0), self.bar)
        for eng in engines:
            ctx.wait(eng, self.bar, self.bar.count)


def build_fused():
    ctx = Ctx()
    pcm = PairComm(ctx)
    nc = ctx.nc
    xT2 = ctx.dram("xT2", [D, 2 * TOK], F32, kind="ExternalInput")
    cvec = ctx.dram("cvec", [128, 8], F32, kind="ExternalInput")
    ident = ctx.dram("ident", [128, 128], F32, kind="ExternalInput")
    ctab4 = ctx.dram("ctab4", [128, 4096], F32, kind="ExternalInput")
    stab4 = ctx.dram("stab4", [128, 4096], F32, kind="ExternalInput")
    trimask = ctx.dram("trimask", [128, 896], F32, kind="ExternalInput")
    WQKV = [ctx.dram(f"wqkvT{l}", [128, 48 * D], F32, kind="ExternalInput")
            for l in range(2)]
    WO = [ctx.dram(f"woT{l}", [128, 32 * D], F32, kind="ExternalInput")
          for l in range(2)]
    w0T = ctx.dram("w0T", [128, 32 * D], F32, kind="ExternalInput")
    bias0 = ctx.dram("bias0", [128, NFC], F32, kind="ExternalInput")
    gwT = ctx.dram("gwT", [D, 8], F32, kind="ExternalInput")
    ics = ctx.dram("ics", [8, D], F32, kind="ExternalInput")
    ocs = ctx.dram("ocs", [8, D], F32, kind="ExternalInput")
    pwT = ctx.dram("pwT", [2048, D], U8, kind="ExternalInput")
    sclT = ctx.dram("sclT", [32, D], F32, kind="ExternalInput")
    sel2 = ctx.dram("sel2", [2, 128], F32, kind="ExternalInput")
    mask8 = ctx.dram("mask8", [D, D], U8, kind="ExternalInput")
    w2aT = ctx.dram("w2aT", [D, D], F32, kind="ExternalInput")
    bias2 = ctx.dram("bias2", [128, NFC], F32, kind="ExternalInput")
    indT = ctx.dram("indT", [128, D], F32, kind="ExternalInput")
    rep4 = ctx.dram("rep4", [4, 128], F32, kind="ExternalInput")
    g4 = ctx.dram("g4", [128, 4], F32, kind="ExternalInput")
    w1qT = ctx.dram("w1qT", [128, 86 * 1024], F32, kind="ExternalInput")
    w3qT = ctx.dram("w3qT", [128, 86 * 1024], F32, kind="ExternalInput")
    w2qT = ctx.dram("w2qT", [2752, D], F32, kind="ExternalInput")
    a1T = ctx.dram("a1T", [D, 2], F32, kind="ExternalInput")
    a3T = ctx.dram("a3T", [D, 2], F32, kind="ExternalInput")
    a2Tp = ctx.dram("a2Tp", [128, 172], F32, kind="ExternalInput")
    b1T = ctx.dram("b1T", [2, 11008], F32, kind="ExternalInput")
    b3T = ctx.dram("b3T", [2, 11008], F32, kind="ExternalInput")
    b2T = ctx.dram("b2T", [2, D], F32, kind="ExternalInput")
    wqkv3 = ctx.dram("wqkv3", [128, 24 * D], F32, kind="ExternalInput")
    bqkv3 = ctx.dram("bqkv3", [128, 24], F32, kind="ExternalInput")
    kern8 = ctx.dram("kern8", [512, D], I8, kind="ExternalInput")
    wo3 = ctx.dram("wo3", [128, 32 * 1024], F32, kind="ExternalInput")
    bo3 = ctx.dram("bo3", [128, NFC], F32, kind="ExternalInput")
    iq_sc = ctx.dram("iq_sc", [128, 48], I32, kind="ExternalInput")
    iq_q = ctx.dram("iq_q", [128, 32], I32, kind="ExternalInput")
    iq_kv = ctx.dram("iq_kv", [128, 16], I32, kind="ExternalInput")
    iq_v = ctx.dram("iq_v", [128, 16], I32, kind="ExternalInput")
    iao_sc = ctx.dram("iao_sc", [128, 32], I32, kind="ExternalInput")
    iao_g = ctx.dram("iao_g", [128, 32], I32, kind="ExternalInput")
    i3_sc = ctx.dram("i3_sc", [128, 16], I32, kind="ExternalInput")
    i3_k = ctx.dram("i3_k", [128, 16], I32, kind="ExternalInput")
    i3_v = ctx.dram("i3_v", [128, 16], I32, kind="ExternalInput")
    xout = ctx.dram("xout", [D, 2 * TOK], F32, kind="ExternalOutput")
    dbgs = None
    if FDBG:
        dbgs = [ctx.dram(f"dbg{i}", [D, 2 * TOK], F32, kind="ExternalOutput")
                for i in range(3)]
        dbgq = ctx.dram("dbgq", [128, 6144], BF16, kind="ExternalOutput")
    sqkv = [ctx.dram(f"sqkv{k}", [2 * 6144, TOK], BF16, addr_space="Shared")
            for k in range(2)]
    sao4 = [ctx.dram(f"sao{k}", [2 * D, 512], BF16, addr_space="Shared")
            for k in range(4)]
    s3kv = [ctx.dram(f"s3kv{k}", [2 * 2048, TOK], BF16, addr_space="Shared")
            for k in range(2)]
    x2d = [ctx.dram(f"x{k}", [D, TOK], F32) for k in range(2)]
    xA2 = [ctx.dram(f"xA{k}", [D, TOK], F32) for k in range(2)]
    w1T = ctx.dram("w1T", [D, D], BF16)
    bmd = ctx.dram("bmd", [128, 128], F32)
    bxd = ctx.dram("bxd", [128, D], BF16)

    S = {k: ctx.sem(k) for k in
         ("wl", "xl", "pe", "peog", "ev", "tr", "ve", "ac", "st", "pe2",
          "tb", "xc", "xac", "ms", "ve2", "ix", "sc2", "gx")}
    acc2 = [ctx.chan("gpsimd", f"acc{i}") for i in range(4)]
    stc = [ctx.chan("sync", "stc0"), ctx.chan("sync", "stc1")]
    scc = [ctx.chan("gpsimd", "scc0"), ctx.chan("gpsimd", "scc1")]
    gcc = [ctx.chan("gpsimd", "gcc0"), ctx.chan("gpsimd", "gcc1")]
    lwch = [ctx.chan("gpsimd", "lw0"), ctx.chan("gpsimd", "lw1")]
    b4ch = ctx.chan("gpsimd", "b4ch")
    bmch = ctx.chan("sync", "bmch")
    mbch = ctx.chan("gpsimd", "mbch")
    wch6 = [ctx.chan("gpsimd", f"wch{i}") for i in range(6)]

    pp = [ctx.psum(f"pp{i}", [128, TOK], F32) for i in range(3)]
    pp3 = ctx.psum("pp3", [128, 512], F32)
    pp4 = ctx.psum("pp4", [128, TOK], BF16)
    pm2a = pp3[0:2, 0:512]
    pm2b = pp[2][0:2, 0:512]

    big = ctx.sbuf("big", [128, NFC * TOK], BF16)
    hf = big[:, 0:22 * TOK]
    arB = ctx.sbuf("arB", [128, 24576], BF16)
    wfb = [ctx.sbuf(f"wfb{i}", [128, 2048], F32) for i in range(2)]
    wbb = [ctx.sbuf(f"wbb{i}", [128, 4096], BF16) for i in range(2)]
    wbufs = (wfb, wbb)
    onesb = ctx.sbuf("onesb", [128, 1], BF16)
    onesf = ctx.sbuf("onesf", [1, 128], F32)
    cv = ctx.sbuf("cv", [128, 8], F32)
    identb = ctx.sbuf("identb", [128, 128], BF16)
    ctb4 = ctx.sbuf("ctb4", [128, 4096], BF16)
    stb4 = ctx.sbuf("stb4", [128, 4096], BF16)
    tri = ctx.sbuf("tri", [128, 896], BF16)
    fstage = [ctx.sbuf(f"fstage{i}", [128, TOK], F32) for i in range(2)]
    stage = [ctx.sbuf(f"stage{i}", [128, TOK], BF16) for i in range(2)]
    idxs = {nm: ctx.sbuf("s_" + nm, [128, w], I32) for nm, w in
            (("iq_sc", 48), ("iq_q", 32), ("iq_kv", 16), ("iq_v", 16),
             ("iao_sc", 32), ("iao_g", 32), ("i3_sc", 16), ("i3_k", 16),
             ("i3_v", 16))}
    aux = dict(onesb=onesb, onesf=onesf, cv=cv, pp=pp, identb=identb,
               pp3=pp3, pp4=pp4, fstage=fstage, acc2=acc2)
    aux["f0_icsb"] = arB[0:8, 0:4096]
    aux["f0_ocsb"] = arB[0:8, 4096:8192]
    aux["f0_gw"] = arB[:, 8192:8448]
    aux["f0_rtT"] = arB[0:8, 8448:9472]
    aux["f1_scl2"] = arB[0:2, 0:4096]
    aux["f1_hi32"] = arB[:, 4096:5120]
    aux["f1_lo32"] = arB[:, 5120:6144]
    aux["f1_bsc"] = arB[:, 6144:7168]
    aux["ln_shared"] = dict(
        sq=[wbb[0][:, 0:TOK], wbb[0][:, TOK:2 * TOK]],
        rows=dict(m=wfb[0][0:1, 0:TOK], msq=wfb[0][0:1, TOK:2 * TOK],
                  var=wfb[1][0:1, 0:TOK], std=wfb[1][0:1, TOK:2 * TOK],
                  rstd=fstage[0][0:1, 0:TOK], bneg=fstage[1][0:1, 0:TOK]))
    rope = dict(rot=ctx.sbuf("rot", [128, TOK], BF16),
                tmpb=ctx.sbuf("tmpb", [128, TOK], BF16))

    ctx.op("gpsimd", lambda e: e.memset(onesb[:, :], 1.0))
    ctx.op("gpsimd", lambda e: e.memset(onesf[:, :], 1.0), S["pe2"], 1)
    ctx.wait("tensor", S["pe2"], 1)
    for dst, srcp in ((cv, cvec), (identb, ident), (ctb4, ctab4), (stb4, stab4),
                      (tri, trimask)):
        ctx.dma("gpsimd", dst[:, :], srcp, S["tb"])
    for nm, ap in (("iq_sc", iq_sc), ("iq_q", iq_q), ("iq_kv", iq_kv),
                   ("iq_v", iq_v), ("iao_sc", iao_sc), ("iao_g", iao_g),
                   ("i3_sc", i3_sc), ("i3_k", i3_k), ("i3_v", i3_v)):
        ctx.dma("gpsimd", idxs[nm][:, :], ap, S["tb"])
    for k in range(2):
        ctx.dma("gpsimd", x2d[k], xT2[:, k * TOK:(k + 1) * TOK], S["xc"])
    tabs = S["tb"].count
    for eng in ("tensor", "vector", "scalar"):
        ctx.wait(eng, S["tb"], tabs)
    ctx.wait("gpsimd", S["xc"], S["xc"].count)

    sems_ln = dict(xl=S["xl"], pe=S["pe"], ve=S["ve"], tr=S["tr"], ac=S["ve"])

    def scatter(ch, dst_base, idx_col, src_ap, cols):
        def f(e, dst=dst_base, idxap=idx_col, src=src_ap, cols=cols):
            e.indirect_dma_start(dst[0:128, cols[0]:cols[1]],
                                 IndirectOffsetOnAxis(idxap, 0),
                                 src, None).then_inc(ch.sem.h, 16)
        if ch.sem.count > 0:
            ctx.wait(ch.queue, ch.sem, ch.sem.count)
        ctx.emit("gpsimd", f)
        ch.sem.count += 16
        return ch.sem.count

    def gather(ch, dst_ap, src_base, idx_col, cols):
        def f(e, dst=dst_ap, idxap=idx_col, src=src_base, cols=cols):
            e.indirect_dma_start(dst, None,
                                 src[0:128, cols[0]:cols[1]],
                                 IndirectOffsetOnAxis(idxap, 0)).then_inc(ch.sem.h, 16)
        if ch.sem.count > 0:
            ctx.wait(ch.queue, ch.sem, ch.sem.count)
        ctx.emit("gpsimd", f)
        ch.sem.count += 16
        return ch.sem.count

    for lay in range(2):
        qkv_sc_done = []
        for ck in range(2):
            ctx.barrier_from("tensor", S["ev"], S["ve"])
            emit_ln(ctx, f"ln{lay}{ck}", x2d[ck], big, aux, sems_ln)
            ctx.wait("gpsimd", S["ve"], S["ve"].count)
            ctx.dma("gpsimd", xA2[ck], x2d[ck], S["xac"])
            xA_copy = S["xac"].count
            ctx.barrier_from("tensor", S["ve"])
            qsc = {}

            def qkv_evac(oc, ps, og_tgt, ck=ck):
                slot = oc % 2
                ctx.wait("scalar", S["peog"], og_tgt)
                if oc >= 2:
                    ctx.wait("scalar", scc[slot].sem, qsc[oc - 2])
                ctx.op("scalar", lambda e, s=slot, p=ps:
                       e.activation(stage[s][:, :], p[:, :], AF.Identity,
                                    bias=0.0, scale=1.0), S["ev"])
                ev_tgt = S["ev"].count
                if oc < 40:
                    ctx.wait("vector", S["ev"], ev_tgt)
                    cta = ctb4[:, ck * TOK:(ck + 1) * TOK]
                    sta = stb4[:, ck * TOK:(ck + 1) * TOK]
                    ctx.op("vector", lambda e, s=slot: e.tensor_copy(
                        rope["rot"][0:64, :], stage[s][64:128, :]))
                    ctx.op("vector", lambda e, s=slot: e.tensor_copy(
                        rope["rot"][64:128, :], stage[s][0:64, :]))
                    ctx.op("vector", lambda e, s=slot, cta=cta: e.tensor_tensor(
                        rope["tmpb"][:, :], stage[s][:, :], cta, ALU.mult))
                    ctx.op("vector", lambda e, sta=sta: e.tensor_tensor(
                        rope["rot"][:, :], rope["rot"][:, :], sta, ALU.mult))
                    ctx.op("vector", lambda e, s=slot: e.tensor_tensor(
                        stage[s][:, :], rope["tmpb"][:, :], rope["rot"][:, :],
                        ALU.add), S["ve"])
                    ctx.wait("gpsimd", S["ve"], S["ve"].count)
                else:
                    ctx.wait("gpsimd", S["ev"], ev_tgt)
                qsc[oc] = scatter(scc[slot], sqkv[ck],
                                  idxs["iq_sc"][:, oc:oc + 1],
                                  stage[slot][:, :], (0, TOK))

            emit_mm(ctx, f"qkv{lay}{ck}", WQKV[lay], NFC, 48,
                    lambda ic: big[:, ic * TOK:(ic + 1) * TOK],
                    qkv_evac, S, pp=pp, goc=3, icb=8, wbufs=wbufs, tiled=True,
                    wchans=wch6)
            qkv_sc_done.append((scc[0].sem, scc[0].sem.count))
            qkv_sc_done.append((scc[1].sem, scc[1].sem.count))

            aux["tag"] = f"_{lay}{ck}"
            if lay == 0:
                emit_ffn0(ctx, big, xA2[ck], aux, S,
                          dict(w0T=w0T, bias0=bias0, gwT=gwT, ics=ics, ocs=ocs),
                          xA_copy, loaded=(ck == 1), wbufs=wbufs, tiled=True,
                          wchans=wch6)
            else:
                emit_ffn1(ctx, big, xA2[ck], aux, S,
                          dict(pwT=pwT, sclT=sclT, w1T=w1T, sel2=sel2),
                          xA_copy, loaded=(ck == 1), wbufs=wbufs, wchans=wch6)
        r = pcm.publish(qkv_sc_done)
        pcm.poll(r)
        pcm.barrier_engines()

        emit_flash_fused(ctx, S, aux, idxs, sqkv, sao4, tri, cv, gcc, scc,
                         rope, arB, stage, pp, pp3, pp4)
        r2 = pcm.publish([(scc[0].sem, scc[0].sem.count),
                          (scc[1].sem, scc[1].sem.count)])
        pcm.poll(r2)
        pcm.barrier_engines()

        for ck in range(2):
            g_done = []
            for icc in range(NFC):
                for j in range(2):
                    g_done.append(gather(
                        gcc[(icc * 2 + j) % 2],
                        big[:, icc * TOK + j * 512:icc * TOK + (j + 1) * 512],
                        sao4[2 * ck + j], idxs["iao_g"][:, icc:icc + 1],
                        (0, 512)))
            ctx.wait("tensor", gcc[0].sem, gcc[0].sem.count)
            ctx.wait("tensor", gcc[1].sem, gcc[1].sem.count)
            wo_acc = {}
            ctx.barrier_from("vector", *[c.sem for c in acc2])
            fst4 = [fstage[0][:, :], fstage[1][:, :],
                    wfb[0][:, 0:TOK], wfb[1][:, 0:TOK]]

            def wo_evac(oc, ps, og_tgt, ck=ck):
                slot = oc % 4
                ctx.wait("vector", S["peog"], og_tgt)
                if oc >= 4:
                    ctx.wait("vector", acc2[slot].sem, wo_acc[oc - 4])
                ctx.op("vector", lambda e, s=slot, p=ps:
                       e.tensor_copy(fst4[s], p[:, :]), S["ev"])
                ctx.wait("gpsimd", S["ev"], S["ev"].count)
                acc2[slot].go(xA2[ck][oc * 128:(oc + 1) * 128, :],
                              fst4[slot], accum_op=ALU.add)
                wo_acc[oc] = acc2[slot].sem.count

            emit_mm(ctx, f"wo{lay}{ck}", WO[lay], NFC, NFC,
                    lambda ic: big[:, ic * TOK:(ic + 1) * TOK], wo_evac,
                    S, pp=pp, goc=3, icb=8, wbufs=wbufs, tiled=True,
                    wchans=wch6)
            for _c in acc2:
                ctx.wait("gpsimd", _c.sem, _c.sem.count)
            ctx.dma("gpsimd", x2d[ck], xA2[ck], S["xc"])
        ctx.wait("gpsimd", S["xc"], S["xc"].count)
        if FDBG:
            for ck in range(2):
                ctx.dma("gpsimd", dbgs[lay][:, ck * TOK:(ck + 1) * TOK],
                        x2d[ck], S["st"])
            ctx.wait("gpsimd", S["st"], S["st"].count)

    emit_l2l3_fused(ctx, locals())
    for ch in acc2 + stc + scc + gcc:
        ctx.wait("gpsimd", ch.sem, ch.sem.count)
    ctx.barrier_from("gpsimd", S["xac"], S["st"])
    return ctx


def emit_flash_fused(ctx, S, aux, idxs, sqkv, sao4, tri, cv, gcc, scc,
                     rope, arB, stage, pp, pp3, pp4):
    K_s = arB[:, 0:4096]
    VT_s = arB[:, 4096:8192]
    Qh = [arB[:, 8192:12288], arB[:, 12288:16384]]
    ebuf = [arB[:, 16384:17408], arB[:, 17408:18432],
            stage[0][:, 0:TOK], stage[1][:, 0:TOK]]
    esum = arB[:, 18432:18944]
    aost = [arB[:, 18944:19456], arB[:, 19456:19968]]
    vtmp = arB[:, 19968:24064]
    bc_sb = aux["fstage"][0][:, 0:512]
    rd_row = aux["fstage"][1][0:1, 0:512]
    onesb, onesf, identb = aux["onesb"], aux["onesf"], aux["identb"]

    def gather(ch, dst_ap, src, idx_col, ncol):
        def f(e, dst=dst_ap, src=src, idxap=idx_col, ncol=ncol):
            e.indirect_dma_start(dst, None, src[0:128, 0:ncol],
                                 IndirectOffsetOnAxis(idxap, 0)).then_inc(ch.sem.h, 16)
        if ch.sem.count > 0:
            ctx.wait(ch.queue, ch.sem, ch.sem.count)
        ctx.emit("gpsimd", f)
        ch.sem.count += 16
        return ch.sem.count

    qload = {}
    for kvh in range(4):
        ctx.barrier_from("gpsimd", S["pe"])
        for w in range(2):
            for ckk in range(2):
                c0 = (w * 2 + ckk) * TOK
                gather(gcc[0], K_s[:, c0:c0 + TOK], sqkv[ckk],
                       idxs["iq_kv"][:, kvh * 2 + w:kvh * 2 + w + 1], TOK)
                gather(gcc[1], vtmp[:, c0:c0 + TOK], sqkv[ckk],
                       idxs["iq_v"][:, kvh * 2 + w:kvh * 2 + w + 1], TOK)
        ctx.wait("tensor", gcc[0].sem, gcc[0].sem.count)
        ctx.wait("tensor", gcc[1].sem, gcc[1].sem.count)
        for tb in range(4):
            if tb > 0:
                ctx.wait("tensor", S["ve"], S["ve"].count)
            for t in range(8):
                ctx.op("tensor", lambda e, tb=tb, t=t: e.transpose(
                    pp4[:, t * 128:(t + 1) * 128],
                    vtmp[:, (tb * 8 + t) * 128:(tb * 8 + t + 1) * 128],
                    identb[:, :]),
                    S["peog"] if t == 7 else None)
            ctx.wait("vector", S["peog"], S["peog"].count)
            ctx.op("vector", lambda e, tb=tb: e.tensor_copy(
                VT_s[:, tb * 1024:(tb + 1) * 1024], pp4[:, 0:1024]), S["ve"])
        ctx.wait("tensor", S["ve"], S["ve"].count)
        for qh in range(4):
            hl = kvh * 4 + qh
            qslot = hl % 2
            if hl >= 2:
                rs, rv = qload[hl - 2]
                ctx.wait("gpsimd", rs, rv)
            for w in range(2):
                for ckk in range(2):
                    c0 = (w * 2 + ckk) * TOK
                    gather(gcc[qslot], Qh[qslot][:, c0:c0 + TOK], sqkv[ckk],
                           idxs["iq_q"][:, hl * 2 + w:hl * 2 + w + 1], TOK)
            ctx.wait("tensor", gcc[qslot].sem, gcc[qslot].sem.count)
            for s in range(8):
                qtile = Qh[qslot][:, s * 512:(s + 1) * 512]
                nblk = 4 * (s + 1)
                groups = [list(range(g, min(g + 2, nblk))) for g in range(0, nblk, 2)]
                grp_tr = {}
                grp_ve = {}

                def emit_av(pgi, pgrp, nblk=nblk):
                    ctx.wait("tensor", S["ve"], grp_ve[pgi])
                    peb = ebuf[pgi % 4]
                    for i, j in enumerate(pgrp):
                        ctx.op("tensor", lambda e, i=i, peb=peb, kc=j * 128,
                               fs=(j == 0), l=(j == nblk - 1):
                               e.matmul(pp[2][:, 0:512], VT_s[:, kc:kc + 128],
                                        peb[:, i * 512:(i + 1) * 512],
                                        start=fs, stop=l))

                pend_av = []
                for gi, grp in enumerate(groups):
                    scp = pp[gi % 2]
                    if gi >= 2:
                        ctx.wait("tensor", S["tr"], grp_tr[gi - 2])
                    for i, j in enumerate(grp):
                        ctx.op("tensor", lambda e, scp=scp, i=i, kc=j * 128, qtile=qtile:
                               e.matmul(scp[:, i * 512:(i + 1) * 512],
                                        K_s[:, kc:kc + 128], qtile,
                                        start=True, stop=True),
                               S["pe"] if i == len(grp) - 1 else None)
                    pe_sc = S["pe"].count
                    if len(pend_av) >= 2:
                        emit_av(*pend_av.pop(0))
                    ctx.wait("scalar", S["pe"], pe_sc)
                    if gi >= 4:
                        ctx.wait("scalar", S["ve"], grp_ve[gi - 4])
                    eb = ebuf[gi % 4]
                    ng = len(grp)
                    ctx.op("scalar", lambda e, scp=scp, eb=eb, ng=ng:
                           e.activation(eb[:, 0:ng * 512], scp[:, 0:ng * 512],
                                        AF.Exp, bias=cv[:, CV_SHIFT01:CV_SHIFT01 + 1],
                                        scale=0.08838834764831845), S["tr"])
                    grp_tr[gi] = S["tr"].count
                    ctx.wait("vector", S["tr"], S["tr"].count)
                    for i, j in enumerate(grp):
                        esl = eb[:, i * 512:(i + 1) * 512]
                        jp = j - (nblk - 4)
                        if jp >= 0:
                            ctx.op("vector", lambda e, esl=esl, jp=jp:
                                   e.tensor_tensor(esl, esl,
                                                   tri[:, (3 - jp) * 128:(3 - jp) * 128 + 512],
                                                   ALU.mult))
                        if j == 0:
                            ctx.op("vector", lambda e, esl=esl:
                                   e.tensor_copy(esum[:, :], esl))
                        else:
                            ctx.op("vector", lambda e, esl=esl:
                                   e.tensor_tensor(esum[:, :], esum[:, :], esl,
                                                   ALU.add))
                    ctx.op("vector", lambda e: e.engine_nop(), S["ve"])
                    grp_ve[gi] = S["ve"].count
                    pend_av.append((gi, grp))
                for pa in pend_av:
                    emit_av(*pa)
                ctx.op("tensor", lambda e: e.matmul(pp3[0:1, 0:512], onesb[:, :],
                                                    esum[:, :], start=True,
                                                    stop=True), S["pe"])
                ctx.wait("vector", S["pe"], S["pe"].count)
                ctx.op("vector", lambda e: e.reciprocal(rd_row, pp3[0:1, 0:512]),
                       S["ve"])
                ctx.wait("tensor", S["ve"], S["ve"].count)
                ctx.op("tensor", lambda e: e.matmul(pp3[:, 0:512], onesf[:, :],
                                                    rd_row, start=True, stop=True),
                       S["pe"])
                ctx.wait("vector", S["pe"], S["pe"].count)
                aslot = s % 2
                if scc[aslot].sem.count > 0:
                    ctx.wait("vector", scc[aslot].sem, scc[aslot].sem.count)
                ctx.op("vector", lambda e: e.tensor_copy(bc_sb, pp3[:, 0:512]))
                ctx.op("vector", lambda e, aslot=aslot:
                       e.tensor_tensor(aost[aslot], pp[2][:, 0:512], bc_sb,
                                       ALU.mult), S["ve"])
                ev_t = S["ve"].count
                ctx.wait("tensor", S["ve"], ev_t)
                ctx.wait("gpsimd", S["ve"], ev_t)
                r = s // 4
                qb = s % 4
                def fsc(e, aslot=aslot, hl=hl, r=r, qb=qb):
                    e.indirect_dma_start(
                        sao4[qb][0:128, 0:512],
                        IndirectOffsetOnAxis(idxs["iao_sc"][:, hl * 2 + r:hl * 2 + r + 1], 0),
                        aost[aslot], None).then_inc(scc[aslot].sem.h, 16)
                if scc[aslot].sem.count > 0:
                    ctx.wait("gpsimd", scc[aslot].sem, scc[aslot].sem.count)
                ctx.emit("gpsimd", fsc)
                scc[aslot].sem.count += 16
            qload[hl] = (S["pe"], S["pe"].count)


def emit_l2l3_fused(ctx, L):
    S = L["S"]
    acc2, stc, lwch, b4ch = L["acc2"], L["stc"], L["lwch"], L["b4ch"]
    bmch, mbch = L["bmch"], L["mbch"]
    gcc, scc = L["gcc"], L["scc"]
    pcm, idxs, s3kv = L["pcm"], L["idxs"], L["s3kv"]
    pp, pp3, pp4 = L["pp"], L["pp3"], L["pp4"]
    pm2a, pm2b = L["pm2a"], L["pm2b"]
    big, hf, arB = L["big"], L["hf"], L["arB"]
    fstage, stage = L["fstage"], L["stage"]
    aux, cv = L["aux"], L["cv"]
    identb, onesb, onesf = L["identb"], L["onesb"], L["onesf"]
    ctb4, stb4 = L["ctb4"], L["stb4"]
    x2d, xA2, xout = L["x2d"], L["xA2"], L["xout"]
    mask8, w2aT, bias2 = L["mask8"], L["w2aT"], L["bias2"]
    indT, rep4, g4 = L["indT"], L["rep4"], L["g4"]
    w1qT, w3qT, w2qT = L["w1qT"], L["w3qT"], L["w2qT"]
    a1T, a3T, a2Tp = L["a1T"], L["a3T"], L["a2Tp"]
    b1T, b3T, b2T = L["b1T"], L["b3T"], L["b2T"]
    wqkv3, bqkv3, kern8 = L["wqkv3"], L["bqkv3"], L["kern8"]
    wo3, bo3 = L["wo3"], L["bo3"]
    bmd, bxd = L["bmd"], L["bxd"]
    wbufs = L["wbufs"]
    wch6 = L["wch6"]
    fst4 = [L["fstage"][0][:, :], L["fstage"][1][:, :],
            wbufs[0][0][:, 0:TOK], wbufs[0][1][:, 0:TOK]]

    q3loc = ctx.dram("q3loc", [1024, 2 * TOK], BF16)
    k3loc = ctx.dram("k3loc", [1024, 4096], BF16)
    v3tloc = ctx.dram("v3tloc", [4096, 1024], BF16)

    mb = arB[:, 0:4096]
    indsb = arB[:, 4096:8192]
    xf4 = arB[:, 8192:16384]
    xf8 = [arB[:, 16384:20480], arB[:, 8192:12288]]
    b4sb = arB[0:4, 20480:24576]
    gsb, usb, hsb = ctb4[:, 0:1024], ctb4[:, 1024:2048], ctb4[:, 2048:3072]
    w1b = [stb4[:, 0:1024], stb4[:, 1024:2048]]
    w3b = [stb4[:, 2048:3072], stb4[:, 3072:4096]]
    kmb = [stb4[:, 0:1024], stb4[:, 1024:2048]]
    b2sb = stb4[0:2, 0:4096]
    hstmp = mb[:, 0:TOK]
    cpl = ctx.sbuf("cpl", [128, 128], F32)
    rtmp = ctx.sbuf("rtmp", [4, 128], F32)
    bmf = ctx.sbuf("bmf", [128, 128], F32)
    binb = ctx.sbuf("binb", [128, 128], BF16)
    rep4b = ctx.sbuf("rep4b", [4, 128], BF16)
    g4b = ctx.sbuf("g4b", [128, 4], BF16)
    wmb = [ctx.sbuf(f"wmb{i}", [128, 384], BF16) for i in range(2)]
    biasb = ctx.sbuf("c2_bias", [128, NFC], F32)
    a1sb = ctx.sbuf("a1sb", [128, 64], BF16)
    a3sb = ctx.sbuf("a3sb", [128, 64], BF16)
    a2sb = ctx.sbuf("a2sb", [128, 172], BF16)
    b1sb = ctx.sbuf("b1sb", [2, 256], BF16)
    b3sb = ctx.sbuf("b3sb", [2, 256], BF16)
    mid1b = arB[0:2, 1024:2048]
    mid3b = arB[0:2, 2048:3072]
    mid2b = arB[0:2, 3072:4096]
    bq3 = ctx.sbuf("bq3", [128, 24], F32)
    bo3sb = ctx.sbuf("bo3sb", [128, NFC], F32)

    ctx.barrier_from("gpsimd", S["pe"], S["ve"], S["peog"])
    ctx.op("gpsimd", lambda e: e.memset(b4sb[:, :], 0.0))
    for dst, srcp in ((indsb, indT), (rep4b, rep4), (g4b, g4), (a2sb, a2Tp),
                      (biasb, bias2), (bq3, bqkv3), (bo3sb, bo3)):
        ctx.dma("gpsimd", dst[:, :], srcp, S["tb"])
    ctx.dma("gpsimd", a1sb[:, :].rearrange("p (c r) -> p c r", r=2),
            a1T.rearrange("(c p) r -> p c r", p=128), S["tb"])
    ctx.dma("gpsimd", a3sb[:, :].rearrange("p (c r) -> p c r", r=2),
            a3T.rearrange("(c p) r -> p c r", p=128), S["tb"])
    tabs2 = S["tb"].count
    for eng in ("tensor", "vector", "scalar"):
        ctx.wait(eng, S["tb"], tabs2)

    ctx.barrier_from("tensor", S["ve"])
    ctx.barrier_from("scalar", S["ve"])
    mb_dve = 0
    for rc in range(NFC):
        if rc > 0:
            ctx.wait("gpsimd", S["ve2"], mb_dve)
        mbch.go(mb[:, :], mask8[rc * 128:(rc + 1) * 128, :])
        ctx.wait("vector", mbch.sem, mbch.sem.count)
        ctx.op("vector", lambda e: e.tensor_reduce(
            cpl[:, :].rearrange("p (c o) -> p c o", o=1),
            mb[:, :].rearrange("p (c w) -> p c w", w=32),
            axis=AX.X, op=ALU.add), S["ve2"])
        mb_dve = S["ve2"].count
        ctx.wait("tensor", S["ve2"], mb_dve)
        ctx.op("vector", lambda e: e.tensor_copy(mb[:, 0:128], cpl[:, :]), S["ve2"])
        ctx.wait("tensor", S["ve2"], S["ve2"].count)
        ctx.op("tensor", lambda e: e.matmul(pp3[0:4, 0:512], g4b[:, :],
                                            mb[:, 0:512], start=True, stop=True),
               S["peog"])
        ctx.wait("vector", S["peog"], S["peog"].count)
        if rc > 0:
            ctx.wait("vector", bmch.sem, bmch.sem.count)
        ctx.op("vector", lambda e: e.tensor_copy(rtmp[:, :], pp3[0:4, 0:128]),
               S["ve2"])
        ctx.wait("sync", S["ve2"], S["ve2"].count)
        bmch.go(bmd[4 * rc:4 * rc + 4, :], rtmp[:, :])
    ctx.wait("gpsimd", bmch.sem, bmch.sem.count)
    ctx.dma("gpsimd", bmf[:, :], bmd, S["ms"])
    ctx.wait("scalar", S["ms"], S["ms"].count)
    ctx.op("scalar", lambda e: e.activation(binb[:, :], bmf[:, :], AF.Sign), S["tr"])
    ctx.wait("tensor", S["tr"], S["tr"].count)
    for hblk in range(8):
        ctx.op("tensor", lambda e, hblk=hblk: e.matmul(
            pp3[:, 0:512], binb[:, :], indsb[:, hblk * 512:(hblk + 1) * 512],
            start=True, stop=True), S["peog"])
        ctx.wait("scalar", S["peog"], S["peog"].count)
        ctx.op("scalar", lambda e, hblk=hblk: e.activation(
            stage[hblk % 2][:, 0:512], pp3[:, 0:512], AF.Identity,
            bias=0.0, scale=1.0), S["tr"])
        ctx.wait("sync", S["tr"], S["tr"].count)
        stc[hblk % 2].go(bxd[:, hblk * 512:(hblk + 1) * 512],
                         stage[hblk % 2][:, 0:512])
        ctx.wait("tensor", stc[hblk % 2].sem, stc[hblk % 2].sem.count)
    for chn in stc:
        ctx.wait("gpsimd", chn.sem, chn.sem.count)

    sems_ln = dict(xl=S["xl"], pe=S["pe"], ve=S["ve"], tr=S["tr"], ac=S["ve"])
    wpi = [0]

    def wgo6(dst_ap, src_ap):
        ch = wch6[wpi[0] % len(wch6)]
        wpi[0] += 1
        ch.go(dst_ap, src_ap)
        return (ch.sem, ch.sem.count)

    for ck in range(2):
        ctx.barrier_from("tensor", S["ev"], S["ve"])
        emit_ln(ctx, f"l2ln{ck}", x2d[ck], big, aux, sems_ln)
        ctx.wait("gpsimd", S["ve"], S["ve"].count)

        ogs = [list(range(g, min(g + 3, NFC))) for g in range(0, NFC, 3)]
        ctx.barrier_from("vector", *[c.sem for c in acc2])
        at_acc = {}
        lrel = {}
        ev_prev = []
        mexp_t = {}
        wl_t = {}
        b4_t = {}
        NLM = len(ogs) * NFC
        b4s = [arB[0:4, 20480:20992], arB[0:4, 20992:21504]]

        def issue_at(l):
            gi_l, ic_l = divmod(l, NFC)
            og_l = ogs[gi_l]
            nc_l = len(og_l) * 128
            c0_l = og_l[0] * 128
            sl = l % 2
            if l >= 2:
                rs, rv = lrel[l - 2]
                ctx.wait("gpsimd", rs, rv)
            wl_t[l] = wgo6(stage[sl][:, 0:nc_l],
                           w2aT[ic_l * 128:(ic_l + 1) * 128, c0_l:c0_l + nc_l])
            if l >= 2:
                ctx.wait("gpsimd", S["peog"], mexp_t[l - 2])
            b4ch.go(b4s[sl][:, 0:nc_l],
                    bxd[4 * ic_l:4 * ic_l + 4, c0_l:c0_l + nc_l])
            b4_t[l] = b4ch.sem.count

        mps = [pp3, pp3]

        def emit_mexp(l):
            sl = l % 2
            ctx.wait("tensor", b4ch.sem, b4_t[l])
            ctx.op("tensor", lambda e, sl=sl: e.matmul(
                mps[sl][:, 0:512], rep4b[:, :], b4s[sl][:, 0:512], start=True,
                stop=True), S["peog"])
            mexp_t[l] = S["peog"].count

        for gi, og in enumerate(ogs):
            ncols = len(og) * 128
            for ic in range(NFC):
                lidx = gi * NFC + ic
                slot = lidx % 2
                if lidx == 0:
                    issue_at(0)
                    issue_at(1)
                    emit_mexp(0)
                ctx.wait("vector", S["peog"], mexp_t[lidx])
                ctx.wait("vector", wl_t[lidx][0], wl_t[lidx][1])
                ctx.op("vector", lambda e, s=slot, n=ncols: e.tensor_tensor(
                    wmb[s][:, 0:n], stage[s][:, 0:n], mps[s][:, 0:n], ALU.mult),
                    S["ve"])
                ctx.wait("tensor", S["ve"], S["ve"].count)
                if gi > 0 and ic == 0:
                    ctx.wait("tensor", S["ev"], ev_prev[gi - 1])
                for oi, oc in enumerate(og):
                    for h2 in range(2):
                        t0, t1 = h2 * 512, (h2 + 1) * 512
                        last = (ic == NFC - 1)
                        sem1 = None
                        if oi == len(og) - 1 and h2 == 1:
                            sem1 = S["peog"] if last else S["pe"]
                        ctx.op("tensor", lambda e, oi=oi, s=slot, o0=oi * 128,
                               t0=t0, t1=t1, fs=(ic == 0), l=last, ic=ic:
                               e.matmul(pp[oi][:, t0:t1], wmb[s][:, o0:o0 + 128],
                                        big[:, ic * TOK + t0:ic * TOK + t1],
                                        start=fs, stop=l), sem1)
                lrel[lidx] = (S["peog"] if ic == NFC - 1 else S["pe"],
                              (S["peog"] if ic == NFC - 1 else S["pe"]).count)
                if lidx + 2 < NLM:
                    issue_at(lidx + 2)
                if lidx + 1 < NLM:
                    emit_mexp(lidx + 1)
            og_tgt = S["peog"].count
            for oi, oc in enumerate(og):
                slot2 = oc % 4
                ctx.wait("vector", S["peog"], og_tgt)
                if oc >= 4:
                    ctx.wait("vector", acc2[slot2].sem, at_acc[oc - 4])
                ctx.op("vector", lambda e, s=slot2, oi=oi:
                       e.tensor_copy(fst4[s], pp[oi][:, :]))
                ctx.op("vector", lambda e, s=slot2, oc=oc:
                       e.tensor_scalar(fst4[s], fst4[s],
                                       biasb[:, oc:oc + 1], None, ALU.add), S["ev"])
                ctx.wait("gpsimd", S["ev"], S["ev"].count)
                acc2[slot2].go(xA2[ck][oc * 128:(oc + 1) * 128, :],
                               fst4[slot2], accum_op=ALU.add)
                at_acc[oc] = acc2[slot2].sem.count
            ev_prev.append(S["ev"].count)

        for f in range(8):
            d = xf4[:, f * TOK:(f + 1) * TOK]
            ctx.op("vector", lambda e, d=d, f=f: e.tensor_tensor(
                d, big[:, f * TOK:(f + 1) * TOK],
                big[:, (f + 8) * TOK:(f + 9) * TOK], ALU.add))
            ctx.op("vector", lambda e, d=d, f=f: e.tensor_tensor(
                d, d, big[:, (f + 16) * TOK:(f + 17) * TOK], ALU.add))
            ctx.op("vector", lambda e, d=d, f=f: e.tensor_tensor(
                d, d, big[:, (f + 24) * TOK:(f + 25) * TOK], ALU.add),
                S["ve"] if f == 7 else None)

        ctx.barrier_from("tensor", S["ev"], S["ve"])
        for mb_, aT, ppx in ((mid1b, a1sb, pp[0]), (mid3b, a3sb, pp[1])):
            for ic in range(NFC):
                for h2 in range(2):
                    t0, t1 = h2 * 512, (h2 + 1) * 512
                    ctx.op("tensor", lambda e, aT=aT, ic=ic, t0=t0, t1=t1, ppx=ppx:
                           e.matmul(ppx[0:2, t0:t1], aT[:, 2 * ic:2 * ic + 2],
                                    big[:, ic * TOK + t0:ic * TOK + t1],
                                    start=(ic == 0), stop=(ic == NFC - 1)),
                           S["peog"] if (ic == NFC - 1 and h2 == 1) else None)
            ctx.wait("vector", S["peog"], S["peog"].count)
            ctx.op("vector", lambda e, mb_=mb_, ppx=ppx: e.tensor_scalar(
                mb_[:, :], ppx[0:2, :], 0.5, None, ALU.mult), S["ve"])
        ctx.wait("tensor", S["ve"], S["ve"].count)

        NOC = 86
        hw_written = set()
        yrel = {}
        ylora = {}
        yw_t = {}
        yb_t = {}
        sil_t = None

        def issue_y(l):
            sl = l % 2
            if l >= 2:
                rs, rv = yrel[l - 2]
                ctx.wait("gpsimd", rs, rv)
            yw_t[l] = [wgo6(w1b[sl][:, :], w1qT[:, l * 1024:(l + 1) * 1024]),
                       wgo6(w3b[sl][:, :], w3qT[:, l * 1024:(l + 1) * 1024])]
            if l >= 2:
                ctx.wait("gpsimd", S["peog"], ylora[l - 2])
            b4ch.go(b1sb[:, sl * 128:(sl + 1) * 128],
                    b1T[:, l * 128:(l + 1) * 128])
            b4ch.go(b3sb[:, sl * 128:(sl + 1) * 128],
                    b3T[:, l * 128:(l + 1) * 128])
            yb_t[l] = b4ch.sem.count

        for oc in range(NOC):
            slot = oc % 2
            if oc == 0:
                issue_y(0)
            if oc + 1 < NOC:
                issue_y(oc + 1)
            for ws, wv in yw_t[oc]:
                ctx.wait("tensor", ws, wv)
            ctx.wait("tensor", b4ch.sem, yb_t[oc])
            if oc >= 1:
                ctx.wait("tensor", S["tr"], sil_t[0])
                ctx.wait("tensor", S["ve"], sil_t[1])
            for wbuf, ppx, midb, bsb in ((w1b[slot], pp[0], mid1b, b1sb),
                                         (w3b[slot], pp[1], mid3b, b3sb)):
                for ic in range(8):
                    for h2 in range(2):
                        t0, t1 = h2 * 512, (h2 + 1) * 512
                        ctx.op("tensor", lambda e, w=wbuf, ic=ic, t0=t0, t1=t1,
                               ppx=ppx:
                               e.matmul(ppx[:, t0:t1],
                                        w[:, ic * 128:(ic + 1) * 128],
                                        xf4[:, ic * TOK + t0:ic * TOK + t1],
                                        start=(ic == 0), stop=False),
                               S["pe"] if (ic == 7 and h2 == 1) else None)
                for h2 in range(2):
                    t0, t1 = h2 * 512, (h2 + 1) * 512
                    ctx.op("tensor", lambda e, b=bsb, m=midb, t0=t0, t1=t1,
                           ppx=ppx, s=slot:
                           e.matmul(ppx[:, t0:t1], b[:, s * 128:(s + 1) * 128],
                                    m[:, t0:t1], start=False, stop=True),
                           S["peog"] if h2 == 1 else None)
            yrel[oc] = (S["pe"], S["pe"].count)
            ylora[oc] = S["peog"].count
            y_done = S["peog"].count
            ctx.wait("scalar", S["peog"], y_done)
            ctx.wait("vector", S["peog"], y_done)
            ctx.op("scalar", lambda e: e.activation(gsb[:, :], pp[0][:, :],
                                                    AF.Silu), S["tr"])
            sil_a = S["tr"].count
            ctx.op("vector", lambda e: e.tensor_copy(usb[:, :], pp[1][:, :]),
                   S["ve"])
            sil_b = S["ve"].count
            sil_t = (sil_a, sil_b)
            ctx.wait("vector", S["tr"], sil_a)
            ctx.op("vector", lambda e: e.tensor_tensor(hsb[:, :], gsb[:, :],
                                                       usb[:, :], ALU.mult),
                   S["ve"])
            ctx.wait("tensor", S["ve"], S["ve"].count)
            for h2 in range(2):
                t0, t1 = h2 * 512, (h2 + 1) * 512
                pm = pm2a if h2 == 0 else pm2b
                ctx.op("tensor", lambda e, pm=pm, t0=t0, t1=t1, oc=oc:
                       e.matmul(pm[:, :], a2sb[:, 2 * oc:2 * oc + 2],
                                hsb[:, t0:t1],
                                start=(oc == 0), stop=(oc == NOC - 1),
                                skip_group_check=True),
                       S["pe2"] if (oc == NOC - 1 and h2 == 1) else None)
            t0g = (128 * oc) % 2752
            pieces = []
            src0 = 0
            rem = 128
            cur = t0g
            while rem > 0:
                tile = cur // 128
                off = cur % 128
                ln = min(rem, 128 - off, 2752 - cur)
                pieces.append((tile, off, src0, ln))
                src0 += ln
                cur = (cur + ln) % 2752
                rem -= ln
            for tile, off, s0, ln in pieces:
                dstap = hf[off:off + ln, tile * TOK:(tile + 1) * TOK]
                srcap = hsb[s0:s0 + ln, :]
                covered = any(k[0] == tile and k[1] <= off and
                              k[1] + k[2] >= off + ln for k in hw_written)
                if covered:
                    if off != s0:
                        ctx.op("vector", lambda e, s=srcap, off=off, ln=ln:
                               e.tensor_copy(hstmp[off:off + ln, :], s))
                        srcap = hstmp[off:off + ln, :]
                    ctx.op("vector", lambda e, d=dstap, s=srcap:
                           e.tensor_tensor(d, d, s, ALU.add), S["ve"])
                else:
                    ctx.op("vector", lambda e, d=dstap, s=srcap:
                           e.tensor_copy(d, s), S["ve"])
                    hw_written.add((tile, off, ln))
        ctx.wait("vector", S["pe2"], S["pe2"].count)
        ctx.op("vector", lambda e: e.tensor_scalar(mid2b[:, 0:512], pm2a[:, :],
                                                   0.5, None, ALU.mult))
        ctx.op("vector", lambda e: e.tensor_scalar(mid2b[:, 512:1024],
                                                   pm2b[:, :], 0.5, None,
                                                   ALU.mult), S["ve"])
        ctx.wait("tensor", S["ve"], S["ve"].count)

        ctx.barrier_from("gpsimd", S["pe"], S["peog"])
        ctx.dma("gpsimd", b2sb[:, :], b2T, S["ix"])
        ctx.wait("tensor", S["ix"], S["ix"].count)
        ctx.barrier_from("vector", *[c.sem for c in acc2])
        dn_acc = {}
        lrel = {}
        dn_t = {}
        ev_prev2 = []
        ogs = [list(range(g, min(g + 3, NFC), 1)) for g in range(0, NFC, 3)]
        NDN = len(ogs) * 22

        def issue_dn(l):
            gi_l, ic_l = divmod(l, 22)
            og_l = ogs[gi_l]
            nc_l = len(og_l) * 128
            c0_l = og_l[0] * 128
            r_l = 128 if ic_l < 21 else 64
            sl = l % 2
            if l >= 2:
                rs, rv = lrel[l - 2]
                ctx.wait("gpsimd", rs, rv)
            dn_t[l] = wgo6(stage[sl][0:r_l, 0:nc_l],
                           w2qT[ic_l * 128:ic_l * 128 + r_l, c0_l:c0_l + nc_l])

        for gi, og in enumerate(ogs):
            ncols = len(og) * 128
            col0 = og[0] * 128
            for ic in range(22):
                rows = 128 if ic < 21 else 64
                lidx = gi * 22 + ic
                slot = lidx % 2
                if lidx == 0:
                    issue_dn(0)
                if lidx + 1 < NDN:
                    issue_dn(lidx + 1)
                ctx.wait("tensor", dn_t[lidx][0], dn_t[lidx][1])
                if gi > 0 and ic == 0:
                    ctx.wait("tensor", S["ev"], ev_prev2[gi - 1])
                for oi, oc in enumerate(og):
                    for h2 in range(2):
                        t0, t1 = h2 * 512, (h2 + 1) * 512
                        sem1 = None
                        if oi == len(og) - 1 and h2 == 1:
                            sem1 = S["pe"]
                        ctx.op("tensor", lambda e, s=slot, oi=oi, o0=oi * 128,
                               r=rows, ic=ic, t0=t0, t1=t1:
                               e.matmul(pp[oi][:, t0:t1],
                                        stage[s][0:r, o0:o0 + 128],
                                        hf[0:r, ic * TOK + t0:ic * TOK + t1],
                                        start=(ic == 0), stop=False), sem1)
                lrel[lidx] = (S["pe"], S["pe"].count)
            for oi, oc in enumerate(og):
                for h2 in range(2):
                    t0, t1 = h2 * 512, (h2 + 1) * 512
                    ctx.op("tensor", lambda e, oi=oi, oc=oc, t0=t0, t1=t1:
                           e.matmul(pp[oi][:, t0:t1],
                                    b2sb[:, oc * 128:oc * 128 + 128],
                                    mid2b[:, t0:t1], start=False, stop=True),
                           S["peog"] if (oi == len(og) - 1 and h2 == 1) else None)
            og_tgt = S["peog"].count
            for oi, oc in enumerate(og):
                slot2 = oc % 4
                ctx.wait("vector", S["peog"], og_tgt)
                if oc >= 4:
                    ctx.wait("vector", acc2[slot2].sem, dn_acc.get(oc - 4, 0))
                ctx.op("vector", lambda e, s=slot2, oi=oi:
                       e.tensor_copy(fst4[s], pp[oi][:, :]), S["ev"])
                ctx.wait("gpsimd", S["ev"], S["ev"].count)
                acc2[slot2].go(xA2[ck][oc * 128:(oc + 1) * 128, :],
                               fst4[slot2], accum_op=ALU.add)
                dn_acc[oc] = acc2[slot2].sem.count
            ev_prev2.append(S["ev"].count)
        for _c in acc2:
            ctx.wait("gpsimd", _c.sem, _c.sem.count)

    q3kv_done = []
    for ck in range(2):
        ctx.dma("gpsimd", x2d[ck], xA2[ck], S["xac"])
        x3copy = S["xac"].count
        ctx.wait("gpsimd", S["xac"], x3copy)
        if FDBG:
            ctx.dma("gpsimd", L["dbgs"][2][:, ck * TOK:(ck + 1) * TOK],
                    x2d[ck], S["st"])
            ctx.wait("gpsimd", S["st"], S["st"].count)
        ctx.barrier_from("tensor", S["ev"], S["ve"])
        emit_ln(ctx, f"l3ln{ck}", x2d[ck], big, aux, sems_ln)
        ctx.wait("gpsimd", S["ve"], S["ve"].count)
        for f in range(4):
            d = xf8[ck][:, f * TOK:(f + 1) * TOK]
            ctx.op("vector", lambda e, d=d, f=f: e.tensor_tensor(
                d, big[:, f * TOK:(f + 1) * TOK],
                big[:, (f + 4) * TOK:(f + 5) * TOK], ALU.add))
            for r in range(2, 8):
                ctx.op("vector", lambda e, d=d, f=f, r=r: e.tensor_tensor(
                    d, d, big[:, (f + 4 * r) * TOK:(f + 4 * r + 1) * TOK],
                    ALU.add), S["ve"] if (f == 3 and r == 7) else None)
        ctx.wait("tensor", S["ve"], S["ve"].count)
        for i3 in range(3):
            for h2 in range(2):
                t0, t1 = h2 * 512, (h2 + 1) * 512
                ctx.op("tensor", lambda e, i3=i3, t0=t0, t1=t1:
                       e.matmul(pp[i3][:, t0:t1], identb[:, :], big[:, 0:512],
                                start=True, stop=True),
                       S["peog"] if (i3 == 2 and h2 == 1) else None)
        ctx.wait("tensor", S["peog"], S["peog"].count)
        q3ev = {}

        def qkv3_evac(oc, ps, og_tgt, ck=ck, q3ev=q3ev):
            slot = oc % 2
            ctx.wait("scalar", S["peog"], og_tgt)
            if oc >= 2:
                ps_, pv = q3ev[oc - 2]
                ctx.wait("scalar", ps_, pv)
            ctx.op("scalar", lambda e, s=slot, p=ps, oc=oc:
                   e.activation(stage[s][:, 0:TOK], p[:, :], AF.Identity,
                                bias=bq3[:, oc:oc + 1], scale=1.0), S["ev"])
            if oc < 8:
                ctx.wait("sync", S["ev"], S["ev"].count)
                stc[slot].go(q3loc[oc * 128:(oc + 1) * 128,
                                   ck * TOK:(ck + 1) * TOK],
                             stage[slot][:, 0:TOK])
                q3ev[oc] = (stc[slot].sem, stc[slot].sem.count)
            else:
                ctx.wait("gpsimd", S["ev"], S["ev"].count)
                j = oc - 8
                ch = scc[slot]
                if ch.sem.count > 0:
                    ctx.wait("gpsimd", ch.sem, ch.sem.count)

                def fsc(e, j=j, s=slot, ck=ck, ch=ch):
                    e.indirect_dma_start(
                        s3kv[ck][0:128, 0:TOK],
                        IndirectOffsetOnAxis(idxs["i3_sc"][:, j:j + 1], 0),
                        stage[s][:, 0:TOK], None).then_inc(ch.sem.h, 16)
                ctx.emit("gpsimd", fsc)
                ch.sem.count += 16
                q3ev[oc] = (ch.sem, ch.sem.count)

        emit_mm(ctx, f"q3c{ck}", wqkv3, NFC, 24,
                lambda ic: big[:, ic * TOK:(ic + 1) * TOK], qkv3_evac,
                S, pp=pp, goc=3, icb=8, wbufs=wbufs, tiled=True,
                wchans=L['wch6'])
    q3kv_done = [(scc[0].sem, scc[0].sem.count),
                 (scc[1].sem, scc[1].sem.count),
                 (stc[0].sem, stc[0].sem.count),
                 (stc[1].sem, stc[1].sem.count)]
    r3 = pcm.publish(q3kv_done)

    for ck in range(2):
        ctx.barrier_from("tensor", S["ev"])
        ctx.barrier_from("vector", *[c.sem for c in acc2])
        f3_acc = {}
        lrel2 = {}
        f3_t = {}
        ev_prev3 = []
        ogs3 = [list(range(g, min(g + 2, NFC))) for g in range(0, NFC, 2)]

        def issue_f3(l):
            og_l = ogs3[l]
            nc_l = len(og_l) * 128
            c0_l = og_l[0] * 128
            sl = l % 2
            if l >= 2:
                rs, rv = lrel2[l - 2]
                ctx.wait("gpsimd", rs, rv)
            f3_t[l] = wgo6(kmb[sl][:, 0:4 * nc_l]
                           .rearrange("p (i c) -> p i c", i=4),
                           kern8[:, c0_l:c0_l + nc_l]
                           .rearrange("(i p) c -> p i c", p=128))

        for gi, og in enumerate(ogs3):
            ncols = len(og) * 128
            slot = gi % 2
            if gi == 0:
                issue_f3(0)
            if gi + 1 < len(ogs3):
                issue_f3(gi + 1)
            ctx.wait("tensor", f3_t[gi][0], f3_t[gi][1])
            if gi > 0:
                ctx.wait("tensor", S["ev"], ev_prev3[gi - 1])
            for ic in range(4):
                for oi, oc in enumerate(og):
                    for h2 in range(2):
                        t0, t1 = h2 * 512, (h2 + 1) * 512
                        sem1 = (S["peog"] if (ic == 3 and oi == len(og) - 1
                                              and h2 == 1) else None)
                        ctx.op("tensor", lambda e, s=slot, ic=ic, oi=oi,
                               o0=ic * ncols + oi * 128, t0=t0, t1=t1,
                               l=(ic == 3), ck=ck:
                               e.matmul(pp[oi][:, t0:t1], kmb[s][:, o0:o0 + 128],
                                        xf8[ck][:, ic * TOK + t0:ic * TOK + t1],
                                        start=(ic == 0), stop=l), sem1)
            lrel2[gi] = (S["peog"], S["peog"].count)
            og_tgt = S["peog"].count
            for oi, oc in enumerate(og):
                slot2 = oc % 4
                ctx.wait("vector", S["peog"], og_tgt)
                if oc >= 4:
                    ctx.wait("vector", acc2[slot2].sem, f3_acc.get(oc - 4, 0))
                ctx.op("vector", lambda e, s=slot2, oi=oi:
                       e.tensor_copy(fst4[s], pp[oi][:, :]), S["ev"])
                ctx.wait("gpsimd", S["ev"], S["ev"].count)
                acc2[slot2].go(xA2[ck][oc * 128:(oc + 1) * 128, :],
                               fst4[slot2], accum_op=ALU.add)
                f3_acc[oc] = acc2[slot2].sem.count
            ev_prev3.append(S["ev"].count)
        for _c in acc2:
            ctx.wait("gpsimd", _c.sem, _c.sem.count)

    pcm.poll(r3)
    pcm.barrier_engines()
    _l3_flash_fused(ctx, L, locals())


def _l3_flash_fused(ctx, L, M):
    S = L["S"]
    stc, lwch, gcc = L["stc"], L["lwch"], L["gcc"]
    acc2 = L["acc2"]
    idxs, s3kv = L["idxs"], L["s3kv"]
    pp, pp3, pp4 = L["pp"], L["pp3"], L["pp4"]
    big, arB = L["big"], L["arB"]
    fstage, stage = L["fstage"], L["stage"]
    cv, identb, onesb, onesf = L["cv"], L["identb"], L["onesb"], L["onesf"]
    ctb4, stb4 = L["ctb4"], L["stb4"]
    xA2, xout = L["xA2"], L["xout"]
    wo3 = L["wo3"]
    wbufs = L["wbufs"]
    q3loc, k3loc, v3tloc = M["q3loc"], M["k3loc"], M["v3tloc"]
    bo3sb = M["bo3sb"]
    fst4 = M["fst4"]

    wch6 = M["wch6"]
    wpi3 = [0]

    def wgo3(dst_ap, src_ap):
        ch = wch6[wpi3[0] % len(wch6)]
        wpi3[0] += 1
        ch.go(dst_ap, src_ap)
        return (ch.sem, ch.sem.count)

    stg = [big[:, 16384 + i * 2048:16384 + (i + 1) * 2048] for i in range(2)]
    asm = [big[:, 20480 + i * 2048:20480 + (i + 1) * 2048] for i in range(2)]
    efull = arB[:, 0:16384]
    esum3 = arB[:, 16384:16896]
    v3b = [stb4[:, 0:4096], arB[:, 20480:24576]]
    kst = [stb4[:, 0:2048], stb4[:, 2048:4096]]
    q3t = ctb4[:, 0:4096]
    bc_sb = fstage[0][:, 0:512]
    rd_row = fstage[1][0:1, 0:512]

    def gather(ch, dst_ap, src, idx_col):
        def f(e, dst=dst_ap, src=src, idxap=idx_col):
            e.indirect_dma_start(dst, None, src[0:128, 0:TOK],
                                 IndirectOffsetOnAxis(idxap, 0)).then_inc(
                ch.sem.h, 16)
        if ch.sem.count > 0:
            ctx.wait(ch.queue, ch.sem, ch.sem.count)
        ctx.emit("gpsimd", f)
        ch.sem.count += 16
        return ch.sem.count

    st_done = {}
    for dc in range(8):
        for w in range(2):
            sl = (dc * 2 + w) % 2
            prev = st_done.get((dc * 2 + w) - 2)
            if prev is not None:
                ctx.wait("gpsimd", prev[0], prev[1])
            for ckk in range(2):
                gather(gcc[sl], stg[sl][:, ckk * TOK:(ckk + 1) * TOK],
                       s3kv[ckk], idxs["i3_k"][:, dc * 2 + w:dc * 2 + w + 1])
            ctx.wait("sync", gcc[sl].sem, gcc[sl].sem.count)
            stc[sl].go(k3loc[dc * 128:(dc + 1) * 128, w * 2048:(w + 1) * 2048],
                       stg[sl])
            st_done[dc * 2 + w] = (stc[sl].sem, stc[sl].sem.count)
    vi = 0
    for dc in range(8):
        for w in range(2):
            sl = vi % 2
            prev = st_done.get(16 + vi - 2)
            if prev is not None:
                ctx.wait("gpsimd", prev[0], prev[1])
            for ckk in range(2):
                gather(gcc[sl], stg[sl][:, ckk * TOK:(ckk + 1) * TOK],
                       s3kv[ckk], idxs["i3_v"][:, dc * 2 + w:dc * 2 + w + 1])
            ctx.wait("tensor", gcc[sl].sem, gcc[sl].sem.count)
            if vi >= 2:
                ctx.wait("tensor", stc[sl].sem, st_done[16 + vi - 2][1])
            for tb in range(2):
                if not (vi == 0 and tb == 0):
                    ctx.wait("tensor", S["ve"], S["ve"].count)
                for t in range(8):
                    ctx.op("tensor", lambda e, sl=sl, tb=tb, t=t: e.transpose(
                        pp4[:, t * 128:(t + 1) * 128],
                        stg[sl][:, (tb * 8 + t) * 128:(tb * 8 + t + 1) * 128],
                        identb[:, :]),
                        S["peog"] if t == 7 else None)
                ctx.wait("vector", S["peog"], S["peog"].count)
                ctx.op("vector", lambda e, sl=sl, tb=tb: e.tensor_copy(
                    asm[sl][:, tb * 1024:(tb + 1) * 1024], pp4[:, 0:1024]),
                    S["ve"])
            ctx.wait("sync", S["ve"], S["ve"].count)
            stc[sl].go(v3tloc[w * 2048:(w + 1) * 2048, dc * 128:(dc + 1) * 128]
                       .rearrange("(j p) c -> p j c", p=128),
                       asm[sl][:, :].rearrange("p (j c) -> p j c", j=16))
            st_done[16 + vi] = (stc[sl].sem, stc[sl].sem.count)
            vi += 1
    for chn in stc:
        ctx.wait("gpsimd", chn.sem, chn.sem.count)
    ctx.wait("tensor", S["ve"], S["ve"].count)

    qch = L["mbch"]
    for s in range(4):
        ctx.barrier_from("gpsimd", S["pe"])
        qch.go(q3t[:, :].rearrange("p (dc c) -> p dc c", c=512),
               q3loc[:, s * 512:(s + 1) * 512]
               .rearrange("(dc p) c -> p dc c", p=128))
        ctx.wait("tensor", qch.sem, qch.sem.count)
        groups = [(g, g + 1) for g in range(0, 32, 2)]
        grp_tr = {}
        grp_ld = {}
        for gi, grp in enumerate(groups):
            scp = pp[gi % 2]
            sl = gi % 2
            if gi >= 2:
                ctx.wait("gpsimd", S["pe"], grp_ld[gi - 2])
            kw = wgo3(kst[sl][:, :].rearrange("p (dc c) -> p dc c", c=256),
                      k3loc[:, grp[0] * 128:grp[0] * 128 + 256]
                      .rearrange("(dc p) c -> p dc c", p=128))
            ctx.wait("tensor", kw[0], kw[1])
            if gi >= 2:
                ctx.wait("tensor", S["tr"], grp_tr[gi - 2])
            for i in range(2):
                for dc in range(8):
                    ctx.op("tensor", lambda e, scp=scp, sl=sl, i=i, dc=dc:
                           e.matmul(scp[:, i * 512:(i + 1) * 512],
                                    kst[sl][:, dc * 256 + i * 128:
                                            dc * 256 + i * 128 + 128],
                                    q3t[:, dc * 512:(dc + 1) * 512],
                                    start=(dc == 0), stop=(dc == 7)),
                           S["pe"] if (i == 1 and dc == 7) else None)
            grp_ld[gi] = S["pe"].count
            ctx.wait("scalar", S["pe"], S["pe"].count)
            ctx.op("scalar", lambda e, scp=scp, c0=grp[0] * 512:
                   e.activation(efull[:, c0:c0 + 1024], scp[:, 0:1024],
                                AF.Exp, bias=cv[:, CV_SHIFT3:CV_SHIFT3 + 1],
                                scale=0.03125), S["tr"])
            grp_tr[gi] = S["tr"].count
            ctx.wait("vector", S["tr"], S["tr"].count)
            for kb in grp:
                esl = efull[:, kb * 512:(kb + 1) * 512]
                if kb == 0:
                    ctx.op("vector", lambda e, esl=esl:
                           e.tensor_copy(esum3[:, :], esl))
                else:
                    ctx.op("vector", lambda e, esl=esl:
                           e.tensor_tensor(esum3[:, :], esum3[:, :], esl,
                                           ALU.add))
            ctx.op("vector", lambda e: e.engine_nop(), S["ve"])
        ctx.wait("tensor", S["ve"], S["ve"].count)
        ctx.op("tensor", lambda e: e.matmul(pp3[0:1, 0:512], onesb[:, :],
                                            esum3[:, :], start=True, stop=True),
               S["pe"])
        ctx.wait("vector", S["pe"], S["pe"].count)
        ctx.op("vector", lambda e: e.reciprocal(rd_row, pp3[0:1, 0:512]),
               S["ve"])
        ctx.wait("tensor", S["ve"], S["ve"].count)
        ctx.op("tensor", lambda e: e.matmul(pp3[:, 0:512], onesf[:, :], rd_row,
                                            start=True, stop=True), S["pe"])
        ctx.wait("vector", S["pe"], S["pe"].count)
        ctx.op("vector", lambda e: e.tensor_copy(bc_sb, pp3[:, 0:512]), S["ve"])
        bc_t = S["ve"].count
        av_ps = [pp[2][:, 0:512], pp3[:, 0:512]]
        av_ev = {}
        for dc in range(8):
            vsl = dc % 2
            gdc = s * 8 + dc
            if dc == 0:
                ctx.wait("gpsimd", S["pe"], grp_ld[len(groups) - 1])
            if gdc >= 2:
                ctx.wait("gpsimd", S["pe"], vload_rel[gdc - 2])
            vw = wgo3(v3b[vsl][:, :].rearrange("p (kb d) -> p kb d", d=128),
                      v3tloc[:, dc * 128:(dc + 1) * 128]
                      .rearrange("(kb p) d -> p kb d", p=128))
            ctx.wait("tensor", vw[0], vw[1])
            if dc >= 2:
                ctx.wait("tensor", S["ve"], av_ev[dc - 2])
            elif dc == 1:
                ctx.wait("tensor", S["ve"], bc_t)
            elif s > 0:
                ctx.wait("tensor", S["ve"], av_prev_tile)
            for kb in range(32):
                ctx.op("tensor", lambda e, vsl=vsl, kb=kb, ps=av_ps[dc % 2]:
                       e.matmul(ps, v3b[vsl][:, kb * 128:(kb + 1) * 128],
                                efull[:, kb * 512:(kb + 1) * 512],
                                start=(kb == 0), stop=(kb == 31)),
                       S["pe"] if kb == 31 else None)
            if gdc == 0:
                vload_rel = {}
            vload_rel[gdc] = S["pe"].count
            ctx.wait("vector", S["pe"], S["pe"].count)
            ctx.wait("vector", S["ve"], bc_t)
            ctx.op("vector", lambda e, dc=dc, s=s, ps=av_ps[dc % 2]:
                   e.tensor_tensor(
                big[:, dc * 2048 + s * 512:dc * 2048 + s * 512 + 512],
                ps, bc_sb, ALU.mult), S["ve"])
            av_ev[dc] = S["ve"].count
        av_prev_tile = S["ve"].count

    ctx.barrier_from("tensor", S["ve"])
    for ck in range(2):
        ctx.barrier_from("vector", *[c.sem for c in acc2])
        w_acc = {}

        def wo3_evac(oc, ps, og_tgt, ck=ck, w_acc=w_acc):
            slot = oc % 4
            ctx.wait("vector", S["peog"], og_tgt)
            if oc >= 4:
                ctx.wait("vector", acc2[slot].sem, w_acc[oc - 4])
            ctx.op("vector", lambda e, s=slot, p=ps:
                   e.tensor_copy(fst4[s], p[:, :]))
            ctx.op("vector", lambda e, s=slot, oc=oc:
                   e.tensor_scalar(fst4[s], fst4[s],
                                   bo3sb[:, oc:oc + 1], None, ALU.add), S["ev"])
            ctx.wait("gpsimd", S["ev"], S["ev"].count)
            acc2[slot].go(xA2[ck][oc * 128:(oc + 1) * 128, :],
                          fst4[slot], accum_op=ALU.add)
            w_acc[oc] = acc2[slot].sem.count

        emit_mm(ctx, f"wo3c{ck}", wo3, 8, NFC,
                lambda ic, ck=ck: big[:, ic * 2048 + ck * TOK:
                                      ic * 2048 + (ck + 1) * TOK],
                wo3_evac, S, pp=pp, goc=3, icb=8, wbufs=wbufs, tiled=True,
                wchans=M['wch6'])
        for _c in acc2:
            ctx.wait("gpsimd", _c.sem, _c.sem.count)
        ctx.dma("gpsimd", xout[:, ck * TOK:(ck + 1) * TOK], xA2[ck], S["st"])
    ctx.wait("gpsimd", S["st"], S["st"].count)


def host_tile_w(w):
    w = np.asarray(w)
    n_in = w.shape[0] // 128
    n_out = w.shape[1] // 128
    return np.ascontiguousarray(
        w.reshape(n_in, 128, n_out, 128).transpose(1, 2, 0, 3)
        .reshape(128, n_out * n_in * 128))


def host_fused_weights(ins):
    def T_(a):
        return np.ascontiguousarray(np.asarray(a).T.astype(np.float32,
                                                           copy=False))
    d = {}
    for l in range(2):
        d[f"wqkvT{l}"] = host_tile_w(np.concatenate(
            [np.asarray(ins[f"L{l}_attn_Wq"]).T,
             np.asarray(ins[f"L{l}_attn_Wk"]).T,
             np.asarray(ins[f"L{l}_attn_Wv"]).T], axis=1)
            .astype(np.float32, copy=False))
        d[f"woT{l}"] = host_tile_w(T_(ins[f"L{l}_attn_Wo"]))
    d["w0T"] = host_tile_w(T_(ins["L0_ffn_weight"]))
    d["bias0"] = np.ascontiguousarray(
        np.asarray(ins["L0_ffn_bias"]).reshape(NFC, 128).T)
    d["gwT"] = T_(ins["L0_ffn_gate_w"])
    d["ics"] = np.ascontiguousarray(np.asarray(ins["L0_ffn_in_channel_scale"]))
    d["ocs"] = np.ascontiguousarray(np.asarray(ins["L0_ffn_out_channel_scale"]))
    pw = np.asarray(ins["L1_ffn_packed_weight"])
    pb = np.ascontiguousarray(pw).view(np.int8)[..., 0::4].astype(np.int8,
                                                                  copy=True)
    pb = pb.view(np.uint8)
    d["pwT"] = np.ascontiguousarray(pb.transpose(1, 2, 0)).reshape(2048, 4096)
    scl = np.asarray(ins["L1_ffn_scales"]).reshape(4096, 32)
    d["sclT"] = np.ascontiguousarray(scl.T.astype(np.float32, copy=False))
    sel2 = np.zeros((2, 128), np.float32)
    sel2[0, 0:64] = 1.0
    sel2[1, 64:128] = 1.0
    d["sel2"] = sel2
    rb = np.arange(128)
    d["indT"] = (np.arange(D)[None, :] // 32 == rb[:, None]).astype(np.float32)
    d["rep4"] = (np.arange(128)[None, :] // 32 ==
                 np.arange(4)[:, None]).astype(np.float32)
    d["g4"] = (np.arange(128)[:, None] // 32 ==
               np.arange(4)[None, :]).astype(np.float32)
    mask = np.ascontiguousarray(np.asarray(ins["L2_attn_mask"]))
    d["mask8"] = np.ascontiguousarray(mask.view(np.int8)[:, 0::4]).view(np.uint8)
    d["w2aT"] = T_(ins["L2_attn_weight"])
    d["bias2"] = np.ascontiguousarray(
        np.asarray(ins["L2_attn_bias"]).reshape(NFC, 128).T)
    a2 = np.asarray(ins["L2_ffn_a2"])
    d["a2Tp"] = np.ascontiguousarray(
        a2.T.reshape(86, 128, 2).transpose(1, 0, 2).reshape(128, 172)
        .astype(np.float32))
    d["w1qT"] = host_tile_w(T_(ins["L2_ffn_w1q"]))
    d["w3qT"] = host_tile_w(T_(ins["L2_ffn_w3q"]))
    d["w2qT"] = T_(ins["L2_ffn_w2q"])
    d["a1T"] = T_(ins["L2_ffn_a1"])
    d["a3T"] = T_(ins["L2_ffn_a3"])
    d["b1T"] = T_(ins["L2_ffn_b1"])
    d["b3T"] = T_(ins["L2_ffn_b3"])
    d["b2T"] = T_(ins["L2_ffn_b2"])
    d["wqkv3"] = host_tile_w(
        np.asarray(ins["L3_attn_Wqkv"]).astype(np.float32))
    d["bqkv3"] = np.ascontiguousarray(
        np.asarray(ins["L3_attn_bqkv"]).reshape(24, 128).T)
    d["kern8"] = np.ascontiguousarray(
        np.asarray(ins["L3_ffn_kernel"]).reshape(512, D))
    d["wo3"] = host_tile_w(
        np.asarray(ins["L3_attn_Wo"]).astype(np.float32))
    d["bo3"] = np.ascontiguousarray(
        np.asarray(ins["L3_attn_bo"]).reshape(NFC, 128).T)
    d["cvec"] = host_cvec()
    d["ident"] = np.eye(128, dtype=np.float32)
    d["trimask"] = host_trimask()
    return d


def host_fused_percore(x0, core, nonces):
    slot = core % 2
    batch = core // 4
    xT2 = np.ascontiguousarray(
        x0[batch, 2048 * slot:2048 * slot + 2048].T.astype(np.float32,
                                                           copy=False))
    inv = 1.0 / ROPE_THETA ** (np.arange(0, 128, 2, dtype=np.float64) / 128.0)
    pos = np.arange(2048 * slot, 2048 * slot + 2048, dtype=np.float64)
    fr = pos[None, :] * np.concatenate([inv, inv])[:, None]
    sgn = np.where(np.arange(128) < 64, -1.0, 1.0)[:, None]
    ct = np.zeros((128, 4096), np.float32)
    st = np.zeros((128, 4096), np.float32)
    ct[:, 0:2048] = np.cos(fr)
    st[:, 0:2048] = np.sin(fr) * sgn
    p = np.arange(128, dtype=np.int32)[:, None]
    iq_sc = (slot * 6144 + np.arange(48, dtype=np.int32)[None, :] * 128 + p)
    iq_q = np.empty((128, 32), np.int32)
    for hl in range(16):
        for w in range(2):
            iq_q[:, hl * 2 + w] = (w * 6144 + (16 * slot + hl) * 128 + p[:, 0])
    iq_kv = np.empty((128, 16), np.int32)
    iq_v = np.empty((128, 16), np.int32)
    for kvh in range(4):
        for w in range(2):
            iq_kv[:, kvh * 2 + w] = (w * 6144 + 4096 +
                                     (4 * slot + kvh) * 128 + p[:, 0])
            iq_v[:, kvh * 2 + w] = (w * 6144 + 5120 +
                                    (4 * slot + kvh) * 128 + p[:, 0])
    iao_sc = np.empty((128, 32), np.int32)
    for hl in range(16):
        for r in range(2):
            iao_sc[:, hl * 2 + r] = (r * 4096 + slot * 2048 +
                                     hl * 128 + p[:, 0])
    iao_g = (slot * 4096 + np.arange(32, dtype=np.int32)[None, :] * 128 + p)
    i3_sc = np.empty((128, 16), np.int32)
    for j in range(16):
        i3_sc[:, j] = (slot * 2048 + (0 if j < 8 else 1024) +
                       (j % 8) * 128 + p[:, 0])
    i3_k = np.empty((128, 16), np.int32)
    i3_v = np.empty((128, 16), np.int32)
    for dc in range(8):
        for w in range(2):
            i3_k[:, dc * 2 + w] = w * 2048 + dc * 128 + p[:, 0]
            i3_v[:, dc * 2 + w] = w * 2048 + 1024 + dc * 128 + p[:, 0]
    fidx = np.array([[slot], [slot + 2]], np.int32)
    fvals = np.empty((2, 64 * PairComm.NROUNDS), np.int32)
    for r in range(PairComm.NROUNDS):
        fvals[:, 64 * r:64 * (r + 1)] = nonces[r]
    return dict(xT2=xT2, ctab4=ct, stab4=st,
                iq_sc=iq_sc.astype(np.int32), iq_q=iq_q, iq_kv=iq_kv,
                iq_v=iq_v, iao_sc=iao_sc, iao_g=iao_g.astype(np.int32),
                i3_sc=i3_sc, i3_k=i3_k, i3_v=i3_v, fidx=fidx, fvals=fvals)


def kernel_fused(ins):
    x0 = np.asarray(ins["L0_attn_hidden_states"], np.float32)
    W = host_fused_weights(ins)
    nc = build_fused().finalize()
    nonces = np.random.randint(1, 2 ** 31 - 1, size=PairComm.NROUNDS)
    in_maps = [dict(W, **host_fused_percore(x0, c, nonces))
               for c in range(NCORES)]
    res = run_bass_kernel_spmd(nc, in_maps, core_ids=list(range(NCORES))).results
    out = np.empty((2, 4096, 4096), np.float32)
    for c in (0, 1, 4, 5):
        out[c // 4, 2048 * (c % 2):2048 * (c % 2) + 2048] = \
            np.asarray(res[c]["xout"]).T
    return out


# version 15 (final)
# speedup vs baseline: 1.0561x; 1.0561x over previous
import contextlib
import numpy as np
import ml_dtypes
import concourse.bass as bass
import concourse.mybir as mybir
from concourse.bass import IndirectOffsetOnAxis
from concourse.bass_utils import run_bass_kernel_spmd

F32 = mybir.dt.float32
BF16 = mybir.dt.bfloat16
I32 = mybir.dt.int32
U8 = mybir.dt.uint8
I8 = mybir.dt.int8
AF = mybir.ActivationFunctionType
ALU = mybir.AluOpType
AX = mybir.AxisListType
BF = ml_dtypes.bfloat16

ENGINES = ("sync", "tensor", "vector", "scalar", "gpsimd")
NCORES = 8
TOK = 1024
D = 4096
NFC = D // 128

CV_EPS, CV_SHIFT01, CV_SHIFT3 = 0, 1, 2
FDBG = False
DBG = None
SOFTMAX_SHIFT01 = -14.0
SOFTMAX_SHIFT3 = -10.0


class Sem:
    __slots__ = ("h", "count", "name")

    def __init__(self, h, name):
        self.h = h
        self.name = name
        self.count = 0


class Ctx:
    def __init__(self):
        self.nc = bass.Bass("TRN2", target_bir_lowering=False)
        self._stack = contextlib.ExitStack()
        self.thunks = {e: [] for e in ENGINES}

    def sem(self, name):
        return Sem(self._stack.enter_context(self.nc.semaphore(name)), name)

    def sbuf(self, name, shape, dtype):
        return self._stack.enter_context(self.nc.sbuf_tensor(name, shape, dtype))

    def sbuf_c(self, name, shape, dtype):
        if not hasattr(self, "_sbufc"):
            self._sbufc = {}
        if name not in self._sbufc:
            self._sbufc[name] = self.sbuf(name, shape, dtype)
        return self._sbufc[name]

    def psum(self, name, shape, dtype=F32):
        return self._stack.enter_context(self.nc.psum_tensor(name, shape, dtype))

    def dram(self, name, shape, dtype, kind="Internal", **kw):
        return self.nc.dram_tensor(name, shape, dtype, kind=kind, **kw).ap()

    def emit(self, eng, fn):
        self.thunks[eng].append(fn)

    def wait(self, eng, sem, val=None):
        v = sem.count if val is None else val
        if v > 0:
            self.emit(eng, lambda e, s=sem.h, v=v: e.wait_ge(s, v))

    def dma(self, queue, out, in_, sem, **kw):
        def f(e, out=out, in_=in_, kw=kw, s=sem.h):
            e.dma_start(out, in_, **kw).then_inc(s, 16)
        self.emit(queue, f)
        sem.count += 16
        return sem.count

    def op(self, eng, build, sem=None, n=1):
        if sem is None:
            self.emit(eng, build)
            return None
        if isinstance(sem, list):
            pairs = sem
        else:
            pairs = [(sem, n)]
        def f(e, build=build, pairs=tuple((s.h, k) for s, k in pairs)):
            ins = build(e)
            for h, k in pairs:
                ins = ins.then_inc(h, k)
        self.emit(eng, f)
        for s, k in pairs:
            s.count += k
        return pairs[0][0].count

    def barrier_from(self, eng, *sems):
        for s in sems:
            self.wait(eng, s, s.count)

    def chan(self, queue, name):
        return DmaChan(self, queue, self.sem(name))

    def finalize(self):
        nc = self.nc
        nc.detect_race_conditions = False
        with nc.Block() as block:
            def runner(thunks):
                def run(eng):
                    for t in thunks:
                        t(eng)
                return run
            for name, meth in (("sync", block.sync), ("tensor", block.tensor),
                               ("vector", block.vector), ("scalar", block.scalar),
                               ("gpsimd", block.gpsimd)):
                if self.thunks[name]:
                    meth(runner(self.thunks[name]))
        self._stack.close()
        return nc


class DmaChan:

    def __init__(self, ctx, queue, sem):
        self.ctx = ctx
        self.queue = queue
        self.sem = sem

    def go(self, out, in_, **kw):
        if self.sem.count > 0:
            self.ctx.wait(self.queue, self.sem, self.sem.count)
        return self.ctx.dma(self.queue, out, in_, self.sem, **kw)


def mk_sems(ctx, tag):
    return {k: ctx.sem(f"{tag}_{k}") for k in ("wl", "pe", "peog", "ev", "tr")}


def emit_mm(ctx, name, wt, n_in, n_out, rhs_fn, evac_fn, sems, *,
            pp, goc=3, icb=8, transform=None, ntok=TOK, extra_passes=None,
            wbufs=None, pre_og=None, tiled=False, wchans=None):
    ogs = [list(range(g, min(g + goc, n_out))) for g in range(0, n_out, goc)]
    nhalf = (ntok + 511) // 512
    maxcols = goc * 128 * icb

    if wbufs is not None:
        fbufs, bufs = wbufs
    elif transform is None:
        bufs = [ctx.sbuf(f"{name}_wb{i}", [128, maxcols], BF16) for i in range(2)]
        fbufs = None
    else:
        fbufs = [ctx.sbuf(f"{name}_wf{i}", [128, maxcols], F32) for i in range(2)]
        bufs = [ctx.sbuf(f"{name}_wb{i}", [128, maxcols], BF16) for i in range(2)]
    pss = [p[:, 0:ntok] for p in pp[:min(goc, n_out)]]

    s_pe, s_peog, s_ev, s_tr = (sems["pe"], sems["peog"],
                                sems["ev"], sems["tr"])
    if wchans is None:
        wlc = [ctx.chan("gpsimd", f"{name}_w0c"),
               ctx.chan("gpsimd", f"{name}_w1c")]
    else:
        wlc = wchans
    wch_i = [0]
    load_idx = 0
    pe_release = {}
    ev_targets = []

    for gi, og in enumerate(ogs):
        if pre_og is not None:
            pre_og(gi, og)
        col0 = og[0] * 128
        ncols = len(og) * 128
        icbs = [list(range(i, min(i + icb, n_in))) for i in range(0, n_in, icb)]
        for bi, ics in enumerate(icbs):
            slot = load_idx % 2
            if load_idx >= 2:
                rs, rv = pe_release[load_idx - 2]
                ctx.wait("gpsimd", rs, rv)
            nic = len(ics)
            dstt = (bufs if transform is None else fbufs)[slot]
            wl_pairs = []

            def wgo(dst_ap, src_ap):
                ch = wlc[wch_i[0] % len(wlc)]
                wch_i[0] += 1
                ch.go(dst_ap, src_ap)
                wl_pairs.append((ch.sem, ch.sem.count))

            if tiled:
                for oi, oc in enumerate(og):
                    c0 = oc * n_in * 128 + ics[0] * 128
                    half = (nic + 1) // 2
                    for h0 in range(0, nic, half):
                        h1 = min(nic, h0 + half)
                        wgo(dstt[:, (oi * nic + h0) * 128:
                                 (oi * nic + h1) * 128],
                            wt[:, c0 + h0 * 128:c0 + h1 * 128])
            elif wchans is not None:
                half = (nic + 1) // 2
                for h0 in range(0, nic, half):
                    h1 = min(nic, h0 + half)
                    srca = wt[(ics[0] + h0) * 128:(ics[0] + h1) * 128,
                              col0:col0 + ncols]
                    srca = srca.rearrange("(i p) c -> p i c", p=128)
                    dsta = dstt[:, h0 * ncols:h1 * ncols].rearrange(
                        "p (i c) -> p i c", i=h1 - h0)
                    wgo(dsta, srca)
            else:
                srca = wt[ics[0] * 128:(ics[-1] + 1) * 128, col0:col0 + ncols]
                srca = srca.rearrange("(i p) c -> p i c", p=128)
                dst = dstt[:, 0:nic * ncols].rearrange("p (i c) -> p i c",
                                                       i=nic)
                wgo(dst, srca)
            if transform == "sign":
                for ws, wv in wl_pairs:
                    ctx.wait("scalar", ws, wv)
                ctx.op("scalar", lambda e, s=slot, n=nic * ncols:
                       e.activation(bufs[s][:, 0:n], fbufs[s][:, 0:n], AF.Sign),
                       s_tr)
                ctx.wait("tensor", s_tr, s_tr.count)
            else:
                for ws, wv in wl_pairs:
                    ctx.wait("tensor", ws, wv)
            if bi == 0 and gi > 0:
                ctx.wait("tensor", s_ev, ev_targets[gi - 1])
            any_extra = (extra_passes is not None
                         and any(extra_passes(oc) for oc in og))
            for j, ic in enumerate(ics):
                first = (bi == 0 and j == 0)
                last_ic = (bi == len(icbs) - 1 and j == len(ics) - 1)
                for oi, oc in enumerate(og):
                    if tiled:
                        lhsT = bufs[slot][:, oi * nic * 128 + j * 128:
                                          oi * nic * 128 + (j + 1) * 128]
                    else:
                        lhsT = bufs[slot][:, j * ncols + oi * 128:
                                          j * ncols + (oi + 1) * 128]
                    rhs = rhs_fn(ic)
                    for h in range(nhalf):
                        t0, t1 = h * 512, min(ntok, (h + 1) * 512)
                        is_last = (last_ic
                                   and (extra_passes is None or not extra_passes(oc)))
                        grp_last = (j == len(ics) - 1 and oi == len(og) - 1
                                    and h == nhalf - 1)
                        og_last = (grp_last and bi == len(icbs) - 1
                                   and not any_extra)
                        sem1 = s_peog if og_last else (s_pe if grp_last else None)
                        ctx.op("tensor", lambda e, o=pss[oi][:, t0:t1], a=lhsT,
                               b=rhs[:, t0:t1], fs=first, l=is_last:
                               e.matmul(o, a, b, start=fs, stop=l), sem1)
                        if grp_last:
                            pe_release[load_idx] = (sem1, sem1.count)
            load_idx += 1
        if any_extra:
            eplist = [(oc, extra_passes(oc)) for oc in og]
            n_ep = sum(len(ep) for _, ep in eplist)
            cnt = 0
            for oi, (oc, ep) in enumerate(eplist):
                for k, (lhsT_ap, rhs_ap) in enumerate(ep):
                    lastk = (k == len(ep) - 1)
                    for h in range(nhalf):
                        t0, t1 = h * 512, min(ntok, (h + 1) * 512)
                        cnt_last = (oi == len(eplist) - 1 and k == len(ep) - 1
                                    and h == nhalf - 1)
                        ctx.op("tensor", lambda e, o=pss[oi][:, t0:t1], a=lhsT_ap,
                               b=rhs_ap[:, t0:t1], l=lastk:
                               e.matmul(o, a, b, start=False, stop=l),
                               [(s_peog, 1)] if cnt_last else None)
        og_tgt = s_peog.count
        for oi, oc in enumerate(og):
            evac_fn(oc, pss[oi], og_tgt)
        ev_targets.append(s_ev.count)
    return ev_targets[-1] if ev_targets else 0


def emit_ln(ctx, name, xsrc, xb, aux, sems, ntok=TOK):
    s_xl, s_pe, s_ve, s_tr, s_ac = (sems["xl"], sems["pe"], sems["ve"],
                                    sems["tr"], sems["ac"])
    pp = aux["pp"]
    nhalf = ntok // 512
    if "ln_shared" not in aux:
        aux["ln_shared"] = dict(
            sq=[ctx.sbuf(f"lnsh_sq{i}", [128, ntok], BF16) for i in range(2)],
            rows={k: ctx.sbuf(f"lnsh_{k}", [1, ntok], F32)
                  for k in ("m", "msq", "var", "std", "rstd", "bneg")})
    sq = aux["ln_shared"]["sq"]
    rows = aux["ln_shared"]["rows"]
    ps_sx = pp[0][0:1, 0:ntok]
    ps_sq = pp[1][0:1, 0:ntok]
    ps_ba = pp[0][:, 0:ntok]
    ps_bb = pp[1][:, 0:ntok]

    for fc in range(NFC):
        ctx.dma("gpsimd", xb[:, fc * ntok:(fc + 1) * ntok],
                xsrc[fc * 128:(fc + 1) * 128, :], s_xl)
    xl_all = s_xl.count
    ctx.wait("scalar", s_xl, xl_all)
    ctx.wait("tensor", s_xl, xl_all)
    sq_release = {}
    for fc in range(NFC):
        slot = fc % 2
        if fc >= 2:
            ctx.wait("scalar", s_pe, sq_release[fc - 2])
        ctx.op("scalar", lambda e, s=slot, fc=fc:
               e.activation(sq[s][:, :], xb[:, fc * ntok:(fc + 1) * ntok], AF.Square),
               s_tr)
        tr_tgt = s_tr.count
        ctx.wait("tensor", s_tr, tr_tgt)
        for h in range(nhalf):
            t0, t1 = h * 512, (h + 1) * 512
            ctx.op("tensor", lambda e, fc=fc, t0=t0, t1=t1:
                   e.matmul(ps_sx[:, t0:t1], aux["onesb"][:, :],
                            xb[:, fc * ntok + t0:fc * ntok + t1],
                            start=(fc == 0), stop=(fc == NFC - 1)))
            ctx.op("tensor", lambda e, s=slot, t0=t0, t1=t1, fc=fc:
                   e.matmul(ps_sq[:, t0:t1], aux["onesb"][:, :],
                            sq[s][:, t0:t1], start=(fc == 0), stop=(fc == NFC - 1)),
                   s_pe if h == nhalf - 1 else None)
        sq_release[fc] = s_pe.count
    pe_sums = s_pe.count
    ctx.wait("vector", s_pe, pe_sums)
    ctx.op("vector", lambda e: e.tensor_scalar(rows["m"][:, :], ps_sx[:, :],
                                               1.0 / D, None, ALU.mult), s_ve)
    ctx.wait("vector", s_ve, s_ve.count)
    ctx.op("vector", lambda e: e.tensor_mul(rows["msq"][:, :], rows["m"][:, :],
                                            rows["m"][:, :]), s_ve)
    ctx.wait("vector", s_ve, s_ve.count)
    ctx.op("vector", lambda e: e.scalar_tensor_tensor(
        rows["var"][:, :], ps_sq[:, :], 1.0 / D, rows["msq"][:, :],
        ALU.mult, ALU.subtract), s_ve)
    ctx.wait("scalar", s_ve, s_ve.count)
    ctx.op("scalar", lambda e: e.activation(rows["std"][:, :], rows["var"][:, :],
                                            AF.Sqrt,
                                            bias=aux["cv"][0:1, CV_EPS:CV_EPS + 1],
                                            scale=1.0), s_tr)
    ctx.wait("vector", s_tr, s_tr.count)
    ctx.op("vector", lambda e: e.reciprocal(rows["rstd"][:, :], rows["std"][:, :]), s_ve)
    ctx.wait("vector", s_ve, s_ve.count)
    ctx.op("vector", lambda e: e.scalar_tensor_tensor(
        rows["bneg"][:, :], rows["m"][:, :], -1.0, rows["rstd"][:, :],
        ALU.mult, ALU.mult), s_ve)
    ctx.wait("tensor", s_ve, s_ve.count)
    for h in range(nhalf):
        t0, t1 = h * 512, (h + 1) * 512
        ctx.op("tensor", lambda e, t0=t0, t1=t1:
               e.matmul(ps_ba[:, t0:t1], aux["onesf"][:, :],
                        rows["rstd"][:, t0:t1], start=True, stop=True))
        ctx.op("tensor", lambda e, t0=t0, t1=t1:
               e.matmul(ps_bb[:, t0:t1], aux["onesf"][:, :],
                        rows["bneg"][:, t0:t1], start=True, stop=True),
               s_pe if h == nhalf - 1 else None)
    ctx.wait("vector", s_pe, s_pe.count)
    for fc in range(NFC):
        a = xb[:, fc * ntok:(fc + 1) * ntok]
        ctx.op("vector", lambda e, a=a: e.tensor_tensor(a, a, ps_ba[:, :], ALU.mult))
        ctx.op("vector", lambda e, a=a: e.tensor_tensor(a, a, ps_bb[:, :], ALU.add))
    ctx.op("vector", lambda e: e.engine_nop(), s_ac)
    return s_ac.count


def build_A01(lay):
    ctx = Ctx()
    if lay == 0:
        xsrc = ctx.dram("xT", [D, TOK], F32, kind="ExternalInput")
    else:
        xAp = ctx.dram("xAp", [D, TOK], F32, kind="ExternalInput")
        aoT = ctx.dram("aoT", [D, TOK], BF16, kind="ExternalInput")
        woT = ctx.dram("woT", [D, D], F32, kind="ExternalInput")
        xsrc = ctx.dram("x", [D, TOK], F32)
    wqkvT = ctx.dram("wqkvT", [D, 6144], F32, kind="ExternalInput")
    ctab = ctx.dram("ctab", [128, TOK], F32, kind="ExternalInput")
    stab = ctx.dram("stab", [128, TOK], F32, kind="ExternalInput")
    cvec = ctx.dram("cvec", [128, 8], F32, kind="ExternalInput")
    ident = ctx.dram("ident", [128, 128], F32, kind="ExternalInput")
    qkvT = ctx.dram("qkvT", [6144, TOK], BF16, kind="ExternalOutput")
    xA = ctx.dram("xA", [D, TOK], F32, kind="ExternalOutput")
    dbg = ctx.dram("dbg", [D, TOK], F32, kind="ExternalOutput") if DBG else None
    if lay == 0:
        w0T = ctx.dram("w0T", [D, D], F32, kind="ExternalInput")
        bias0 = ctx.dram("bias0", [128, NFC], F32, kind="ExternalInput")
        gwT = ctx.dram("gwT", [D, 8], F32, kind="ExternalInput")
        ics = ctx.dram("ics", [8, D], F32, kind="ExternalInput")
        ocs = ctx.dram("ocs", [8, D], F32, kind="ExternalInput")
    else:
        pwT = ctx.dram("pwT", [2048, D], U8, kind="ExternalInput")
        sclT = ctx.dram("sclT", [32, D], F32, kind="ExternalInput")
        sel2 = ctx.dram("sel2", [2, 128], F32, kind="ExternalInput")
        w1T = ctx.dram("w1T", [D, D], BF16)

    S = {k: ctx.sem(k) for k in
         ("wl", "xl", "pe", "peog", "ev", "tr", "ve", "ac", "st", "pe2",
          "tb", "ao", "xc", "xac")}
    acc2 = [ctx.chan("gpsimd", "acc0"), ctx.chan("gpsimd", "acc1")]
    stc = [ctx.chan("sync", "stc0"), ctx.chan("sync", "stc1")]
    pp = [ctx.psum(f"pp{i}", [128, TOK], F32) for i in range(3)]
    pp3 = ctx.psum("pp3", [128, 512], F32)
    pp4 = ctx.psum("pp4", [128, TOK], BF16)

    big = ctx.sbuf("big", [128, NFC * TOK], BF16)
    onesb = ctx.sbuf("onesb", [128, 1], BF16)
    onesf = ctx.sbuf("onesf", [1, 128], F32)
    cv = ctx.sbuf("cv", [128, 8], F32)
    ctb = ctx.sbuf("ctb", [128, TOK], BF16)
    stb = ctx.sbuf("stb", [128, TOK], BF16)
    stage = [ctx.sbuf(f"stage{i}", [128, TOK], BF16) for i in range(2)]
    fstage = [ctx.sbuf(f"fstage{i}", [128, TOK], F32) for i in range(2)]
    rot = ctx.sbuf("rot", [128, TOK], BF16)
    tmpb = ctx.sbuf("tmpb", [128, TOK], BF16)
    identb = ctx.sbuf("identb", [128, 128], BF16)
    aux = dict(onesb=onesb, onesf=onesf, cv=cv, pp=pp, identb=identb)

    ctx.op("gpsimd", lambda e: e.memset(onesb[:, :], 1.0))
    ctx.op("gpsimd", lambda e: e.memset(onesf[:, :], 1.0), S["pe2"], 1)
    ctx.wait("tensor", S["pe2"], 1)
    ctx.dma("gpsimd", cv[:, :], cvec, S["tb"])
    ctx.dma("gpsimd", ctb[:, :], ctab, S["tb"])
    ctx.dma("gpsimd", stb[:, :], stab, S["tb"])
    ctx.dma("gpsimd", identb[:, :], ident, S["tb"])
    tabs_tgt = S["tb"].count
    ctx.wait("tensor", S["tb"], tabs_tgt)
    ctx.wait("vector", S["tb"], tabs_tgt)
    ctx.wait("scalar", S["tb"], tabs_tgt)

    if lay == 1:
        ctx.dma("gpsimd", xsrc, xAp, S["xc"])
        xcopy_tgt = S["xc"].count
        for fc in range(NFC):
            ctx.dma("sync", big[:, fc * TOK:(fc + 1) * TOK],
                    aoT[fc * 128:(fc + 1) * 128, :], S["ao"])
        ao_tgt = S["ao"].count
        ctx.wait("tensor", S["ao"], ao_tgt)

        def wo_evac(oc, ps, og_tgt):
            ctx.wait("vector", S["peog"], og_tgt)
            slot = oc % 2
            if oc >= 2:
                ctx.wait("vector", acc2[slot].sem, wo_acc[oc - 2])
            ctx.op("vector", lambda e, s=slot, p=ps:
                   e.tensor_copy(fstage[s][:, :], p[:, :]), S["ev"])
            ctx.wait("gpsimd", S["ev"], S["ev"].count)
            ctx.wait("gpsimd", S["xc"], xcopy_tgt)
            acc2[slot].go(xsrc[oc * 128:(oc + 1) * 128, :],
                          fstage[slot][:, :], accum_op=ALU.add)
            wo_acc[oc] = acc2[slot].sem.count
        wo_acc = {}
        emit_mm(ctx, "wo", woT, NFC, NFC,
                lambda ic: big[:, ic * TOK:(ic + 1) * TOK], wo_evac,
                S, pp=pp, goc=3, icb=8)
        for _c in acc2:
            ctx.wait("gpsimd", _c.sem, _c.sem.count)

    sems_ln = dict(xl=S["xl"], pe=S["pe"], ve=S["ve"], tr=S["tr"], ac=S["ve"])
    ctx.barrier_from("tensor", S["ev"], S["ve"])
    xn_tgt = emit_ln(ctx, "ln", xsrc, big, aux, sems_ln)

    ctx.wait("gpsimd", S["ve"], xn_tgt)
    ctx.dma("gpsimd", xA, xsrc, S["xac"])
    xA_copy = S["xac"].count

    ctx.barrier_from("tensor", S["ve"])
    qkv_store = {}

    def qkv_evac(oc, ps, og_tgt):
        slot = oc % 2
        ctx.wait("scalar", S["peog"], og_tgt)
        if oc >= 2:
            ctx.wait("scalar", stc[slot].sem, qkv_store[oc - 2])
        ctx.op("scalar", lambda e, s=slot, p=ps:
               e.activation(stage[s][:, :], p[:, :], AF.Identity, bias=0.0, scale=1.0),
               S["ev"])
        ev_tgt = S["ev"].count
        if oc < 40:
            ctx.wait("vector", S["ev"], ev_tgt)
            ctx.op("vector", lambda e, s=slot: e.tensor_copy(rot[0:64, :], stage[s][64:128, :]))
            ctx.op("vector", lambda e, s=slot: e.tensor_copy(rot[64:128, :], stage[s][0:64, :]))
            ctx.op("vector", lambda e, s=slot: e.tensor_tensor(tmpb[:, :], stage[s][:, :], ctb[:, :], ALU.mult))
            ctx.op("vector", lambda e: e.tensor_tensor(rot[:, :], rot[:, :], stb[:, :], ALU.mult))
            ctx.op("vector", lambda e, s=slot: e.tensor_tensor(stage[s][:, :], tmpb[:, :], rot[:, :], ALU.add),
                   S["ve"])
            ctx.wait("sync", S["ve"], S["ve"].count)
        else:
            ctx.wait("sync", S["ev"], ev_tgt)
        stc[slot].go(qkvT[oc * 128:(oc + 1) * 128, :], stage[slot][:, :])
        qkv_store[oc] = stc[slot].sem.count

    emit_mm(ctx, "qkv",
            wqkvT, NFC, 48, lambda ic: big[:, ic * TOK:(ic + 1) * TOK],
            qkv_evac, S, pp=pp, goc=3, icb=8)

    aux2 = dict(aux, pp3=pp3, pp4=pp4, fstage=fstage, acc2=acc2, dbg=dbg)
    if lay == 0:
        emit_ffn0(ctx, big, xA, aux2, S,
                  dict(w0T=w0T, bias0=bias0, gwT=gwT, ics=ics, ocs=ocs),
                  xA_copy)
    else:
        emit_ffn1(ctx, big, xA, aux2, S, dict(pwT=pwT, sclT=sclT, w1T=w1T,
                                              sel2=sel2), xA_copy)

    for ch in acc2 + stc:
        ctx.wait("gpsimd", ch.sem, ch.sem.count)
    ctx.barrier_from("gpsimd", S["xac"], S["st"])
    return ctx


def emit_ffn0(ctx, big, xA, aux, S, W, xA_copy, loaded=False, wbufs=None,
              tiled=False, wchans=None):
    pp, pp3, pp4, fstage = aux["pp"], aux["pp3"], aux["pp4"], aux["fstage"]
    gw = aux.get("f0_gw") if "f0_gw" in aux else ctx.sbuf_c(
        "f0_gw", [128, NFC * 8], BF16)
    icsb = aux.get("f0_icsb") if "f0_icsb" in aux else ctx.sbuf_c(
        "f0_ics", [8, D], BF16)
    ocsb = aux.get("f0_ocsb") if "f0_ocsb" in aux else ctx.sbuf_c(
        "f0_ocs", [8, D], BF16)
    biasb = ctx.sbuf_c("f0_bias", [128, NFC], F32)
    rtT = aux.get("f0_rtT") if "f0_rtT" in aux else ctx.sbuf_c(
        "f0_rtT", [8, TOK], BF16)
    pexp = ctx.sbuf_c("f0_pexp", [128, 8], BF16)
    rm = ctx.sbuf_c("f0_rm", [128, 1], F32)
    rsum = ctx.sbuf_c("f0_rsum", [128, 1], F32)
    rrcp = ctx.sbuf_c("f0_rrcp", [128, 1], F32)
    rtile = ctx.sbuf_c("f0_rtile", [128, 8], BF16)
    osb = ctx.sbuf_c("f0_osb", [128, 3 * TOK], F32)

    if not loaded:
        ctx.dma("gpsimd", gw[:, :].rearrange("p (c e) -> p c e", e=8),
                W["gwT"].rearrange("(c p) e -> p c e", p=128), S["wl"])
        ctx.dma("gpsimd", icsb[:, :], W["ics"], S["wl"])
        ctx.dma("gpsimd", ocsb[:, :], W["ocs"], S["wl"])
        ctx.dma("gpsimd", biasb[:, :], W["bias0"], S["wl"])
    lw = S["wl"].count

    ctx.barrier_from("tensor", S["ev"], S["ve"], S["tr"])
    ctx.wait("tensor", S["wl"], lw)
    ctx.wait("vector", S["wl"], lw)
    ctx.wait("scalar", S["wl"], lw)

    for tt in range(8):
        psl = pp3[:, 0:8]
        for ic in range(NFC):
            ctx.op("tensor", lambda e, ic=ic, tt=tt, psl=psl:
                   e.matmul(psl, big[:, ic * TOK + tt * 128: ic * TOK + (tt + 1) * 128],
                            gw[:, ic * 8:(ic + 1) * 8],
                            start=(ic == 0), stop=(ic == NFC - 1)),
                   S["peog"] if ic == NFC - 1 else None)
        ctx.wait("vector", S["peog"], S["peog"].count)
        ctx.op("vector", lambda e, psl=psl:
               e.reduce_max(rm[:, :], psl, axis=AX.X, negate=True), S["ve"])
        ctx.wait("scalar", S["ve"], S["ve"].count)
        ctx.op("scalar", lambda e, psl=psl:
               e.activation(pexp[:, :], psl, AF.Exp, bias=rm[:, :], scale=1.0),
               S["tr"])
        ctx.wait("vector", S["tr"], S["tr"].count)
        ctx.op("vector", lambda e: e.reduce_sum(rsum[:, :], pexp[:, :], axis=AX.X),
               S["ve"])
        ctx.wait("vector", S["ve"], S["ve"].count)
        ctx.op("vector", lambda e: e.reciprocal(rrcp[:, :], rsum[:, :]), S["ve"])
        ctx.wait("vector", S["ve"], S["ve"].count)
        ctx.op("vector", lambda e:
               e.tensor_scalar(rtile[:, :], pexp[:, :], rrcp[:, :], None, ALU.mult),
               S["ve"])
        ctx.wait("tensor", S["ve"], S["ve"].count)
        ctx.op("tensor", lambda e: e.transpose(pp4[0:8, 0:128], rtile[:, :],
                                               aux["identb"][:, :]), S["peog"])
        ctx.wait("vector", S["peog"], S["peog"].count)
        ctx.op("vector", lambda e, tt=tt:
               e.tensor_copy(rtT[:, tt * 128:(tt + 1) * 128], pp4[0:8, 0:128]),
               S["ve"])
        ctx.wait("tensor", S["ve"], S["ve"].count)

    for fc in range(NFC):
        for h in range(2):
            t0, t1 = h * 512, (h + 1) * 512
            ctx.op("tensor", lambda e, fc=fc, t0=t0, t1=t1:
                   e.matmul(pp3[:, 0:512], icsb[:, fc * 128:(fc + 1) * 128],
                            rtT[:, t0:t1], start=True, stop=True), S["peog"])
            ctx.wait("vector", S["peog"], S["peog"].count)
            a = big[:, fc * TOK + t0: fc * TOK + t1]
            ctx.op("vector", lambda e, a=a:
                   e.tensor_tensor(a, a, pp3[:, 0:512], ALU.mult), S["ve"])
            ctx.wait("tensor", S["ve"], S["ve"].count)
    z_done = S["ve"].count
    if DBG == "rt":
        dbgch = ctx.chan("gpsimd", "dbgch")
        ctx.wait("gpsimd", S["ve"], z_done)
        dbgch.go(aux["dbg"][0:8, :], rtT[:, :])
        ctx.wait("gpsimd", dbgch.sem, dbgch.sem.count)
    if DBG == "z":
        dbgch = ctx.chan("gpsimd", "dbgch")
        ctx.wait("gpsimd", S["ve"], z_done)
        for fc in range(NFC):
            dbgch.go(aux["dbg"][fc * 128:(fc + 1) * 128, :],
                     big[:, fc * TOK:(fc + 1) * TOK])
        ctx.wait("gpsimd", dbgch.sem, dbgch.sem.count)

    os_evt = {}

    def pre_og(gi, og):
        for oi, oc in enumerate(og):
            for h in range(2):
                t0, t1 = h * 512, (h + 1) * 512
                ctx.op("tensor", lambda e, oc=oc, t0=t0, t1=t1:
                       e.matmul(pp3[:, 0:512], ocsb[:, oc * 128:(oc + 1) * 128],
                                rtT[:, t0:t1], start=True, stop=True), S["peog"])
                ctx.wait("vector", S["peog"], S["peog"].count)
                ctx.op("vector", lambda e, oi=oi, t0=t0, t1=t1:
                       e.tensor_copy(osb[:, oi * TOK + t0: oi * TOK + t1],
                                     pp3[:, 0:512]), S["ve"])
                ctx.wait("tensor", S["ve"], S["ve"].count)

    acc_t = {}

    acc2 = aux["acc2"]
    ctx.barrier_from("vector", *[c.sem for c in acc2])

    def evac(oc, ps, og_tgt):
        oi = oc % 3
        slot = oc % 2
        ctx.wait("vector", S["peog"], og_tgt)
        if oc >= 2:
            ctx.wait("vector", acc2[slot].sem, acc_t[oc - 2])
        ctx.op("vector", lambda e, s=slot, p=ps, oi=oi:
               e.tensor_tensor(fstage[s][:, :], p[:, :], osb[:, oi * TOK:(oi + 1) * TOK],
                               ALU.mult))
        ctx.op("vector", lambda e, s=slot, oc=oc:
               e.tensor_scalar(fstage[s][:, :], fstage[s][:, :],
                               biasb[:, oc:oc + 1], None, ALU.add), S["ev"])
        ctx.wait("gpsimd", S["ev"], S["ev"].count)
        ctx.wait("gpsimd", S["xac"], xA_copy)
        acc2[slot].go(xA[oc * 128:(oc + 1) * 128, :], fstage[slot][:, :],
                      accum_op=ALU.add)
        if DBG == "ffn":
            acc2[slot].go(aux["dbg"][oc * 128:(oc + 1) * 128, :], fstage[slot][:, :])
        acc_t[oc] = acc2[slot].sem.count

    emit_mm(ctx, f"f0{aux.get('tag','')}", W["w0T"], NFC, NFC,
            lambda ic: big[:, ic * TOK:(ic + 1) * TOK], evac, S,
            pp=pp, goc=3, icb=4, transform="sign", pre_og=pre_og, wbufs=wbufs,
            tiled=tiled, wchans=wchans)


def emit_ffn1(ctx, big, xA, aux, S, W, xA_copy, loaded=False, wbufs=None,
              wchans=None):
    pp, pp3, fstage = aux["pp"], aux["pp3"], aux["fstage"]
    pk = ctx.sbuf_c("f1_pk", [128, D], U8)
    hi8 = ctx.sbuf_c("f1_hi8", [128, TOK], U8)
    lo8 = ctx.sbuf_c("f1_lo8", [128, TOK], U8)
    hi32 = aux.get("f1_hi32") if "f1_hi32" in aux else ctx.sbuf_c(
        "f1_hi", [128, TOK], BF16)
    lo32 = aux.get("f1_lo32") if "f1_lo32" in aux else ctx.sbuf_c(
        "f1_lo", [128, TOK], BF16)
    scl2 = aux.get("f1_scl2") if "f1_scl2" in aux else ctx.sbuf_c(
        "f1_scl2", [2, D], BF16)
    sel2b = ctx.sbuf_c("f1_sel2", [2, 128], BF16)
    bsc = aux.get("f1_bsc") if "f1_bsc" in aux else ctx.sbuf_c(
        "f1_bsc", [128, TOK], BF16)

    acc2 = aux["acc2"]
    ctx.barrier_from("tensor", S["ev"], S["ve"], S["tr"])
    if not loaded:
        pkc = ctx.chan("gpsimd", "f1_pkc")
        dqc = ctx.chan("sync", "f1_dqc")
        ctx.dma("gpsimd", sel2b[:, :], W["sel2"], S["wl"])
        ctx.wait("tensor", S["wl"], S["wl"].count)
        dq_store = 0
        prev_ve = 0
        prev_pe = 0
        for cp in range(16):
            if cp > 0:
                ctx.wait("gpsimd", S["ve"], prev_ve)
                ctx.wait("gpsimd", S["peog"], prev_pe)
            pkc.go(pk[:, :], W["pwT"][cp * 128:(cp + 1) * 128, :])
            pkc.go(scl2[:, :], W["sclT"][2 * cp:2 * cp + 2, :])
            lw = pkc.sem.count
            ctx.wait("tensor", pkc.sem, lw)
            ctx.wait("vector", pkc.sem, lw)
            for pc in range(4):
                c0 = pc * TOK
                if dq_store > 0:
                    ctx.wait("vector", dqc.sem, dq_store)
                for h in range(2):
                    t0 = c0 + h * 512
                    ctx.op("tensor", lambda e, t0=t0:
                           e.matmul(pp[0][:, (t0 % TOK):(t0 % TOK) + 512],
                                    sel2b[:, :], scl2[:, t0:t0 + 512],
                                    start=True, stop=True),
                           S["peog"] if h == 1 else None)
                ctx.wait("vector", S["peog"], S["peog"].count)
                ctx.op("vector", lambda e: e.tensor_copy(bsc[:, :], pp[0][:, 0:TOK]))
                ctx.op("vector", lambda e, c0=c0:
                       e.tensor_scalar(hi8[:, :], pk[:, c0:c0 + TOK], 4, None,
                                       ALU.logical_shift_right))
                ctx.op("vector", lambda e, c0=c0:
                       e.tensor_scalar(lo8[:, :], pk[:, c0:c0 + TOK], 15, None,
                                       ALU.bitwise_and))
                ctx.op("vector", lambda e:
                       e.scalar_tensor_tensor(hi32[:, :], hi8[:, :], -8.0, bsc[:, :],
                                              ALU.add, ALU.mult))
                ctx.op("vector", lambda e:
                       e.scalar_tensor_tensor(lo32[:, :], lo8[:, :], -8.0, bsc[:, :],
                                              ALU.add, ALU.mult), S["ve"])
                ve_tgt = S["ve"].count
                ctx.wait("tensor", S["ve"], ve_tgt)
                ctx.wait("sync", S["ve"], ve_tgt)
                for par, bufv in ((0, hi32), (1, lo32)):
                    for half in range(2):
                        blk = 2 * cp + half
                        dst = W["w1T"][blk * 128:(blk + 1) * 128, pc * TOK:(pc + 1) * TOK]
                        dst = dst.rearrange("(r two) c -> two r c", two=2)[par]
                        dqc.go(dst, bufv[half * 64:(half + 1) * 64, :])
                dq_store = dqc.sem.count
            prev_ve = S["ve"].count
            prev_pe = S["peog"].count
        ctx.wait("gpsimd", dqc.sem, dq_store)

    acc_t = {}
    ctx.barrier_from("vector", *[c.sem for c in acc2])

    def evac(oc, ps, og_tgt):
        slot = oc % 2
        ctx.wait("vector", S["peog"], og_tgt)
        if oc >= 2:
            ctx.wait("vector", acc2[slot].sem, acc_t[oc - 2])
        ctx.op("vector", lambda e, s=slot, p=ps:
               e.tensor_copy(fstage[s][:, :], p[:, :]), S["ev"])
        ctx.wait("gpsimd", S["ev"], S["ev"].count)
        ctx.wait("gpsimd", S["xac"], xA_copy)
        acc2[slot].go(xA[oc * 128:(oc + 1) * 128, :], fstage[slot][:, :],
                      accum_op=ALU.add)
        acc_t[oc] = acc2[slot].sem.count

    emit_mm(ctx, f"f1{aux.get('tag','')}", W["w1T"], NFC, NFC,
            lambda ic: big[:, ic * TOK:(ic + 1) * TOK], evac, S,
            pp=pp, goc=3, icb=4, wbufs=wbufs, wchans=wchans)


ROPE_THETA = 10000.0


def host_rope_tables(core):
    inv = 1.0 / ROPE_THETA ** (np.arange(0, 128, 2, dtype=np.float64) / 128.0)
    pos = np.arange(512 * core, 512 * core + 512, dtype=np.float64)
    fr = pos[None, :] * np.concatenate([inv, inv])[:, None]
    c = np.cos(fr)
    s = np.sin(fr)
    sgn = np.where(np.arange(128) < 64, -1.0, 1.0)[:, None]
    ct = np.concatenate([c, c], axis=1).astype(np.float32)
    st = (np.concatenate([s, s], axis=1) * sgn).astype(np.float32)
    return np.ascontiguousarray(ct), np.ascontiguousarray(st)


def host_xT(x, core):
    sl = np.concatenate([x[0, 512 * core:512 * core + 512],
                         x[1, 512 * core:512 * core + 512]], axis=0)
    return np.ascontiguousarray(sl.T.astype(np.float32, copy=False))


def host_unxT(xTs):
    out = np.empty((2, 4096, 4096), np.float32)
    for c, xT in enumerate(xTs):
        t = np.asarray(xT).T
        out[0, 512 * c:512 * c + 512] = t[0:512]
        out[1, 512 * c:512 * c + 512] = t[512:1024]
    return out


def host_cvec():
    v = np.zeros((128, 8), np.float32)
    v[:, CV_EPS] = 1e-5
    v[:, CV_SHIFT01] = SOFTMAX_SHIFT01
    v[:, CV_SHIFT3] = SOFTMAX_SHIFT3
    return v


def host_A0_inputs(ins, core, xT):
    wqkvT = np.ascontiguousarray(np.concatenate(
        [np.asarray(ins["L0_attn_Wq"]).T, np.asarray(ins["L0_attn_Wk"]).T,
         np.asarray(ins["L0_attn_Wv"]).T], axis=1).astype(np.float32, copy=False))
    ct, st = host_rope_tables(core)
    return dict(
        xT=xT, wqkvT=wqkvT, ctab=ct, stab=st, cvec=host_cvec(),
        ident=np.eye(128, dtype=np.float32),
        w0T=np.ascontiguousarray(np.asarray(ins["L0_ffn_weight"]).T),
        bias0=np.ascontiguousarray(np.asarray(ins["L0_ffn_bias"]).reshape(NFC, 128).T),
        gwT=np.ascontiguousarray(np.asarray(ins["L0_ffn_gate_w"]).T),
        ics=np.ascontiguousarray(np.asarray(ins["L0_ffn_in_channel_scale"])),
        ocs=np.ascontiguousarray(np.asarray(ins["L0_ffn_out_channel_scale"])),
    )


def host_A1_inputs(ins, core, xAp, aoT):
    wqkvT = np.ascontiguousarray(np.concatenate(
        [np.asarray(ins["L1_attn_Wq"]).T, np.asarray(ins["L1_attn_Wk"]).T,
         np.asarray(ins["L1_attn_Wv"]).T], axis=1).astype(np.float32, copy=False))
    ct, st = host_rope_tables(core)
    pw = np.asarray(ins["L1_ffn_packed_weight"])
    pb = np.ascontiguousarray(pw).view(np.int8)[..., 0::4].astype(np.int8, copy=True)
    pb = pb.view(np.uint8)
    pwT = np.ascontiguousarray(pb.transpose(1, 2, 0)).reshape(2048, 4096)
    scl = np.asarray(ins["L1_ffn_scales"]).reshape(4096, 32)
    sclT = np.ascontiguousarray(scl.T.astype(np.float32, copy=False))
    sel2 = np.zeros((2, 128), np.float32)
    sel2[0, 0:64] = 1.0
    sel2[1, 64:128] = 1.0
    return dict(
        xAp=xAp, aoT=aoT,
        woT=np.ascontiguousarray(np.asarray(ins["L0_attn_Wo"]).T),
        wqkvT=wqkvT, ctab=ct, stab=st, cvec=host_cvec(),
        ident=np.eye(128, dtype=np.float32),
        pwT=pwT, sclT=sclT, sel2=sel2,
    )


def host_reshard_qkv(qkvs, core):
    allq = np.concatenate([np.asarray(q)[None] for q in qkvs], axis=0)
    qm = allq[:, 512 * core:512 * core + 512, :]
    qm = np.ascontiguousarray(qm.transpose(1, 0, 2).reshape(512, 8192))
    km = allq[:, 4096 + 128 * core:4096 + 128 * (core + 1), :]
    km = np.ascontiguousarray(km.transpose(1, 0, 2).reshape(128, 8192))
    vm = allq[:, 5120 + 128 * core:5120 + 128 * (core + 1), :]
    vm = vm.transpose(1, 0, 2).reshape(128, 8192)
    vtok = np.ascontiguousarray(vm.T)
    return qm, km, vtok


def build_B01():
    ctx = Ctx()
    qmine = ctx.dram("qmine", [512, 8192], BF16, kind="ExternalInput")
    kmine = ctx.dram("kmine", [128, 8192], BF16, kind="ExternalInput")
    vtok = ctx.dram("vtok", [8192, 128], BF16, kind="ExternalInput")
    trimask = ctx.dram("trimask", [128, 896], F32, kind="ExternalInput")
    cvec = ctx.dram("cvec", [128, 8], F32, kind="ExternalInput")
    aomine = ctx.dram("aomine", [512, 8192], BF16, kind="ExternalOutput")

    S = {k: ctx.sem(k) for k in ("pe", "tr", "ve", "kv")}
    qc = [ctx.chan("gpsimd", "qc0"), ctx.chan("gpsimd", "qc1")]
    stc = [ctx.chan("sync", "stc0"), ctx.chan("sync", "stc1")]

    kb_s = ctx.sbuf("kb_s", [128, 8192], BF16)
    vt_s = ctx.sbuf("vt_s", [128, 8192], BF16)
    qh = [ctx.sbuf(f"qh{i}", [128, 8192], BF16) for i in range(2)]
    tri = ctx.sbuf("tri", [128, 896], BF16)
    cv = ctx.sbuf("cv", [128, 8], F32)
    onesb = ctx.sbuf("onesb", [128, 1], BF16)
    onesf = ctx.sbuf("onesf", [1, 128], F32)
    ebuf = [ctx.sbuf(f"ebuf{i}", [128, 1536], BF16) for i in range(2)]
    esum = ctx.sbuf("esum", [128, 512], BF16)
    rd_row = ctx.sbuf("rd_row", [1, 512], F32)
    bc_sb = ctx.sbuf("bc_sb", [128, 512], F32)
    aost = [ctx.sbuf(f"aost{i}", [128, 512], BF16) for i in range(2)]

    ps_sc = [ctx.psum(f"ps_sc{i}", [128, 1536], F32) for i in range(2)]
    ps_av = ctx.psum("ps_av", [128, 512], F32)
    ps_m = ctx.psum("ps_m", [128, 512], F32)

    ctx.op("gpsimd", lambda e: e.memset(onesb[:, :], 1.0))
    ctx.op("gpsimd", lambda e: e.memset(onesf[:, :], 1.0))
    ctx.dma("gpsimd", kb_s[:, :], kmine, S["kv"])
    ctx.dma("gpsimd", vt_s[:, :].rearrange("p (kb d) -> p kb d", d=128),
            vtok.rearrange("(kb p) d -> p kb d", p=128), S["kv"])
    ctx.dma("gpsimd", tri[:, :], trimask, S["kv"])
    ctx.dma("gpsimd", cv[:, :], cvec, S["kv"])
    kv_tgt = S["kv"].count
    for eng in ("tensor", "scalar", "vector"):
        ctx.wait(eng, S["kv"], kv_tgt)

    qload = {}
    for h in range(4):
        qslot = h % 2
        if h >= 2:
            rs, rv = qload[h - 2]
            ctx.wait("gpsimd", rs, rv)
        qc[qslot].go(qh[qslot][:, :], qmine[h * 128:(h + 1) * 128, :])
        q_tgt = qc[qslot].sem.count
        ctx.wait("tensor", qc[qslot].sem, q_tgt)
        for b in range(2):
            for s in range(8):
                qcol = s * 1024 + b * 512
                qtile = qh[qslot][:, qcol:qcol + 512]
                nblk = 4 * (s + 1)
                kcols = [(j // 4) * 1024 + b * 512 + (j % 4) * 128
                         for j in range(nblk)]
                groups = [list(range(g, min(g + 3, nblk))) for g in range(0, nblk, 3)]
                grp_tr = {}
                grp_ve = {}
                for gi, grp in enumerate(groups):
                    scp = ps_sc[gi % 2]
                    if gi >= 2:
                        ctx.wait("tensor", S["tr"], grp_tr[gi - 2])
                    for i, j in enumerate(grp):
                        ctx.op("tensor", lambda e, scp=scp, i=i, kc=kcols[j], qtile=qtile:
                               e.matmul(scp[:, i * 512:(i + 1) * 512],
                                        kb_s[:, kc:kc + 128], qtile,
                                        start=True, stop=True),
                               S["pe"] if i == len(grp) - 1 else None)
                    ctx.wait("scalar", S["pe"], S["pe"].count)
                    if gi >= 2:
                        ctx.wait("scalar", S["ve"], grp_ve[gi - 2])
                    eb = ebuf[gi % 2]
                    ng = len(grp)
                    ctx.op("scalar", lambda e, scp=scp, eb=eb, ng=ng:
                           e.activation(eb[:, 0:ng * 512], scp[:, 0:ng * 512],
                                        AF.Exp, bias=cv[:, CV_SHIFT01:CV_SHIFT01 + 1],
                                        scale=0.08838834764831845), S["tr"])
                    grp_tr[gi] = S["tr"].count
                    ctx.wait("vector", S["tr"], S["tr"].count)
                    for i, j in enumerate(grp):
                        esl = eb[:, i * 512:(i + 1) * 512]
                        jp = j - (nblk - 4)
                        if jp >= 0:
                            ctx.op("vector", lambda e, esl=esl, jp=jp:
                                   e.tensor_tensor(esl, esl,
                                                   tri[:, (3 - jp) * 128:(3 - jp) * 128 + 512],
                                                   ALU.mult))
                        if j == 0:
                            ctx.op("vector", lambda e, esl=esl:
                                   e.tensor_copy(esum[:, :], esl))
                        else:
                            ctx.op("vector", lambda e, esl=esl:
                                   e.tensor_tensor(esum[:, :], esum[:, :], esl, ALU.add))
                    ctx.op("vector", lambda e: e.engine_nop(), S["ve"])
                    grp_ve[gi] = S["ve"].count
                    ctx.wait("tensor", S["ve"], grp_ve[gi])
                    for i, j in enumerate(grp):
                        kbv = kcols[j] // 128
                        ctx.op("tensor", lambda e, i=i, eb=eb, kbv=kbv,
                               fs=(j == 0), l=(j == nblk - 1):
                               e.matmul(ps_av[:, :], vt_s[:, kbv * 128:(kbv + 1) * 128],
                                        eb[:, i * 512:(i + 1) * 512],
                                        start=fs, stop=l),
                               S["pe"] if j == nblk - 1 else None)
                av_done = S["pe"].count
                ctx.op("tensor", lambda e: e.matmul(ps_m[0:1, :], onesb[:, :],
                                                    esum[:, :], start=True, stop=True),
                       S["pe"])
                ctx.wait("vector", S["pe"], S["pe"].count)
                ctx.op("vector", lambda e: e.reciprocal(rd_row[:, :], ps_m[0:1, :]),
                       S["ve"])
                ctx.wait("tensor", S["ve"], S["ve"].count)
                ctx.op("tensor", lambda e: e.matmul(ps_m[:, :], onesf[:, :],
                                                    rd_row[:, :], start=True, stop=True),
                       S["pe"])
                ctx.wait("vector", S["pe"], S["pe"].count)
                aslot = (h * 16 + b * 8 + s) % 2
                ctx.wait("vector", stc[aslot].sem, stc[aslot].sem.count)
                ctx.op("vector", lambda e: e.tensor_copy(bc_sb[:, :], ps_m[:, :]))
                ctx.op("vector", lambda e, aslot=aslot:
                       e.tensor_tensor(aost[aslot][:, :], ps_av[:, :], bc_sb[:, :],
                                       ALU.mult), S["ve"])
                ev_t = S["ve"].count
                ctx.wait("tensor", S["ve"], ev_t)
                ctx.wait("sync", S["ve"], ev_t)
                stc[aslot].go(aomine[h * 128:(h + 1) * 128, qcol:qcol + 512],
                              aost[aslot][:, :])
        qload[h] = (S["pe"], S["pe"].count)
    for chn in stc:
        ctx.wait("gpsimd", chn.sem, chn.sem.count)
    return ctx


def host_trimask():
    t = (np.arange(128)[:, None] <= np.arange(896)[None, :] - 384)
    return np.ascontiguousarray(t.astype(np.float32))


def host_B_inputs(qkvs, core):
    qm, km, vtokm = host_reshard_qkv(qkvs, core)
    return dict(qmine=qm, kmine=km, vtok=vtokm, trimask=host_trimask(),
                cvec=host_cvec())


def host_reshard_ao(aos, core):
    cols = np.concatenate([np.arange(core * 1024, core * 1024 + 512),
                           np.arange(core * 1024 + 512, core * 1024 + 1024)])
    out = np.concatenate([np.asarray(a)[:, cols] for a in aos], axis=0)
    return np.ascontiguousarray(out)


def build_C2A3():
    ctx = Ctx()
    xAp = ctx.dram("xAp", [D, TOK], F32, kind="ExternalInput")
    aoT = ctx.dram("aoT", [D, TOK], BF16, kind="ExternalInput")
    woT = ctx.dram("woT", [D, D], F32, kind="ExternalInput")
    cvec = ctx.dram("cvec", [128, 8], F32, kind="ExternalInput")
    ident = ctx.dram("ident", [128, 128], F32, kind="ExternalInput")
    mask8 = ctx.dram("mask8", [D, D], U8, kind="ExternalInput")
    w2aT = ctx.dram("w2aT", [D, D], F32, kind="ExternalInput")
    bias2 = ctx.dram("bias2", [128, NFC], F32, kind="ExternalInput")
    indT = ctx.dram("indT", [128, D], F32, kind="ExternalInput")
    rep4 = ctx.dram("rep4", [4, 128], F32, kind="ExternalInput")
    g4 = ctx.dram("g4", [128, 4], F32, kind="ExternalInput")
    w1qT = ctx.dram("w1qT", [128, 86 * 1024], F32, kind="ExternalInput")
    w3qT = ctx.dram("w3qT", [128, 86 * 1024], F32, kind="ExternalInput")
    w2qT = ctx.dram("w2qT", [2752, D], F32, kind="ExternalInput")
    a1T = ctx.dram("a1T", [D, 2], F32, kind="ExternalInput")
    a3T = ctx.dram("a3T", [D, 2], F32, kind="ExternalInput")
    a2Tp = ctx.dram("a2Tp", [128, 172], F32, kind="ExternalInput")
    b1T = ctx.dram("b1T", [2, 11008], F32, kind="ExternalInput")
    b3T = ctx.dram("b3T", [2, 11008], F32, kind="ExternalInput")
    b2T = ctx.dram("b2T", [2, D], F32, kind="ExternalInput")
    wqkv3 = ctx.dram("wqkv3", [D, 3072], F32, kind="ExternalInput")
    bqkv3 = ctx.dram("bqkv3", [128, 24], F32, kind="ExternalInput")
    kern8 = ctx.dram("kern8", [512, D], I8, kind="ExternalInput")
    qkv3T = ctx.dram("qkv3T", [3072, TOK], BF16, kind="ExternalOutput")
    xA3 = ctx.dram("xA3", [D, TOK], F32, kind="ExternalOutput")
    xsrc = ctx.dram("x", [D, TOK], F32)
    x2 = ctx.dram("x2", [D, TOK], F32)
    bmd = ctx.dram("bmd", [128, 128], F32)
    bxd = ctx.dram("bxd", [128, D], BF16)

    S = {k: ctx.sem(k) for k in
         ("wl", "xl", "pe", "peog", "ev", "tr", "ve", "ac", "st", "pe2",
          "tb", "ao", "xc", "xac", "ms", "ve2")}
    acc2 = [ctx.chan("gpsimd", "acc0"), ctx.chan("gpsimd", "acc1")]
    stc = [ctx.chan("sync", "stc0"), ctx.chan("sync", "stc1")]
    mbch = ctx.chan("gpsimd", "mbch")
    bmch = ctx.chan("sync", "bmch")
    b4ch = ctx.chan("gpsimd", "b4ch")
    lwch = [ctx.chan("gpsimd", "lw0"), ctx.chan("gpsimd", "lw1")]

    pp = [ctx.psum(f"pp{i}", [128, TOK], F32) for i in range(3)]
    pp3 = ctx.psum("pp3", [128, 512], F32)
    pm2a = pp3[0:2, 0:512]
    pm2b = pp[2][0:2, 0:512]

    big = ctx.sbuf("big", [128, NFC * TOK], BF16)
    hf = big[:, 0:22 * TOK]
    onesb = ctx.sbuf("onesb", [128, 1], BF16)
    onesf = ctx.sbuf("onesf", [1, 128], F32)
    cv = ctx.sbuf("cv", [128, 8], F32)
    identb = ctx.sbuf("identb", [128, 128], BF16)
    fstage = [ctx.sbuf(f"fstage{i}", [128, TOK], F32) for i in range(2)]
    stage = [ctx.sbuf(f"stage{i}", [128, TOK], BF16) for i in range(2)]
    aux = dict(onesb=onesb, onesf=onesf, cv=cv, pp=pp, identb=identb)

    mb = ctx.sbuf("mb", [128, D], BF16)
    cpl = ctx.sbuf("cpl", [128, 128], F32)
    rtmp = ctx.sbuf("rtmp", [4, 128], F32)
    bmf = ctx.sbuf("bmf", [128, 128], F32)
    binb = ctx.sbuf("binb", [128, 128], BF16)
    indsb = ctx.sbuf("indsb", [128, D], BF16)
    rep4b = ctx.sbuf("rep4b", [4, 128], BF16)
    g4b = ctx.sbuf("g4b", [128, 4], BF16)
    b4sb = ctx.sbuf("b4sb", [4, D], BF16)
    wmb = [ctx.sbuf(f"wmb{i}", [128, 384], BF16) for i in range(2)]
    biasb = ctx.sbuf("c2_bias", [128, NFC], F32)

    xf4 = ctx.sbuf("xf4", [128, 8 * TOK], BF16)
    a1sb = ctx.sbuf("a1sb", [128, 64], BF16)
    a3sb = ctx.sbuf("a3sb", [128, 64], BF16)
    a2sb = ctx.sbuf("a2sb", [128, 172], BF16)
    b1sb = ctx.sbuf("b1sb", [2, 11008 // 64], BF16)
    b3sb = ctx.sbuf("b3sb", [2, 11008 // 64], BF16)
    b2sb = ctx.sbuf("b2sb", [2, D], BF16)
    mid1b = ctx.sbuf("mid1b", [2, TOK], BF16)
    mid3b = ctx.sbuf("mid3b", [2, TOK], BF16)
    mid2b = ctx.sbuf("mid2b", [2, TOK], BF16)
    gsb = ctx.sbuf("gsb", [128, TOK], BF16)
    usb = ctx.sbuf("usb", [128, TOK], BF16)
    hsb = ctx.sbuf("hsb", [128, TOK], BF16)

    xf8 = ctx.sbuf("xf8", [128, 4 * TOK], BF16)
    bq3 = ctx.sbuf("bq3", [128, 24], F32)

    ctx.op("gpsimd", lambda e: e.memset(onesb[:, :], 1.0))
    ctx.op("gpsimd", lambda e: e.memset(b4sb[:, :], 0.0))
    ctx.op("gpsimd", lambda e: e.memset(onesf[:, :], 1.0), S["pe2"], 1)
    ctx.wait("tensor", S["pe2"], 1)
    for dst, srcp in ((cv, cvec), (identb, ident), (indsb, indT), (rep4b, rep4),
                     (g4b, g4), (a1sb, None), (a3sb, None), (a2sb, a2Tp),
                     (b2sb, b2T), (biasb, bias2), (bq3, bqkv3)):
        if dst is a1sb:
            ctx.dma("gpsimd", a1sb[:, :].rearrange("p (c r) -> p c r", r=2),
                    a1T.rearrange("(c p) r -> p c r", p=128), S["tb"])
        elif dst is a3sb:
            ctx.dma("gpsimd", a3sb[:, :].rearrange("p (c r) -> p c r", r=2),
                    a3T.rearrange("(c p) r -> p c r", p=128), S["tb"])
        else:
            ctx.dma("gpsimd", dst[:, :], srcp, S["tb"])
    tabs = S["tb"].count
    for eng in ("tensor", "vector", "scalar"):
        ctx.wait(eng, S["tb"], tabs)

    ctx.dma("gpsimd", xsrc, xAp, S["xc"])
    xcopy_tgt = S["xc"].count
    for fc in range(NFC):
        ctx.dma("sync", big[:, fc * TOK:(fc + 1) * TOK],
                aoT[fc * 128:(fc + 1) * 128, :], S["ao"])
    ctx.wait("tensor", S["ao"], S["ao"].count)

    wo_acc = {}

    def wo_evac(oc, ps, og_tgt):
        ctx.wait("vector", S["peog"], og_tgt)
        slot = oc % 2
        if oc >= 2:
            ctx.wait("vector", acc2[slot].sem, wo_acc[oc - 2])
        ctx.op("vector", lambda e, s=slot, p=ps:
               e.tensor_copy(fstage[s][:, :], p[:, :]), S["ev"])
        ctx.wait("gpsimd", S["ev"], S["ev"].count)
        ctx.wait("gpsimd", S["xc"], xcopy_tgt)
        acc2[slot].go(xsrc[oc * 128:(oc + 1) * 128, :],
                      fstage[slot][:, :], accum_op=ALU.add)
        wo_acc[oc] = acc2[slot].sem.count
    emit_mm(ctx, "wo", woT, NFC, NFC,
            lambda ic: big[:, ic * TOK:(ic + 1) * TOK], wo_evac,
            S, pp=pp, goc=3, icb=8)
    for _c in acc2:
        ctx.wait("gpsimd", _c.sem, _c.sem.count)

    sems_ln = dict(xl=S["xl"], pe=S["pe"], ve=S["ve"], tr=S["tr"], ac=S["ve"])
    ctx.barrier_from("tensor", S["ev"], S["ve"])
    emit_ln(ctx, "ln2", xsrc, big, aux, sems_ln)
    ctx.wait("gpsimd", S["ve"], S["ve"].count)
    ctx.dma("gpsimd", x2, xsrc, S["xac"])
    x2copy = S["xac"].count

    ctx.barrier_from("tensor", S["ve"])
    ctx.barrier_from("scalar", S["ve"])
    mb_dve = 0
    for rc in range(NFC):
        if rc > 0:
            ctx.wait("gpsimd", S["ve2"], mb_dve)
        mbch.go(mb[:, :], mask8[rc * 128:(rc + 1) * 128, :])
        ctx.wait("vector", mbch.sem, mbch.sem.count)
        ctx.op("vector", lambda e: e.tensor_reduce(
            cpl[:, :].rearrange("p (c o) -> p c o", o=1),
            mb[:, :].rearrange("p (c w) -> p c w", w=32),
            axis=AX.X, op=ALU.add), S["ve2"])
        mb_dve = S["ve2"].count
        ctx.wait("tensor", S["ve2"], mb_dve)
        ctx.op("vector", lambda e: e.tensor_copy(mb[:, 0:128], cpl[:, :]), S["ve2"])
        ctx.wait("tensor", S["ve2"], S["ve2"].count)
        ctx.op("tensor", lambda e: e.matmul(pp3[0:4, 0:512], g4b[:, :],
                                            mb[:, 0:512], start=True, stop=True),
               S["peog"])
        ctx.wait("vector", S["peog"], S["peog"].count)
        if rc > 0:
            ctx.wait("vector", bmch.sem, bmch.sem.count)
        ctx.op("vector", lambda e: e.tensor_copy(rtmp[:, :], pp3[0:4, 0:128]),
               S["ve2"])
        ctx.wait("sync", S["ve2"], S["ve2"].count)
        bmch.go(bmd[4 * rc:4 * rc + 4, :], rtmp[:, :])
    ctx.wait("gpsimd", bmch.sem, bmch.sem.count)
    ctx.dma("gpsimd", bmf[:, :], bmd, S["ms"])
    ctx.wait("scalar", S["ms"], S["ms"].count)
    ctx.op("scalar", lambda e: e.activation(binb[:, :], bmf[:, :], AF.Sign), S["tr"])
    bin_tr = S["tr"].count
    ctx.wait("tensor", S["tr"], bin_tr)
    for hblk in range(8):
        ctx.op("tensor", lambda e, hblk=hblk: e.matmul(
            pp3[:, 0:512], binb[:, :], indsb[:, hblk * 512:(hblk + 1) * 512],
            start=True, stop=True), S["peog"])
        ctx.wait("scalar", S["peog"], S["peog"].count)
        ctx.op("scalar", lambda e, hblk=hblk: e.activation(
            stage[hblk % 2][:, 0:512], pp3[:, 0:512], AF.Identity,
            bias=0.0, scale=1.0), S["tr"])
        ctx.wait("sync", S["tr"], S["tr"].count)
        stc[hblk % 2].go(bxd[:, hblk * 512:(hblk + 1) * 512],
                         stage[hblk % 2][:, 0:512])
        ctx.wait("tensor", stc[hblk % 2].sem, stc[hblk % 2].sem.count)
    for chn in stc:
        ctx.wait("gpsimd", chn.sem, chn.sem.count)

    n_out = NFC
    ogs = [list(range(g, min(g + 3, n_out))) for g in range(0, n_out, 3)]
    ctx.barrier_from("vector", *[c.sem for c in acc2])
    at_acc = {}
    lidx = 0
    lrel = {}
    ev_prev = []
    for gi, og in enumerate(ogs):
        ncols = len(og) * 128
        col0 = og[0] * 128
        for ic in range(NFC):
            slot = lidx % 2
            if lidx >= 2:
                rs, rv = lrel[lidx - 2]
                ctx.wait("gpsimd", rs, rv)
            lwch[slot].go(stage[slot][:, 0:ncols],
                          w2aT[ic * 128:(ic + 1) * 128, col0:col0 + ncols])
            wl_tgt = lwch[slot].sem.count
            b4ch.go(b4sb[:, 0:ncols], bxd[4 * ic:4 * ic + 4, col0:col0 + ncols])
            b4t = b4ch.sem.count
            ctx.wait("tensor", b4ch.sem, b4t)
            if gi > 0 and ic == 0:
                ctx.wait("tensor", S["ev"], ev_prev[gi - 1])
            ctx.op("tensor", lambda e: e.matmul(
                pp3[:, 0:512], rep4b[:, :], b4sb[:, 0:512], start=True, stop=True),
                S["peog"])
            ctx.wait("vector", S["peog"], S["peog"].count)
            ctx.wait("vector", lwch[slot].sem, wl_tgt)
            ctx.op("vector", lambda e, s=slot, n=ncols: e.tensor_tensor(
                wmb[s][:, 0:n], stage[s][:, 0:n], pp3[:, 0:n], ALU.mult),
                S["ve"])
            wm_t = S["ve"].count
            ctx.wait("tensor", S["ve"], wm_t)
            for oi, oc in enumerate(og):
                for h in range(2):
                    t0, t1 = h * 512, (h + 1) * 512
                    last = (ic == NFC - 1)
                    sem1 = None
                    if oi == len(og) - 1 and h == 1:
                        sem1 = S["peog"] if last else S["pe"]
                    ctx.op("tensor", lambda e, oi=oi, s=slot, o0=oi * 128,
                           t0=t0, t1=t1, fs=(ic == 0), l=last:
                           e.matmul(pp[oi][:, t0:t1], wmb[s][:, o0:o0 + 128],
                                    big[:, ic * TOK + t0:ic * TOK + t1],
                                    start=fs, stop=l), sem1)
            if (True):
                lrel[lidx] = (S["peog"] if ic == NFC - 1 else S["pe"],
                              (S["peog"] if ic == NFC - 1 else S["pe"]).count)
            lidx += 1
        og_tgt = S["peog"].count
        for oi, oc in enumerate(og):
            slot2 = oc % 2
            ctx.wait("vector", S["peog"], og_tgt)
            if oc >= 2:
                ctx.wait("vector", acc2[slot2].sem, at_acc[oc - 2])
            ctx.op("vector", lambda e, s=slot2, oi=oi:
                   e.tensor_copy(fstage[s][:, :], pp[oi][:, :]))
            ctx.op("vector", lambda e, s=slot2, oc=oc:
                   e.tensor_scalar(fstage[s][:, :], fstage[s][:, :],
                                   biasb[:, oc:oc + 1], None, ALU.add), S["ev"])
            ctx.wait("gpsimd", S["ev"], S["ev"].count)
            ctx.wait("gpsimd", S["xac"], x2copy)
            acc2[slot2].go(x2[oc * 128:(oc + 1) * 128, :], fstage[slot2][:, :],
                           accum_op=ALU.add)
            at_acc[oc] = acc2[slot2].sem.count
        ev_prev.append(S["ev"].count)
    attn2_done_pe = S["peog"].count
    attn2_done_ve = S["ev"].count
    return _c2a3_part2(ctx, locals())


def _c2a3_part2(ctx, L):
    S = L["S"]
    acc2, stc, lwch, b4ch = L["acc2"], L["stc"], L["lwch"], L["b4ch"]
    pp, pp3, pm2a, pm2b = L["pp"], L["pp3"], L["pm2a"], L["pm2b"]
    big, hf, xf4, xf8 = L["big"], L["hf"], L["xf4"], L["xf8"]
    fstage, stage = L["fstage"], L["stage"]
    a1sb, a3sb, a2sb = L["a1sb"], L["a3sb"], L["a2sb"]
    b1sb, b3sb, b2sb = L["b1sb"], L["b3sb"], L["b2sb"]
    mid1b, mid3b, mid2b = L["mid1b"], L["mid3b"], L["mid2b"]
    gsb, usb, hsb = L["gsb"], L["usb"], L["hsb"]
    aux, cv, bq3 = L["aux"], L["cv"], L["bq3"]
    w1qT, w3qT, w2qT = L["w1qT"], L["w3qT"], L["w2qT"]
    b1T, b3T = L["b1T"], L["b3T"]
    wqkv3, kern8 = L["wqkv3"], L["kern8"]
    qkv3T, xA3, x2 = L["qkv3T"], L["xA3"], L["x2"]
    x2copy = L["x2copy"]

    ctx.barrier_from("vector", S["peog"])
    for f in range(8):
        d = xf4[:, f * TOK:(f + 1) * TOK]
        ctx.op("vector", lambda e, d=d, f=f: e.tensor_tensor(
            d, big[:, f * TOK:(f + 1) * TOK], big[:, (f + 8) * TOK:(f + 9) * TOK],
            ALU.add))
        ctx.op("vector", lambda e, d=d, f=f: e.tensor_tensor(
            d, d, big[:, (f + 16) * TOK:(f + 17) * TOK], ALU.add))
        ctx.op("vector", lambda e, d=d, f=f: e.tensor_tensor(
            d, d, big[:, (f + 24) * TOK:(f + 25) * TOK], ALU.add),
            S["ve"] if f == 7 else None)
    xf4_done = S["ve"].count

    ctx.barrier_from("tensor", S["ev"], S["ve"])
    for mb_, aT, ppx in ((mid1b, a1sb, pp[0]), (mid3b, a3sb, pp[1])):
        for ic in range(NFC):
            for h2 in range(2):
                t0, t1 = h2 * 512, (h2 + 1) * 512
                ctx.op("tensor", lambda e, aT=aT, ic=ic, t0=t0, t1=t1, ppx=ppx:
                       e.matmul(ppx[0:2, t0:t1], aT[:, 2 * ic:2 * ic + 2],
                                big[:, ic * TOK + t0:ic * TOK + t1],
                                start=(ic == 0), stop=(ic == NFC - 1)),
                       S["peog"] if (ic == NFC - 1 and h2 == 1) else None)
        ctx.wait("vector", S["peog"], S["peog"].count)
        ctx.op("vector", lambda e, mb_=mb_, ppx=ppx: e.tensor_scalar(
            mb_[:, :], ppx[0:2, :], 0.5, None, ALU.mult), S["ve"])
    mids_done = S["ve"].count
    ctx.wait("tensor", S["ve"], mids_done)

    w1b = [ctx.sbuf(f"w1b{i}", [128, 1024], BF16) for i in range(2)]
    w3b = [ctx.sbuf(f"w3b{i}", [128, 1024], BF16) for i in range(2)]
    hstmp = L["mb"][:, 0:TOK]
    NOC = 86
    hw_written = set()
    yrel = {}
    hfold_ve = {}
    for oc in range(NOC):
        slot = oc % 2
        if oc >= 2:
            rs, rv = yrel[oc - 2]
            ctx.wait("gpsimd", rs, rv)
        lwch[slot].go(w1b[slot][:, :].rearrange("p (i c) -> p i c", i=8),
                      w1qT[:, oc * 128:(oc + 1) * 128]
                      .rearrange("(i p) c -> p i c", p=128))
        lwch[slot].go(w3b[slot][:, :].rearrange("p (i c) -> p i c", i=8),
                      w3qT[:, oc * 128:(oc + 1) * 128]
                      .rearrange("(i p) c -> p i c", p=128))
        w_tgt = lwch[slot].sem.count
        b4ch.go(b1sb[:, 0:128], b1T[:, oc * 128:(oc + 1) * 128])
        b4ch.go(b3sb[:, 0:128], b3T[:, oc * 128:(oc + 1) * 128])
        b_tgt = b4ch.sem.count
        ctx.wait("tensor", lwch[slot].sem, w_tgt)
        ctx.wait("tensor", b4ch.sem, b_tgt)
        if oc >= 1:
            ctx.wait("tensor", S["tr"], sil_t[0])
            ctx.wait("tensor", S["ve"], sil_t[1])
        for wbuf, ppx, midb, bsb in ((w1b[slot], pp[0], mid1b, b1sb),
                                     (w3b[slot], pp[1], mid3b, b3sb)):
            for ic in range(8):
                for h2 in range(2):
                    t0, t1 = h2 * 512, (h2 + 1) * 512
                    ctx.op("tensor", lambda e, w=wbuf, ic=ic, t0=t0, t1=t1, ppx=ppx:
                           e.matmul(ppx[:, t0:t1], w[:, ic * 128:(ic + 1) * 128],
                                    xf4[:, ic * TOK + t0:ic * TOK + t1],
                                    start=(ic == 0), stop=False),
                           S["pe"] if (ic == 7 and h2 == 1) else None)
            for h2 in range(2):
                t0, t1 = h2 * 512, (h2 + 1) * 512
                ctx.op("tensor", lambda e, b=bsb, m=midb, t0=t0, t1=t1, ppx=ppx:
                       e.matmul(ppx[:, t0:t1], b[:, 0:128], m[:, t0:t1],
                                start=False, stop=True),
                       S["peog"] if h2 == 1 else None)
        yrel[oc] = (S["pe"], S["pe"].count)
        y_done = S["peog"].count
        ctx.wait("scalar", S["peog"], y_done)
        ctx.wait("vector", S["peog"], y_done)
        ctx.op("scalar", lambda e: e.activation(gsb[:, :], pp[0][:, :], AF.Silu),
               S["tr"])
        sil_a = S["tr"].count
        ctx.op("vector", lambda e: e.tensor_copy(usb[:, :], pp[1][:, :]), S["ve"])
        sil_b = S["ve"].count
        sil_t = (sil_a, sil_b)
        ctx.wait("vector", S["tr"], sil_a)
        ctx.op("vector", lambda e: e.tensor_tensor(hsb[:, :], gsb[:, :], usb[:, :],
                                                   ALU.mult), S["ve"])
        h_t = S["ve"].count
        ctx.wait("tensor", S["ve"], h_t)
        for h2 in range(2):
            t0, t1 = h2 * 512, (h2 + 1) * 512
            pm = pm2a if h2 == 0 else pm2b
            ctx.op("tensor", lambda e, pm=pm, t0=t0, t1=t1, oc=oc:
                   e.matmul(pm[:, :], a2sb[:, 2 * oc:2 * oc + 2], hsb[:, t0:t1],
                            start=(oc == 0), stop=(oc == NOC - 1),
                            skip_group_check=True),
                   S["pe2"] if (oc == NOC - 1 and h2 == 1) else None)
        t0g = (128 * oc) % 2752
        pieces = []
        src0 = 0
        rem = 128
        cur = t0g
        while rem > 0:
            tile = cur // 128
            off = cur % 128
            ln = min(rem, 128 - off, 2752 - cur)
            pieces.append((tile, off, src0, ln))
            src0 += ln
            cur = (cur + ln) % 2752
            rem -= ln
        for tile, off, s0, ln in pieces:
            dstap = hf[off:off + ln, tile * TOK:(tile + 1) * TOK]
            srcap = hsb[s0:s0 + ln, :]
            covered = any(k[0] == tile and k[1] <= off and k[1] + k[2] >= off + ln
                          for k in hw_written)
            if covered:
                if off != s0:
                    ctx.op("vector", lambda e, s=srcap, off=off, ln=ln:
                           e.tensor_copy(hstmp[off:off + ln, :], s))
                    srcap = hstmp[off:off + ln, :]
                ctx.op("vector", lambda e, d=dstap, s=srcap:
                       e.tensor_tensor(d, d, s, ALU.add),
                       S["ve"])
            else:
                ctx.op("vector", lambda e, d=dstap, s=srcap: e.tensor_copy(d, s),
                       S["ve"])
                hw_written.add((tile, off, ln))
        hfold_ve[oc] = S["ve"].count
    hf_done = S["ve"].count
    ctx.wait("vector", S["pe2"], S["pe2"].count)
    ctx.op("vector", lambda e: e.tensor_scalar(mid2b[:, 0:512], pm2a[:, :], 0.5,
                                               None, ALU.mult))
    ctx.op("vector", lambda e: e.tensor_scalar(mid2b[:, 512:1024], pm2b[:, :], 0.5,
                                               None, ALU.mult), S["ve"])
    ctx.wait("tensor", S["ve"], S["ve"].count)

    ctx.barrier_from("vector", *[c.sem for c in acc2])
    dn_acc = {}
    lidx = 0
    lrel = {}
    ev_prev2 = []
    ogs = [list(range(g, min(g + 3, NFC), 1)) for g in range(0, NFC, 3)]
    for gi, og in enumerate(ogs):
        ncols = len(og) * 128
        col0 = og[0] * 128
        for ic in range(22):
            rows = 128 if ic < 21 else 64
            slot = lidx % 2
            if lidx >= 2:
                rs, rv = lrel[lidx - 2]
                ctx.wait("gpsimd", rs, rv)
            lwch[slot].go(stage[slot][0:rows, 0:ncols],
                          w2qT[ic * 128:ic * 128 + rows, col0:col0 + ncols])
            w_tgt = lwch[slot].sem.count
            ctx.wait("tensor", lwch[slot].sem, w_tgt)
            if gi > 0 and ic == 0:
                ctx.wait("tensor", S["ev"], ev_prev2[gi - 1])
            for oi, oc in enumerate(og):
                for h2 in range(2):
                    t0, t1 = h2 * 512, (h2 + 1) * 512
                    sem1 = None
                    if oi == len(og) - 1 and h2 == 1:
                        sem1 = S["pe"]
                    ctx.op("tensor", lambda e, s=slot, oi=oi, o0=oi * 128, r=rows,
                           ic=ic, t0=t0, t1=t1:
                           e.matmul(pp[oi][:, t0:t1],
                                    stage[s][0:r, o0:o0 + 128],
                                    hf[0:r, ic * TOK + t0:ic * TOK + t1],
                                    start=(ic == 0), stop=False), sem1)
            lrel[lidx] = (S["pe"], S["pe"].count)
            lidx += 1
        for oi, oc in enumerate(og):
            for h2 in range(2):
                t0, t1 = h2 * 512, (h2 + 1) * 512
                ctx.op("tensor", lambda e, oi=oi, oc=oc, t0=t0, t1=t1:
                       e.matmul(pp[oi][:, t0:t1], b2sb[:, oc * 128:oc * 128 + 128],
                                mid2b[:, t0:t1], start=False, stop=True),
                       S["peog"] if (oi == len(og) - 1 and h2 == 1) else None)
        og_tgt = S["peog"].count
        for oi, oc in enumerate(og):
            slot2 = oc % 2
            ctx.wait("vector", S["peog"], og_tgt)
            if oc >= 2:
                ctx.wait("vector", acc2[slot2].sem, dn_acc.get(oc - 2, 0))
            ctx.op("vector", lambda e, s=slot2, oi=oi:
                   e.tensor_copy(fstage[s][:, :], pp[oi][:, :]), S["ev"])
            ctx.wait("gpsimd", S["ev"], S["ev"].count)
            acc2[slot2].go(x2[oc * 128:(oc + 1) * 128, :], fstage[slot2][:, :],
                           accum_op=ALU.add)
            dn_acc[oc] = acc2[slot2].sem.count
        ev_prev2.append(S["ev"].count)

    for _c in acc2:
        ctx.wait("gpsimd", _c.sem, _c.sem.count)
    ctx.dma("gpsimd", xA3, x2, S["xac"])
    x3copy = S["xac"].count
    sems_ln = dict(xl=S["xl"], pe=S["pe"], ve=S["ve"], tr=S["tr"], ac=S["ve"])
    ctx.barrier_from("tensor", S["ev"], S["ve"])
    emit_ln(ctx, "ln3", x2, big, aux, sems_ln)

    ctx.barrier_from("tensor", S["ve"], S["ev"])
    identb = L["identb"]
    for i in range(3):
        for h2 in range(2):
            t0, t1 = h2 * 512, (h2 + 1) * 512
            ctx.op("tensor", lambda e, i=i, t0=t0, t1=t1:
                   e.matmul(pp[i][:, t0:t1], identb[:, :], big[:, 0:512],
                            start=True, stop=True),
                   S["peog"] if (i == 2 and h2 == 1) else None)
    ctx.wait("tensor", S["peog"], S["peog"].count)
    q3_store = {}

    def qkv3_evac(oc, ps, og_tgt):
        slot = oc % 2
        ctx.wait("scalar", S["peog"], og_tgt)
        if oc >= 2:
            ctx.wait("scalar", stc[slot].sem, q3_store[oc - 2])
        ctx.op("scalar", lambda e, s=slot, p=ps, oc=oc:
               e.activation(stage[s][:, 0:TOK], p[:, :], AF.Identity,
                            bias=bq3[:, oc:oc + 1], scale=1.0), S["ev"])
        ctx.wait("sync", S["ev"], S["ev"].count)
        stc[slot].go(qkv3T[oc * 128:(oc + 1) * 128, :], stage[slot][:, 0:TOK])
        q3_store[oc] = stc[slot].sem.count

    emit_mm(ctx, "q3", wqkv3, NFC, 24,
            lambda ic: big[:, ic * TOK:(ic + 1) * TOK], qkv3_evac,
            S, pp=pp, goc=3, icb=4)

    for f in range(4):
        d = xf8[:, f * TOK:(f + 1) * TOK]
        ctx.op("vector", lambda e, d=d, f=f: e.tensor_tensor(
            d, big[:, f * TOK:(f + 1) * TOK], big[:, (f + 4) * TOK:(f + 5) * TOK],
            ALU.add))
        for r in range(2, 8):
            ctx.op("vector", lambda e, d=d, f=f, r=r: e.tensor_tensor(
                d, d, big[:, (f + 4 * r) * TOK:(f + 4 * r + 1) * TOK], ALU.add),
                S["ve"] if (f == 3 and r == 7) else None)
    ctx.wait("tensor", S["ve"], S["ve"].count)
    kmb = [ctx.sbuf(f"kmb{i}", [128, 1024], BF16) for i in range(2)]
    ctx.barrier_from("tensor", S["ev"])
    f3_acc = {}
    lidx = 0
    lrel2 = {}
    ev_prev3 = []
    ctx.barrier_from("vector", *[c.sem for c in acc2])
    ogs3 = [list(range(g, min(g + 2, NFC))) for g in range(0, NFC, 2)]
    for gi, og in enumerate(ogs3):
        ncols = len(og) * 128
        col0 = og[0] * 128
        slot = gi % 2
        if gi >= 2:
            rs, rv = lrel2[gi - 2]
            ctx.wait("gpsimd", rs, rv)
        lwch[slot].go(kmb[slot][:, 0:4 * ncols].rearrange("p (i c) -> p i c", i=4),
                      kern8[:, col0:col0 + ncols].rearrange("(i p) c -> p i c", p=128))
        w_tgt = lwch[slot].sem.count
        ctx.wait("tensor", lwch[slot].sem, w_tgt)
        if gi > 0:
            ctx.wait("tensor", S["ev"], ev_prev3[gi - 1])
        for ic in range(4):
            for oi, oc in enumerate(og):
                for h2 in range(2):
                    t0, t1 = h2 * 512, (h2 + 1) * 512
                    sem1 = S["peog"] if (ic == 3 and oi == len(og) - 1 and h2 == 1) else None
                    ctx.op("tensor", lambda e, s=slot, ic=ic, oi=oi, o0=ic * ncols + oi * 128,
                           t0=t0, t1=t1, l=(ic == 3):
                           e.matmul(pp[oi][:, t0:t1], kmb[s][:, o0:o0 + 128],
                                    xf8[:, ic * TOK + t0:ic * TOK + t1],
                                    start=(ic == 0), stop=l), sem1)
        lrel2[gi] = (S["peog"], S["peog"].count)
        og_tgt = S["peog"].count
        for oi, oc in enumerate(og):
            slot2 = oc % 2
            ctx.wait("vector", S["peog"], og_tgt)
            if oc >= 2:
                ctx.wait("vector", acc2[slot2].sem, f3_acc.get(oc - 2, 0))
            ctx.op("vector", lambda e, s=slot2, oi=oi:
                   e.tensor_copy(fstage[s][:, :], pp[oi][:, :]), S["ev"])
            ctx.wait("gpsimd", S["ev"], S["ev"].count)
            ctx.wait("gpsimd", S["xac"], x3copy)
            acc2[slot2].go(xA3[oc * 128:(oc + 1) * 128, :], fstage[slot2][:, :],
                           accum_op=ALU.add)
            f3_acc[oc] = acc2[slot2].sem.count
        ev_prev3.append(S["ev"].count)

    for ch in acc2 + stc:
        ctx.wait("gpsimd", ch.sem, ch.sem.count)
    ctx.barrier_from("gpsimd", S["xac"])
    return ctx


def build_B3():
    ctx = Ctx()
    q3 = ctx.dram("q3", [1024, TOK], BF16, kind="ExternalInput")
    k3all = ctx.dram("k3all", [1024, 8192], BF16, kind="ExternalInput")
    v3tok = ctx.dram("v3tok", [8192, 1024], BF16, kind="ExternalInput")
    xA3 = ctx.dram("xA3", [D, TOK], F32, kind="ExternalInput")
    wo3 = ctx.dram("wo3", [1024, D], F32, kind="ExternalInput")
    bo3 = ctx.dram("bo3", [128, NFC], F32, kind="ExternalInput")
    cvec = ctx.dram("cvec", [128, 8], F32, kind="ExternalInput")
    xout = ctx.dram("xout", [D, TOK], F32, kind="ExternalOutput")

    S = {k: ctx.sem(k) for k in ("pe", "peog", "ev", "tr", "ve", "xac", "q")}
    kch = ctx.chan("gpsimd", "kch")
    vch = [ctx.chan("gpsimd", "vch0"), ctx.chan("gpsimd", "vch1")]
    acc2 = [ctx.chan("gpsimd", "acc0"), ctx.chan("gpsimd", "acc1")]

    q3sb = ctx.sbuf("q3sb", [128, 8 * TOK], BF16)
    k3sb = ctx.sbuf("k3sb", [128, 8 * 4096], BF16)
    efull = ctx.sbuf("efull", [128, 32 * 512], BF16)
    esum = ctx.sbuf("esum", [128, 512], BF16)
    v3sb = [ctx.sbuf(f"v3sb{i}", [128, 32 * 128], BF16) for i in range(2)]
    ao3 = ctx.sbuf("ao3", [128, 8 * TOK], BF16)
    rd_row = ctx.sbuf("rd_row", [1, 512], F32)
    bc_sb = ctx.sbuf("bc_sb", [128, 512], F32)
    onesb = ctx.sbuf("onesb", [128, 1], BF16)
    onesf = ctx.sbuf("onesf", [1, 128], F32)
    cv = ctx.sbuf("cv", [128, 8], F32)
    fstage = [ctx.sbuf(f"fstage{i}", [128, TOK], F32) for i in range(2)]

    ps_sc = [ctx.psum(f"ps_sc{i}", [128, 1536], F32) for i in range(2)]
    ps_av = ctx.psum("ps_av", [128, 512], F32)
    ps_m = ctx.psum("ps_m", [128, 512], F32)

    ctx.op("gpsimd", lambda e: e.memset(onesb[:, :], 1.0))
    ctx.op("gpsimd", lambda e: e.memset(onesf[:, :], 1.0))
    ctx.dma("gpsimd", cv[:, :], cvec, S["q"])
    ctx.dma("gpsimd", q3sb[:, :].rearrange("p (dc t) -> p dc t", t=TOK),
            q3.rearrange("(dc p) t -> p dc t", p=128), S["q"])
    ctx.dma("gpsimd", xout, xA3, S["xac"])
    x_copy = S["xac"].count
    q_tgt = S["q"].count
    for eng in ("tensor", "vector", "scalar"):
        ctx.wait(eng, S["q"], q_tgt)

    av_last = 0
    for b in range(2):
        kch.go(k3sb[:, :].rearrange("p (dc t) -> p dc t", t=4096),
               k3all[:, b * 4096:(b + 1) * 4096].rearrange("(dc p) t -> p dc t", p=128))
        k_tgt = kch.sem.count
        ctx.wait("tensor", kch.sem, k_tgt)
        if b == 1:
            ctx.wait("scalar", S["pe"], av_last)
        groups = [list(range(g, min(g + 3, 32))) for g in range(0, 32, 3)]
        grp_tr = {}
        for gi, grp in enumerate(groups):
            scp = ps_sc[gi % 2]
            if gi >= 2:
                ctx.wait("tensor", S["tr"], grp_tr[gi - 2])
            for i, kb in enumerate(grp):
                for dc in range(8):
                    ctx.op("tensor", lambda e, scp=scp, i=i, kb=kb, dc=dc, b=b:
                           e.matmul(scp[:, i * 512:(i + 1) * 512],
                                    k3sb[:, dc * 4096 + kb * 128: dc * 4096 + kb * 128 + 128],
                                    q3sb[:, dc * TOK + b * 512: dc * TOK + b * 512 + 512],
                                    start=(dc == 0), stop=(dc == 7)),
                           S["pe"] if (i == len(grp) - 1 and dc == 7) else None)
            ctx.wait("scalar", S["pe"], S["pe"].count)
            ng = len(grp)
            ctx.op("scalar", lambda e, scp=scp, ng=ng, c0=grp[0] * 512:
                   e.activation(efull[:, c0:c0 + ng * 512], scp[:, 0:ng * 512],
                                AF.Exp, bias=cv[:, CV_SHIFT3:CV_SHIFT3 + 1],
                                scale=0.03125), S["tr"])
            grp_tr[gi] = S["tr"].count
            ctx.wait("vector", S["tr"], S["tr"].count)
            for i, kb in enumerate(grp):
                esl = efull[:, kb * 512:(kb + 1) * 512]
                if kb == 0:
                    ctx.op("vector", lambda e, esl=esl: e.tensor_copy(esum[:, :], esl))
                else:
                    ctx.op("vector", lambda e, esl=esl:
                           e.tensor_tensor(esum[:, :], esum[:, :], esl, ALU.add))
            ctx.op("vector", lambda e: e.engine_nop(), S["ve"])
        ctx.wait("tensor", S["ve"], S["ve"].count)
        ctx.op("tensor", lambda e: e.matmul(ps_m[0:1, :], onesb[:, :], esum[:, :],
                                            start=True, stop=True), S["pe"])
        ctx.wait("vector", S["pe"], S["pe"].count)
        ctx.op("vector", lambda e: e.reciprocal(rd_row[:, :], ps_m[0:1, :]), S["ve"])
        ctx.wait("tensor", S["ve"], S["ve"].count)
        ctx.op("tensor", lambda e: e.matmul(ps_m[:, :], onesf[:, :], rd_row[:, :],
                                            start=True, stop=True), S["pe"])
        ctx.wait("vector", S["pe"], S["pe"].count)
        ctx.op("vector", lambda e: e.tensor_copy(bc_sb[:, :], ps_m[:, :]), S["ve"])
        bc_t = S["ve"].count
        if b == 0:
            vrel = {}
        for dc in range(8):
            gdc = b * 8 + dc
            vslot = dc % 2
            if gdc >= 2:
                ctx.wait("gpsimd", S["pe"], vrel[gdc - 2])
            vch[vslot].go(v3sb[vslot][:, :].rearrange("p (kb d) -> p kb d", d=128),
                          v3tok[b * 4096:(b + 1) * 4096, dc * 128:(dc + 1) * 128]
                          .rearrange("(kb p) d -> p kb d", p=128))
            v_tgt = vch[vslot].sem.count
            ctx.wait("tensor", vch[vslot].sem, v_tgt)
            if b == 0 and dc == 0:
                pass
            else:
                ctx.wait("tensor", S["ve"], ev_prev_av)
            for kb in range(32):
                ctx.op("tensor", lambda e, vs=vslot, kb=kb:
                       e.matmul(ps_av[:, :], v3sb[vs][:, kb * 128:(kb + 1) * 128],
                                efull[:, kb * 512:(kb + 1) * 512],
                                start=(kb == 0), stop=(kb == 31)),
                       S["pe"] if kb == 31 else None)
            vrel[gdc] = S["pe"].count
            ctx.wait("vector", S["pe"], S["pe"].count)
            ctx.wait("vector", S["ve"], bc_t)
            ctx.op("vector", lambda e, dc=dc, b=b:
                   e.tensor_tensor(ao3[:, dc * TOK + b * 512: dc * TOK + b * 512 + 512],
                                   ps_av[:, :], bc_sb[:, :], ALU.mult), S["ve"])
            ev_prev_av = S["ve"].count
        av_last = S["pe"].count

    ctx.barrier_from("tensor", S["ve"])
    bo3sb = ctx.sbuf("bo3sb", [128, NFC], F32)
    ctx.dma("gpsimd", bo3sb[:, :], bo3, S["q"])
    ctx.wait("vector", S["q"], S["q"].count)
    w_acc = {}
    ctx.barrier_from("vector", *[c.sem for c in acc2])

    def wo3_evac(oc, ps, og_tgt):
        slot = oc % 2
        ctx.wait("vector", S["peog"], og_tgt)
        if oc >= 2:
            ctx.wait("vector", acc2[slot].sem, w_acc[oc - 2])
        ctx.op("vector", lambda e, s=slot, p=ps:
               e.tensor_copy(fstage[s][:, :], p[:, :]))
        ctx.op("vector", lambda e, s=slot, oc=oc:
               e.tensor_scalar(fstage[s][:, :], fstage[s][:, :],
                               bo3sb[:, oc:oc + 1], None, ALU.add), S["ev"])
        ctx.wait("gpsimd", S["ev"], S["ev"].count)
        ctx.wait("gpsimd", S["xac"], x_copy)
        acc2[slot].go(xout[oc * 128:(oc + 1) * 128, :], fstage[slot][:, :],
                      accum_op=ALU.add)
        w_acc[oc] = acc2[slot].sem.count

    emit_mm(ctx, "wo3", wo3, 8, NFC,
            lambda ic: ao3[:, ic * TOK:(ic + 1) * TOK], wo3_evac,
            S, pp=[ps_sc[0][:, 0:TOK], ps_sc[1][:, 0:TOK]], goc=2, icb=8)
    for ch in acc2:
        ctx.wait("gpsimd", ch.sem, ch.sem.count)
    ctx.barrier_from("gpsimd", S["xac"])
    return ctx


def host_C2A3_inputs(ins, core, xAp, aoT):
    def T_(a):
        return np.ascontiguousarray(np.asarray(a).T.astype(np.float32, copy=False))
    rb = np.arange(128)
    indT = (np.arange(D)[None, :] // 32 == rb[:, None]).astype(np.float32)
    rep4 = (np.arange(128)[None, :] // 32 == np.arange(4)[:, None]).astype(np.float32)
    g4 = (np.arange(128)[:, None] // 32 == np.arange(4)[None, :]).astype(np.float32)
    mask = np.ascontiguousarray(np.asarray(ins["L2_attn_mask"]))
    mask8 = np.ascontiguousarray(mask.view(np.int8)[:, 0::4]).view(np.uint8)
    a2 = np.asarray(ins["L2_ffn_a2"])
    a2Tp = np.ascontiguousarray(
        a2.T.reshape(86, 128, 2).transpose(1, 0, 2).reshape(128, 172)
        .astype(np.float32))
    kern = np.asarray(ins["L3_ffn_kernel"]).reshape(512, D)
    return dict(
        xAp=xAp, aoT=aoT, woT=T_(ins["L1_attn_Wo"]), cvec=host_cvec(),
        ident=np.eye(128, dtype=np.float32),
        mask8=mask8, w2aT=T_(ins["L2_attn_weight"]),
        bias2=np.ascontiguousarray(np.asarray(ins["L2_attn_bias"]).reshape(NFC, 128).T),
        indT=indT, rep4=rep4, g4=g4,
        w1qT=T_(ins["L2_ffn_w1q"]), w3qT=T_(ins["L2_ffn_w3q"]),
        w2qT=T_(ins["L2_ffn_w2q"]),
        a1T=T_(ins["L2_ffn_a1"]), a3T=T_(ins["L2_ffn_a3"]), a2Tp=a2Tp,
        b1T=T_(ins["L2_ffn_b1"]), b3T=T_(ins["L2_ffn_b3"]), b2T=T_(ins["L2_ffn_b2"]),
        wqkv3=np.ascontiguousarray(np.asarray(ins["L3_attn_Wqkv"]).astype(np.float32)),
        bqkv3=np.ascontiguousarray(np.asarray(ins["L3_attn_bqkv"]).reshape(24, 128).T),
        kern8=np.ascontiguousarray(kern),
    )


def host_B3_inputs(ins, qkv3s, core, xA3c):
    k3 = np.empty((1024, 8192), BF)
    v3 = np.empty((8192, 1024), BF)
    for r in range(8):
        qr = np.asarray(qkv3s[r])
        for b in range(2):
            k3[:, b * 4096 + 512 * r: b * 4096 + 512 * (r + 1)] = \
                qr[1024:2048, b * 512:(b + 1) * 512]
            v3[b * 4096 + 512 * r: b * 4096 + 512 * (r + 1), :] = \
                qr[2048:3072, b * 512:(b + 1) * 512].T
    return dict(
        q3=np.ascontiguousarray(np.asarray(qkv3s[core])[0:1024]),
        k3all=np.ascontiguousarray(k3), v3tok=np.ascontiguousarray(v3),
        xA3=xA3c, wo3=np.ascontiguousarray(np.asarray(ins["L3_attn_Wo"]).astype(np.float32)),
        bo3=np.ascontiguousarray(np.asarray(ins["L3_attn_bo"]).reshape(NFC, 128).T),
        cvec=host_cvec(),
    )


FUSED = True


def kernel(**ins):
    if FUSED:
        return kernel_fused(ins)
    x0 = np.asarray(ins["L0_attn_hidden_states"], np.float32)

    def run(nc, in_maps):
        return run_bass_kernel_spmd(nc, in_maps, core_ids=list(range(NCORES))).results

    nc = build_A01(0).finalize()
    r = run(nc, [host_A0_inputs(ins, c, host_xT(x0, c)) for c in range(NCORES)])
    qkv0 = [r[c]["qkvT"] for c in range(NCORES)]
    xA0 = [np.asarray(r[c]["xA"]) for c in range(NCORES)]
    ncb = build_B01().finalize()
    rb = run(ncb, [host_B_inputs(qkv0, c) for c in range(NCORES)])
    ao0 = [rb[c]["aomine"] for c in range(NCORES)]
    nca1 = build_A01(1).finalize()
    r1 = run(nca1, [host_A1_inputs(ins, c, xA0[c], host_reshard_ao(ao0, c))
                    for c in range(NCORES)])
    qkv1 = [r1[c]["qkvT"] for c in range(NCORES)]
    xA1 = [np.asarray(r1[c]["xA"]) for c in range(NCORES)]
    rb1 = run(ncb, [host_B_inputs(qkv1, c) for c in range(NCORES)])
    ao1 = [rb1[c]["aomine"] for c in range(NCORES)]
    ncc = build_C2A3().finalize()
    r2 = run(ncc, [host_C2A3_inputs(ins, c, xA1[c], host_reshard_ao(ao1, c))
                   for c in range(NCORES)])
    qkv3 = [r2[c]["qkv3T"] for c in range(NCORES)]
    xA3 = [np.asarray(r2[c]["xA3"]) for c in range(NCORES)]
    ncb3 = build_B3().finalize()
    r3 = run(ncb3, [host_B3_inputs(ins, qkv3, c, xA3[c]) for c in range(NCORES)])
    return host_unxT([np.asarray(r3[c]["xout"]) for c in range(NCORES)])


class PairComm:

    NROUNDS = 8

    def __init__(self, ctx):
        self.ctx = ctx
        self.sflag = [ctx.dram(f"sflag{r}", [4, 64], I32, addr_space="Shared")
                      for r in range(self.NROUNDS)]
        self.fidx = ctx.dram("fidx", [2, 1], I32, kind="ExternalInput")
        self.fvals = ctx.dram("fvals", [2, 64 * self.NROUNDS], I32,
                              kind="ExternalInput")
        self.s_fidx = ctx.sbuf("s_fidx", [2, 1], I32)
        self.s_fval = ctx.sbuf("s_fval", [2, 64 * self.NROUNDS], I32)
        self.s_flg = ctx.sbuf("s_flg", [2, 64], I32)
        self.sem = ctx.sem("pcflag")
        self.round = 0
        ctx.dma("gpsimd", self.s_fidx[:, :], self.fidx, self.sem)
        ctx.dma("gpsimd", self.s_fval[:, :], self.fvals, self.sem)
        ctx.wait("gpsimd", self.sem, self.sem.count)

    def publish(self, after_sems):
        ctx = self.ctx
        r = self.round
        self.round += 1
        assert r < self.NROUNDS
        for s, v in after_sems:
            ctx.wait("gpsimd", s, v)
        def f(e, r=r):
            e.indirect_dma_start(
                self.sflag[r][0:2, :],
                IndirectOffsetOnAxis(self.s_fidx[:, :], 0),
                self.s_fval[:, 64 * r:64 * (r + 1)], None).then_inc(self.sem.h, 16)
        ctx.emit("gpsimd", f)
        self.sem.count += 16
        ctx.wait("gpsimd", self.sem, self.sem.count)
        return r

    def poll(self, r):
        ctx = self.ctx
        psem = ctx.sem(f"pcpoll{r}")

        def f(e, r=r, psem=psem):
            with (e.register(f"it{r}") as it, e.register(f"nr{r}") as nr,
                  e.register(f"r0_{r}") as r0, e.register(f"r1_{r}") as r1,
                  e.register(f"nn{r}") as nn,
                  e.register(f"a{r}") as a, e.register(f"b{r}") as b):
                e.reg_mov(it, 0)
                e.reg_mov(nr, 1)
                e.reg_load(nn, self.s_fval[0:1, 64 * r:64 * r + 1])
                with e.While(nr):
                    e.dma_start(self.s_flg[:, :],
                                self.sflag[r][0:2, :]).then_inc(psem.h, 16)
                    e.reg_add(it, it, 16)
                    e.wait_ge(psem.h, it)
                    e.reg_load(r0, self.s_flg[0:1, 0:1])
                    e.reg_load(r1, self.s_flg[1:2, 0:1])
                    e.reg_alu(a, r0, nn, ALU.is_equal)
                    e.reg_alu(b, r1, nn, ALU.is_equal)
                    e.reg_alu(a, a, b, ALU.mult)
                    e.reg_alu(nr, a, 1, ALU.is_lt)
        ctx.emit("gpsimd", f)

    def barrier_engines(self, engines=("tensor", "vector", "scalar", "sync")):
        ctx = self.ctx
        if not hasattr(self, "bar"):
            self.bar = ctx.sem("pcbar")
            self.scr = ctx.sbuf("pc_scr", [1, 8], I32)
        ctx.op("gpsimd", lambda e: e.memset(self.scr[:, :], 0), self.bar)
        for eng in engines:
            ctx.wait(eng, self.bar, self.bar.count)


def build_fused():
    ctx = Ctx()
    pcm = PairComm(ctx)
    nc = ctx.nc
    xT2 = ctx.dram("xT2", [D, 2 * TOK], F32, kind="ExternalInput")
    cvec = ctx.dram("cvec", [128, 8], F32, kind="ExternalInput")
    ident = ctx.dram("ident", [128, 128], F32, kind="ExternalInput")
    ctab4 = ctx.dram("ctab4", [128, 4096], F32, kind="ExternalInput")
    stab4 = ctx.dram("stab4", [128, 4096], F32, kind="ExternalInput")
    trimask = ctx.dram("trimask", [128, 896], F32, kind="ExternalInput")
    WQKV = [ctx.dram(f"wqkvT{l}", [128, 48 * D], F32, kind="ExternalInput")
            for l in range(2)]
    WO = [ctx.dram(f"woT{l}", [128, 32 * D], F32, kind="ExternalInput")
          for l in range(2)]
    w0T = ctx.dram("w0T", [128, 32 * D], F32, kind="ExternalInput")
    bias0 = ctx.dram("bias0", [128, NFC], F32, kind="ExternalInput")
    gwT = ctx.dram("gwT", [D, 8], F32, kind="ExternalInput")
    ics = ctx.dram("ics", [8, D], F32, kind="ExternalInput")
    ocs = ctx.dram("ocs", [8, D], F32, kind="ExternalInput")
    pwT = ctx.dram("pwT", [2048, D], U8, kind="ExternalInput")
    sclT = ctx.dram("sclT", [32, D], F32, kind="ExternalInput")
    sel2 = ctx.dram("sel2", [2, 128], F32, kind="ExternalInput")
    mask8 = ctx.dram("mask8", [D, D], U8, kind="ExternalInput")
    w2aT = ctx.dram("w2aT", [D, D], F32, kind="ExternalInput")
    bias2 = ctx.dram("bias2", [128, NFC], F32, kind="ExternalInput")
    indT = ctx.dram("indT", [128, D], F32, kind="ExternalInput")
    rep4 = ctx.dram("rep4", [4, 128], F32, kind="ExternalInput")
    g4 = ctx.dram("g4", [128, 4], F32, kind="ExternalInput")
    w1qT = ctx.dram("w1qT", [128, 86 * 1024], F32, kind="ExternalInput")
    w3qT = ctx.dram("w3qT", [128, 86 * 1024], F32, kind="ExternalInput")
    w2qT = ctx.dram("w2qT", [2752, D], F32, kind="ExternalInput")
    a1T = ctx.dram("a1T", [D, 2], F32, kind="ExternalInput")
    a3T = ctx.dram("a3T", [D, 2], F32, kind="ExternalInput")
    a2Tp = ctx.dram("a2Tp", [128, 172], F32, kind="ExternalInput")
    b1T = ctx.dram("b1T", [2, 11008], F32, kind="ExternalInput")
    b3T = ctx.dram("b3T", [2, 11008], F32, kind="ExternalInput")
    b2T = ctx.dram("b2T", [2, D], F32, kind="ExternalInput")
    wqkv3 = ctx.dram("wqkv3", [128, 24 * D], F32, kind="ExternalInput")
    bqkv3 = ctx.dram("bqkv3", [128, 24], F32, kind="ExternalInput")
    kern8 = ctx.dram("kern8", [512, D], I8, kind="ExternalInput")
    wo3 = ctx.dram("wo3", [128, 32 * 1024], F32, kind="ExternalInput")
    bo3 = ctx.dram("bo3", [128, NFC], F32, kind="ExternalInput")
    iq_sc = ctx.dram("iq_sc", [128, 48], I32, kind="ExternalInput")
    iq_q = ctx.dram("iq_q", [128, 32], I32, kind="ExternalInput")
    iq_kv = ctx.dram("iq_kv", [128, 16], I32, kind="ExternalInput")
    iq_v = ctx.dram("iq_v", [128, 16], I32, kind="ExternalInput")
    iao_sc = ctx.dram("iao_sc", [128, 32], I32, kind="ExternalInput")
    iao_g = ctx.dram("iao_g", [128, 32], I32, kind="ExternalInput")
    i3_sc = ctx.dram("i3_sc", [128, 16], I32, kind="ExternalInput")
    i3_k = ctx.dram("i3_k", [128, 16], I32, kind="ExternalInput")
    i3_v = ctx.dram("i3_v", [128, 16], I32, kind="ExternalInput")
    xout = ctx.dram("xout", [D, 2 * TOK], F32, kind="ExternalOutput")
    dbgs = None
    if FDBG:
        dbgs = [ctx.dram(f"dbg{i}", [D, 2 * TOK], F32, kind="ExternalOutput")
                for i in range(3)]
        dbgq = ctx.dram("dbgq", [128, 6144], BF16, kind="ExternalOutput")
    sqkv = [ctx.dram(f"sqkv{k}", [2 * 6144, TOK], BF16, addr_space="Shared")
            for k in range(2)]
    sao4 = [ctx.dram(f"sao{k}", [2 * D, 512], BF16, addr_space="Shared")
            for k in range(4)]
    s3kv = [ctx.dram(f"s3kv{k}", [2 * 2048, TOK], BF16, addr_space="Shared")
            for k in range(2)]
    x2d = [ctx.dram(f"x{k}", [D, TOK], F32) for k in range(2)]
    xA2 = [ctx.dram(f"xA{k}", [D, TOK], F32) for k in range(2)]
    w1T = ctx.dram("w1T", [D, D], BF16)
    bmd = ctx.dram("bmd", [128, 128], F32)
    bxd = ctx.dram("bxd", [128, D], BF16)

    S = {k: ctx.sem(k) for k in
         ("wl", "xl", "pe", "peog", "ev", "tr", "ve", "ac", "st", "pe2",
          "tb", "xc", "xac", "ms", "ve2", "ix", "sc2", "gx")}
    acc2 = [ctx.chan("gpsimd", f"acc{i}") for i in range(4)]
    stc = [ctx.chan("sync", "stc0"), ctx.chan("sync", "stc1")]
    scc = [ctx.chan("gpsimd", "scc0"), ctx.chan("gpsimd", "scc1")]
    gcc = [ctx.chan("gpsimd", "gcc0"), ctx.chan("gpsimd", "gcc1")]
    lwch = [ctx.chan("gpsimd", "lw0"), ctx.chan("gpsimd", "lw1")]
    b4ch = ctx.chan("gpsimd", "b4ch")
    bmch = ctx.chan("sync", "bmch")
    mbch = ctx.chan("gpsimd", "mbch")
    wch6 = [ctx.chan("gpsimd", f"wch{i}") for i in range(6)]

    pp = [ctx.psum(f"pp{i}", [128, TOK], F32) for i in range(3)]
    pp3 = ctx.psum("pp3", [128, 512], F32)
    pp4 = ctx.psum("pp4", [128, TOK], BF16)
    pm2a = pp3[0:2, 0:512]
    pm2b = pp[2][0:2, 0:512]

    big = ctx.sbuf("big", [128, NFC * TOK], BF16)
    hf = big[:, 0:22 * TOK]
    arB = ctx.sbuf("arB", [128, 24576], BF16)
    wfb = [ctx.sbuf(f"wfb{i}", [128, 2048], F32) for i in range(2)]
    wbb = [ctx.sbuf(f"wbb{i}", [128, 4096], BF16) for i in range(2)]
    wbufs = (wfb, wbb)
    onesb = ctx.sbuf("onesb", [128, 1], BF16)
    onesf = ctx.sbuf("onesf", [1, 128], F32)
    cv = ctx.sbuf("cv", [128, 8], F32)
    identb = ctx.sbuf("identb", [128, 128], BF16)
    ctb4 = ctx.sbuf("ctb4", [128, 4096], BF16)
    stb4 = ctx.sbuf("stb4", [128, 4096], BF16)
    tri = ctx.sbuf("tri", [128, 896], BF16)
    fstage = [ctx.sbuf(f"fstage{i}", [128, TOK], F32) for i in range(2)]
    stage = [ctx.sbuf(f"stage{i}", [128, TOK], BF16) for i in range(2)]
    idxs = {nm: ctx.sbuf("s_" + nm, [128, w], I32) for nm, w in
            (("iq_sc", 48), ("iq_q", 32), ("iq_kv", 16), ("iq_v", 16),
             ("iao_sc", 32), ("iao_g", 32), ("i3_sc", 16), ("i3_k", 16),
             ("i3_v", 16))}
    aux = dict(onesb=onesb, onesf=onesf, cv=cv, pp=pp, identb=identb,
               pp3=pp3, pp4=pp4, fstage=fstage, acc2=acc2)
    aux["f0_icsb"] = arB[0:8, 0:4096]
    aux["f0_ocsb"] = arB[0:8, 4096:8192]
    aux["f0_gw"] = arB[:, 8192:8448]
    aux["f0_rtT"] = arB[0:8, 8448:9472]
    aux["f1_scl2"] = arB[0:2, 0:4096]
    aux["f1_hi32"] = arB[:, 4096:5120]
    aux["f1_lo32"] = arB[:, 5120:6144]
    aux["f1_bsc"] = arB[:, 6144:7168]
    aux["ln_shared"] = dict(
        sq=[wbb[0][:, 0:TOK], wbb[0][:, TOK:2 * TOK]],
        rows=dict(m=wfb[0][0:1, 0:TOK], msq=wfb[0][0:1, TOK:2 * TOK],
                  var=wfb[1][0:1, 0:TOK], std=wfb[1][0:1, TOK:2 * TOK],
                  rstd=fstage[0][0:1, 0:TOK], bneg=fstage[1][0:1, 0:TOK]))
    rope = dict(rot=ctx.sbuf("rot", [128, TOK], BF16),
                tmpb=ctx.sbuf("tmpb", [128, TOK], BF16))

    ctx.op("gpsimd", lambda e: e.memset(onesb[:, :], 1.0))
    ctx.op("gpsimd", lambda e: e.memset(onesf[:, :], 1.0), S["pe2"], 1)
    ctx.wait("tensor", S["pe2"], 1)
    for dst, srcp in ((cv, cvec), (identb, ident), (ctb4, ctab4), (stb4, stab4),
                      (tri, trimask)):
        ctx.dma("gpsimd", dst[:, :], srcp, S["tb"])
    for nm, ap in (("iq_sc", iq_sc), ("iq_q", iq_q), ("iq_kv", iq_kv),
                   ("iq_v", iq_v), ("iao_sc", iao_sc), ("iao_g", iao_g),
                   ("i3_sc", i3_sc), ("i3_k", i3_k), ("i3_v", i3_v)):
        ctx.dma("gpsimd", idxs[nm][:, :], ap, S["tb"])
    for k in range(2):
        ctx.dma("gpsimd", x2d[k], xT2[:, k * TOK:(k + 1) * TOK], S["xc"])
    tabs = S["tb"].count
    for eng in ("tensor", "vector", "scalar"):
        ctx.wait(eng, S["tb"], tabs)
    ctx.wait("gpsimd", S["xc"], S["xc"].count)

    sems_ln = dict(xl=S["xl"], pe=S["pe"], ve=S["ve"], tr=S["tr"], ac=S["ve"])

    def scatter(ch, dst_base, idx_col, src_ap, cols):
        def f(e, dst=dst_base, idxap=idx_col, src=src_ap, cols=cols):
            e.indirect_dma_start(dst[0:128, cols[0]:cols[1]],
                                 IndirectOffsetOnAxis(idxap, 0),
                                 src, None).then_inc(ch.sem.h, 16)
        if ch.sem.count > 0:
            ctx.wait(ch.queue, ch.sem, ch.sem.count)
        ctx.emit("gpsimd", f)
        ch.sem.count += 16
        return ch.sem.count

    def gather(ch, dst_ap, src_base, idx_col, cols):
        def f(e, dst=dst_ap, idxap=idx_col, src=src_base, cols=cols):
            e.indirect_dma_start(dst, None,
                                 src[0:128, cols[0]:cols[1]],
                                 IndirectOffsetOnAxis(idxap, 0)).then_inc(ch.sem.h, 16)
        if ch.sem.count > 0:
            ctx.wait(ch.queue, ch.sem, ch.sem.count)
        ctx.emit("gpsimd", f)
        ch.sem.count += 16
        return ch.sem.count

    for lay in range(2):
        qkv_sc_done = []
        for ck in range(2):
            ctx.barrier_from("tensor", S["ev"], S["ve"])
            emit_ln(ctx, f"ln{lay}{ck}", x2d[ck], big, aux, sems_ln)
            ctx.wait("gpsimd", S["ve"], S["ve"].count)
            ctx.dma("gpsimd", xA2[ck], x2d[ck], S["xac"])
            xA_copy = S["xac"].count
            ctx.barrier_from("tensor", S["ve"])
            qsc = {}

            def qkv_evac(oc, ps, og_tgt, ck=ck):
                slot = oc % 2
                ctx.wait("scalar", S["peog"], og_tgt)
                if oc >= 2:
                    ctx.wait("scalar", scc[slot].sem, qsc[oc - 2])
                ctx.op("scalar", lambda e, s=slot, p=ps:
                       e.activation(stage[s][:, :], p[:, :], AF.Identity,
                                    bias=0.0, scale=1.0), S["ev"])
                ev_tgt = S["ev"].count
                if oc < 40:
                    ctx.wait("vector", S["ev"], ev_tgt)
                    cta = ctb4[:, ck * TOK:(ck + 1) * TOK]
                    sta = stb4[:, ck * TOK:(ck + 1) * TOK]
                    ctx.op("vector", lambda e, s=slot: e.tensor_copy(
                        rope["rot"][0:64, :], stage[s][64:128, :]))
                    ctx.op("vector", lambda e, s=slot: e.tensor_copy(
                        rope["rot"][64:128, :], stage[s][0:64, :]))
                    ctx.op("vector", lambda e, s=slot, cta=cta: e.tensor_tensor(
                        rope["tmpb"][:, :], stage[s][:, :], cta, ALU.mult))
                    ctx.op("vector", lambda e, sta=sta: e.tensor_tensor(
                        rope["rot"][:, :], rope["rot"][:, :], sta, ALU.mult))
                    ctx.op("vector", lambda e, s=slot: e.tensor_tensor(
                        stage[s][:, :], rope["tmpb"][:, :], rope["rot"][:, :],
                        ALU.add), S["ve"])
                    ctx.wait("gpsimd", S["ve"], S["ve"].count)
                else:
                    ctx.wait("gpsimd", S["ev"], ev_tgt)
                qsc[oc] = scatter(scc[slot], sqkv[ck],
                                  idxs["iq_sc"][:, oc:oc + 1],
                                  stage[slot][:, :], (0, TOK))

            emit_mm(ctx, f"qkv{lay}{ck}", WQKV[lay], NFC, 48,
                    lambda ic: big[:, ic * TOK:(ic + 1) * TOK],
                    qkv_evac, S, pp=pp, goc=3, icb=8, wbufs=wbufs, tiled=True,
                    wchans=wch6)
            qkv_sc_done.append((scc[0].sem, scc[0].sem.count))
            qkv_sc_done.append((scc[1].sem, scc[1].sem.count))

            aux["tag"] = f"_{lay}{ck}"
            if lay == 0:
                emit_ffn0(ctx, big, xA2[ck], aux, S,
                          dict(w0T=w0T, bias0=bias0, gwT=gwT, ics=ics, ocs=ocs),
                          xA_copy, loaded=(ck == 1), wbufs=wbufs, tiled=True,
                          wchans=wch6)
            else:
                emit_ffn1(ctx, big, xA2[ck], aux, S,
                          dict(pwT=pwT, sclT=sclT, w1T=w1T, sel2=sel2),
                          xA_copy, loaded=(ck == 1), wbufs=wbufs, wchans=wch6)
        r = pcm.publish(qkv_sc_done)
        pcm.poll(r)
        pcm.barrier_engines()

        emit_flash_fused(ctx, S, aux, idxs, sqkv, sao4, tri, cv, gcc, scc,
                         rope, arB, stage, pp, pp3, pp4)
        r2 = pcm.publish([(scc[0].sem, scc[0].sem.count),
                          (scc[1].sem, scc[1].sem.count)])
        pcm.poll(r2)
        pcm.barrier_engines()

        for ck in range(2):
            g_done = []
            for icc in range(NFC):
                for j in range(2):
                    g_done.append(gather(
                        gcc[(icc * 2 + j) % 2],
                        big[:, icc * TOK + j * 512:icc * TOK + (j + 1) * 512],
                        sao4[2 * ck + j], idxs["iao_g"][:, icc:icc + 1],
                        (0, 512)))
            ctx.wait("tensor", gcc[0].sem, gcc[0].sem.count)
            ctx.wait("tensor", gcc[1].sem, gcc[1].sem.count)
            wo_acc = {}
            ctx.barrier_from("vector", *[c.sem for c in acc2])
            fst4 = [fstage[0][:, :], fstage[1][:, :],
                    wfb[0][:, 0:TOK], wfb[1][:, 0:TOK]]

            def wo_evac(oc, ps, og_tgt, ck=ck):
                slot = oc % 4
                ctx.wait("vector", S["peog"], og_tgt)
                if oc >= 4:
                    ctx.wait("vector", acc2[slot].sem, wo_acc[oc - 4])
                ctx.op("vector", lambda e, s=slot, p=ps:
                       e.tensor_copy(fst4[s], p[:, :]), S["ev"])
                ctx.wait("gpsimd", S["ev"], S["ev"].count)
                acc2[slot].go(xA2[ck][oc * 128:(oc + 1) * 128, :],
                              fst4[slot], accum_op=ALU.add)
                wo_acc[oc] = acc2[slot].sem.count

            emit_mm(ctx, f"wo{lay}{ck}", WO[lay], NFC, NFC,
                    lambda ic: big[:, ic * TOK:(ic + 1) * TOK], wo_evac,
                    S, pp=pp, goc=3, icb=8, wbufs=wbufs, tiled=True,
                    wchans=wch6)
            for _c in acc2:
                ctx.wait("gpsimd", _c.sem, _c.sem.count)
            ctx.dma("gpsimd", x2d[ck], xA2[ck], S["xc"])
        ctx.wait("gpsimd", S["xc"], S["xc"].count)
        if FDBG:
            for ck in range(2):
                ctx.dma("gpsimd", dbgs[lay][:, ck * TOK:(ck + 1) * TOK],
                        x2d[ck], S["st"])
            ctx.wait("gpsimd", S["st"], S["st"].count)

    emit_l2l3_fused(ctx, locals())
    for ch in acc2 + stc + scc + gcc:
        ctx.wait("gpsimd", ch.sem, ch.sem.count)
    ctx.barrier_from("gpsimd", S["xac"], S["st"])
    return ctx


def emit_flash_fused(ctx, S, aux, idxs, sqkv, sao4, tri, cv, gcc, scc,
                     rope, arB, stage, pp, pp3, pp4):
    K_s = arB[:, 0:4096]
    VT_s = arB[:, 4096:8192]
    Qh = [arB[:, 8192:12288], arB[:, 12288:16384]]
    ebuf = [arB[:, 16384:17408], arB[:, 17408:18432],
            stage[0][:, 0:TOK], stage[1][:, 0:TOK]]
    esum = arB[:, 18432:18944]
    aost = [arB[:, 18944:19456], arB[:, 19456:19968]]
    vtmp = arB[:, 19968:24064]
    bc_sb = aux["fstage"][0][:, 0:512]
    rd_row = aux["fstage"][1][0:1, 0:512]
    onesb, onesf, identb = aux["onesb"], aux["onesf"], aux["identb"]

    def gather(ch, dst_ap, src, idx_col, ncol):
        def f(e, dst=dst_ap, src=src, idxap=idx_col, ncol=ncol):
            e.indirect_dma_start(dst, None, src[0:128, 0:ncol],
                                 IndirectOffsetOnAxis(idxap, 0)).then_inc(ch.sem.h, 16)
        if ch.sem.count > 0:
            ctx.wait(ch.queue, ch.sem, ch.sem.count)
        ctx.emit("gpsimd", f)
        ch.sem.count += 16
        return ch.sem.count

    qload = {}
    for kvh in range(4):
        ctx.barrier_from("gpsimd", S["pe"])
        for w in range(2):
            for ckk in range(2):
                c0 = (w * 2 + ckk) * TOK
                gather(gcc[0], K_s[:, c0:c0 + TOK], sqkv[ckk],
                       idxs["iq_kv"][:, kvh * 2 + w:kvh * 2 + w + 1], TOK)
                gather(gcc[1], vtmp[:, c0:c0 + TOK], sqkv[ckk],
                       idxs["iq_v"][:, kvh * 2 + w:kvh * 2 + w + 1], TOK)
        ctx.wait("tensor", gcc[0].sem, gcc[0].sem.count)
        ctx.wait("tensor", gcc[1].sem, gcc[1].sem.count)
        for tb in range(4):
            if tb > 0:
                ctx.wait("tensor", S["ve"], S["ve"].count)
            for t in range(8):
                ctx.op("tensor", lambda e, tb=tb, t=t: e.transpose(
                    pp4[:, t * 128:(t + 1) * 128],
                    vtmp[:, (tb * 8 + t) * 128:(tb * 8 + t + 1) * 128],
                    identb[:, :]),
                    S["peog"] if t == 7 else None)
            ctx.wait("vector", S["peog"], S["peog"].count)
            ctx.op("vector", lambda e, tb=tb: e.tensor_copy(
                VT_s[:, tb * 1024:(tb + 1) * 1024], pp4[:, 0:1024]), S["ve"])
        ctx.wait("tensor", S["ve"], S["ve"].count)
        for qh in range(4):
            hl = kvh * 4 + qh
            qslot = hl % 2
            if hl >= 2:
                rs, rv = qload[hl - 2]
                ctx.wait("gpsimd", rs, rv)
            for w in range(2):
                for ckk in range(2):
                    c0 = (w * 2 + ckk) * TOK
                    gather(gcc[qslot], Qh[qslot][:, c0:c0 + TOK], sqkv[ckk],
                           idxs["iq_q"][:, hl * 2 + w:hl * 2 + w + 1], TOK)
            ctx.wait("tensor", gcc[qslot].sem, gcc[qslot].sem.count)
            for s in range(8):
                qtile = Qh[qslot][:, s * 512:(s + 1) * 512]
                nblk = 4 * (s + 1)
                groups = [list(range(g, min(g + 2, nblk))) for g in range(0, nblk, 2)]
                grp_tr = {}
                grp_ve = {}

                def emit_av(pgi, pgrp, nblk=nblk):
                    ctx.wait("tensor", S["ve"], grp_ve[pgi])
                    peb = ebuf[pgi % 4]
                    for i, j in enumerate(pgrp):
                        ctx.op("tensor", lambda e, i=i, peb=peb, kc=j * 128,
                               fs=(j == 0), l=(j == nblk - 1):
                               e.matmul(pp[2][:, 0:512], VT_s[:, kc:kc + 128],
                                        peb[:, i * 512:(i + 1) * 512],
                                        start=fs, stop=l))

                pend_av = []
                for gi, grp in enumerate(groups):
                    scp = pp[gi % 2]
                    if gi >= 2:
                        ctx.wait("tensor", S["tr"], grp_tr[gi - 2])
                    for i, j in enumerate(grp):
                        ctx.op("tensor", lambda e, scp=scp, i=i, kc=j * 128, qtile=qtile:
                               e.matmul(scp[:, i * 512:(i + 1) * 512],
                                        K_s[:, kc:kc + 128], qtile,
                                        start=True, stop=True),
                               S["pe"] if i == len(grp) - 1 else None)
                    pe_sc = S["pe"].count
                    if len(pend_av) >= 2:
                        emit_av(*pend_av.pop(0))
                    ctx.wait("scalar", S["pe"], pe_sc)
                    if gi >= 4:
                        ctx.wait("scalar", S["ve"], grp_ve[gi - 4])
                    eb = ebuf[gi % 4]
                    ng = len(grp)
                    ctx.op("scalar", lambda e, scp=scp, eb=eb, ng=ng:
                           e.activation(eb[:, 0:ng * 512], scp[:, 0:ng * 512],
                                        AF.Exp, bias=cv[:, CV_SHIFT01:CV_SHIFT01 + 1],
                                        scale=0.08838834764831845), S["tr"])
                    grp_tr[gi] = S["tr"].count
                    ctx.wait("vector", S["tr"], S["tr"].count)
                    for i, j in enumerate(grp):
                        esl = eb[:, i * 512:(i + 1) * 512]
                        jp = j - (nblk - 4)
                        if jp >= 0:
                            ctx.op("vector", lambda e, esl=esl, jp=jp:
                                   e.tensor_tensor(esl, esl,
                                                   tri[:, (3 - jp) * 128:(3 - jp) * 128 + 512],
                                                   ALU.mult))
                        if j == 0:
                            ctx.op("vector", lambda e, esl=esl:
                                   e.tensor_copy(esum[:, :], esl))
                        else:
                            ctx.op("vector", lambda e, esl=esl:
                                   e.tensor_tensor(esum[:, :], esum[:, :], esl,
                                                   ALU.add))
                    ctx.op("vector", lambda e: e.engine_nop(), S["ve"])
                    grp_ve[gi] = S["ve"].count
                    pend_av.append((gi, grp))
                for pa in pend_av:
                    emit_av(*pa)
                ctx.op("tensor", lambda e: e.matmul(pp3[0:1, 0:512], onesb[:, :],
                                                    esum[:, :], start=True,
                                                    stop=True), S["pe"])
                ctx.wait("vector", S["pe"], S["pe"].count)
                ctx.op("vector", lambda e: e.reciprocal(rd_row, pp3[0:1, 0:512]),
                       S["ve"])
                ctx.wait("tensor", S["ve"], S["ve"].count)
                ctx.op("tensor", lambda e: e.matmul(pp3[:, 0:512], onesf[:, :],
                                                    rd_row, start=True, stop=True),
                       S["pe"])
                ctx.wait("vector", S["pe"], S["pe"].count)
                aslot = s % 2
                if scc[aslot].sem.count > 0:
                    ctx.wait("vector", scc[aslot].sem, scc[aslot].sem.count)
                ctx.op("vector", lambda e: e.tensor_copy(bc_sb, pp3[:, 0:512]))
                ctx.op("vector", lambda e, aslot=aslot:
                       e.tensor_tensor(aost[aslot], pp[2][:, 0:512], bc_sb,
                                       ALU.mult), S["ve"])
                ev_t = S["ve"].count
                ctx.wait("tensor", S["ve"], ev_t)
                ctx.wait("gpsimd", S["ve"], ev_t)
                r = s // 4
                qb = s % 4
                def fsc(e, aslot=aslot, hl=hl, r=r, qb=qb):
                    e.indirect_dma_start(
                        sao4[qb][0:128, 0:512],
                        IndirectOffsetOnAxis(idxs["iao_sc"][:, hl * 2 + r:hl * 2 + r + 1], 0),
                        aost[aslot], None).then_inc(scc[aslot].sem.h, 16)
                if scc[aslot].sem.count > 0:
                    ctx.wait("gpsimd", scc[aslot].sem, scc[aslot].sem.count)
                ctx.emit("gpsimd", fsc)
                scc[aslot].sem.count += 16
            qload[hl] = (S["pe"], S["pe"].count)


def emit_l2l3_fused(ctx, L):
    S = L["S"]
    acc2, stc, lwch, b4ch = L["acc2"], L["stc"], L["lwch"], L["b4ch"]
    bmch, mbch = L["bmch"], L["mbch"]
    gcc, scc = L["gcc"], L["scc"]
    pcm, idxs, s3kv = L["pcm"], L["idxs"], L["s3kv"]
    pp, pp3, pp4 = L["pp"], L["pp3"], L["pp4"]
    pm2a, pm2b = L["pm2a"], L["pm2b"]
    big, hf, arB = L["big"], L["hf"], L["arB"]
    fstage, stage = L["fstage"], L["stage"]
    aux, cv = L["aux"], L["cv"]
    identb, onesb, onesf = L["identb"], L["onesb"], L["onesf"]
    ctb4, stb4 = L["ctb4"], L["stb4"]
    x2d, xA2, xout = L["x2d"], L["xA2"], L["xout"]
    mask8, w2aT, bias2 = L["mask8"], L["w2aT"], L["bias2"]
    indT, rep4, g4 = L["indT"], L["rep4"], L["g4"]
    w1qT, w3qT, w2qT = L["w1qT"], L["w3qT"], L["w2qT"]
    a1T, a3T, a2Tp = L["a1T"], L["a3T"], L["a2Tp"]
    b1T, b3T, b2T = L["b1T"], L["b3T"], L["b2T"]
    wqkv3, bqkv3, kern8 = L["wqkv3"], L["bqkv3"], L["kern8"]
    wo3, bo3 = L["wo3"], L["bo3"]
    bmd, bxd = L["bmd"], L["bxd"]
    wbufs = L["wbufs"]
    wch6 = L["wch6"]
    fst4 = [L["fstage"][0][:, :], L["fstage"][1][:, :],
            wbufs[0][0][:, 0:TOK], wbufs[0][1][:, 0:TOK]]

    q3loc = ctx.dram("q3loc", [1024, 2 * TOK], BF16)
    k3loc = ctx.dram("k3loc", [1024, 4096], BF16)
    v3tloc = ctx.dram("v3tloc", [4096, 1024], BF16)

    mb = arB[:, 0:4096]
    indsb = arB[:, 4096:8192]
    xf4 = arB[:, 8192:16384]
    xf8 = [arB[:, 16384:20480], arB[:, 8192:12288]]
    b4sb = arB[0:4, 20480:24576]
    gsb, usb, hsb = ctb4[:, 0:1024], ctb4[:, 1024:2048], ctb4[:, 2048:3072]
    w1b = [stb4[:, 0:1024], stb4[:, 1024:2048]]
    w3b = [stb4[:, 2048:3072], stb4[:, 3072:4096]]
    kmb = [stb4[:, 0:1024], stb4[:, 1024:2048]]
    b2sb = stb4[0:2, 0:4096]
    hstmp = mb[:, 0:TOK]
    cpl = ctx.sbuf("cpl", [128, 128], F32)
    rtmp = ctx.sbuf("rtmp", [4, 128], F32)
    bmf = ctx.sbuf("bmf", [128, 128], F32)
    binb = ctx.sbuf("binb", [128, 128], BF16)
    rep4b = ctx.sbuf("rep4b", [4, 128], BF16)
    g4b = ctx.sbuf("g4b", [128, 4], BF16)
    wmb = [ctx.sbuf(f"wmb{i}", [128, 384], BF16) for i in range(2)]
    biasb = ctx.sbuf("c2_bias", [128, NFC], F32)
    a1sb = ctx.sbuf("a1sb", [128, 64], BF16)
    a3sb = ctx.sbuf("a3sb", [128, 64], BF16)
    a2sb = ctx.sbuf("a2sb", [128, 172], BF16)
    b1sb = ctx.sbuf("b1sb", [2, 256], BF16)
    b3sb = ctx.sbuf("b3sb", [2, 256], BF16)
    mid1b = arB[0:2, 1024:2048]
    mid3b = arB[0:2, 2048:3072]
    mid2b = arB[0:2, 3072:4096]
    bq3 = ctx.sbuf("bq3", [128, 24], F32)
    bo3sb = ctx.sbuf("bo3sb", [128, NFC], F32)

    ctx.barrier_from("gpsimd", S["pe"], S["ve"], S["peog"])
    ctx.op("gpsimd", lambda e: e.memset(b4sb[:, :], 0.0))
    for dst, srcp in ((indsb, indT), (rep4b, rep4), (g4b, g4), (a2sb, a2Tp),
                      (biasb, bias2), (bq3, bqkv3), (bo3sb, bo3)):
        ctx.dma("gpsimd", dst[:, :], srcp, S["tb"])
    ctx.dma("gpsimd", a1sb[:, :].rearrange("p (c r) -> p c r", r=2),
            a1T.rearrange("(c p) r -> p c r", p=128), S["tb"])
    ctx.dma("gpsimd", a3sb[:, :].rearrange("p (c r) -> p c r", r=2),
            a3T.rearrange("(c p) r -> p c r", p=128), S["tb"])
    tabs2 = S["tb"].count
    for eng in ("tensor", "vector", "scalar"):
        ctx.wait(eng, S["tb"], tabs2)

    ctx.barrier_from("tensor", S["ve"])
    ctx.barrier_from("scalar", S["ve"])
    mb_dve = 0
    for rc in range(NFC):
        if rc > 0:
            ctx.wait("gpsimd", S["ve2"], mb_dve)
        mbch.go(mb[:, :], mask8[rc * 128:(rc + 1) * 128, :])
        ctx.wait("vector", mbch.sem, mbch.sem.count)
        ctx.op("vector", lambda e: e.tensor_reduce(
            cpl[:, :].rearrange("p (c o) -> p c o", o=1),
            mb[:, :].rearrange("p (c w) -> p c w", w=32),
            axis=AX.X, op=ALU.add), S["ve2"])
        mb_dve = S["ve2"].count
        ctx.wait("tensor", S["ve2"], mb_dve)
        ctx.op("vector", lambda e: e.tensor_copy(mb[:, 0:128], cpl[:, :]), S["ve2"])
        ctx.wait("tensor", S["ve2"], S["ve2"].count)
        ctx.op("tensor", lambda e: e.matmul(pp3[0:4, 0:512], g4b[:, :],
                                            mb[:, 0:512], start=True, stop=True),
               S["peog"])
        ctx.wait("vector", S["peog"], S["peog"].count)
        if rc > 0:
            ctx.wait("vector", bmch.sem, bmch.sem.count)
        ctx.op("vector", lambda e: e.tensor_copy(rtmp[:, :], pp3[0:4, 0:128]),
               S["ve2"])
        ctx.wait("sync", S["ve2"], S["ve2"].count)
        bmch.go(bmd[4 * rc:4 * rc + 4, :], rtmp[:, :])
    ctx.wait("gpsimd", bmch.sem, bmch.sem.count)
    ctx.dma("gpsimd", bmf[:, :], bmd, S["ms"])
    ctx.wait("scalar", S["ms"], S["ms"].count)
    ctx.op("scalar", lambda e: e.activation(binb[:, :], bmf[:, :], AF.Sign), S["tr"])
    ctx.wait("tensor", S["tr"], S["tr"].count)
    for hblk in range(8):
        ctx.op("tensor", lambda e, hblk=hblk: e.matmul(
            pp3[:, 0:512], binb[:, :], indsb[:, hblk * 512:(hblk + 1) * 512],
            start=True, stop=True), S["peog"])
        ctx.wait("scalar", S["peog"], S["peog"].count)
        ctx.op("scalar", lambda e, hblk=hblk: e.activation(
            stage[hblk % 2][:, 0:512], pp3[:, 0:512], AF.Identity,
            bias=0.0, scale=1.0), S["tr"])
        ctx.wait("sync", S["tr"], S["tr"].count)
        stc[hblk % 2].go(bxd[:, hblk * 512:(hblk + 1) * 512],
                         stage[hblk % 2][:, 0:512])
        ctx.wait("tensor", stc[hblk % 2].sem, stc[hblk % 2].sem.count)
    for chn in stc:
        ctx.wait("gpsimd", chn.sem, chn.sem.count)

    sems_ln = dict(xl=S["xl"], pe=S["pe"], ve=S["ve"], tr=S["tr"], ac=S["ve"])
    wpi = [0]

    def wgo6(dst_ap, src_ap):
        ch = wch6[wpi[0] % len(wch6)]
        wpi[0] += 1
        ch.go(dst_ap, src_ap)
        return (ch.sem, ch.sem.count)

    for ck in range(2):
        ctx.barrier_from("tensor", S["ev"], S["ve"])
        emit_ln(ctx, f"l2ln{ck}", x2d[ck], big, aux, sems_ln)
        ctx.wait("gpsimd", S["ve"], S["ve"].count)

        ogs = [list(range(g, min(g + 3, NFC))) for g in range(0, NFC, 3)]
        ctx.barrier_from("vector", *[c.sem for c in acc2])
        at_acc = {}
        lrel = {}
        ev_prev = []
        mexp_t = {}
        wl_t = {}
        b4_t = {}
        NLM = len(ogs) * NFC
        b4s = [arB[0:4, 20480:20992], arB[0:4, 20992:21504]]

        def issue_at(l):
            gi_l, ic_l = divmod(l, NFC)
            og_l = ogs[gi_l]
            nc_l = len(og_l) * 128
            c0_l = og_l[0] * 128
            sl = l % 2
            if l >= 2:
                rs, rv = lrel[l - 2]
                ctx.wait("gpsimd", rs, rv)
            wl_t[l] = wgo6(stage[sl][:, 0:nc_l],
                           w2aT[ic_l * 128:(ic_l + 1) * 128, c0_l:c0_l + nc_l])
            if l >= 2:
                ctx.wait("gpsimd", S["peog"], mexp_t[l - 2])
            b4ch.go(b4s[sl][:, 0:nc_l],
                    bxd[4 * ic_l:4 * ic_l + 4, c0_l:c0_l + nc_l])
            b4_t[l] = b4ch.sem.count

        mps = [pp3, pp3]

        def emit_mexp(l):
            sl = l % 2
            ctx.wait("tensor", b4ch.sem, b4_t[l])
            ctx.op("tensor", lambda e, sl=sl: e.matmul(
                mps[sl][:, 0:512], rep4b[:, :], b4s[sl][:, 0:512], start=True,
                stop=True), S["peog"])
            mexp_t[l] = S["peog"].count

        for gi, og in enumerate(ogs):
            ncols = len(og) * 128
            for ic in range(NFC):
                lidx = gi * NFC + ic
                slot = lidx % 2
                if lidx == 0:
                    issue_at(0)
                    issue_at(1)
                    emit_mexp(0)
                ctx.wait("vector", S["peog"], mexp_t[lidx])
                ctx.wait("vector", wl_t[lidx][0], wl_t[lidx][1])
                ctx.op("vector", lambda e, s=slot, n=ncols: e.tensor_tensor(
                    wmb[s][:, 0:n], stage[s][:, 0:n], mps[s][:, 0:n], ALU.mult),
                    S["ve"])
                ctx.wait("tensor", S["ve"], S["ve"].count)
                if gi > 0 and ic == 0:
                    ctx.wait("tensor", S["ev"], ev_prev[gi - 1])
                for oi, oc in enumerate(og):
                    for h2 in range(2):
                        t0, t1 = h2 * 512, (h2 + 1) * 512
                        last = (ic == NFC - 1)
                        sem1 = None
                        if oi == len(og) - 1 and h2 == 1:
                            sem1 = S["peog"] if last else S["pe"]
                        ctx.op("tensor", lambda e, oi=oi, s=slot, o0=oi * 128,
                               t0=t0, t1=t1, fs=(ic == 0), l=last, ic=ic:
                               e.matmul(pp[oi][:, t0:t1], wmb[s][:, o0:o0 + 128],
                                        big[:, ic * TOK + t0:ic * TOK + t1],
                                        start=fs, stop=l), sem1)
                lrel[lidx] = (S["peog"] if ic == NFC - 1 else S["pe"],
                              (S["peog"] if ic == NFC - 1 else S["pe"]).count)
                if lidx + 2 < NLM:
                    issue_at(lidx + 2)
                if lidx + 1 < NLM:
                    emit_mexp(lidx + 1)
            og_tgt = S["peog"].count
            for oi, oc in enumerate(og):
                slot2 = oc % 4
                ctx.wait("vector", S["peog"], og_tgt)
                if oc >= 4:
                    ctx.wait("vector", acc2[slot2].sem, at_acc[oc - 4])
                ctx.op("vector", lambda e, s=slot2, oi=oi:
                       e.tensor_copy(fst4[s], pp[oi][:, :]))
                ctx.op("vector", lambda e, s=slot2, oc=oc:
                       e.tensor_scalar(fst4[s], fst4[s],
                                       biasb[:, oc:oc + 1], None, ALU.add), S["ev"])
                ctx.wait("gpsimd", S["ev"], S["ev"].count)
                acc2[slot2].go(xA2[ck][oc * 128:(oc + 1) * 128, :],
                               fst4[slot2], accum_op=ALU.add)
                at_acc[oc] = acc2[slot2].sem.count
            ev_prev.append(S["ev"].count)

        for f in range(8):
            d = xf4[:, f * TOK:(f + 1) * TOK]
            ctx.op("vector", lambda e, d=d, f=f: e.tensor_tensor(
                d, big[:, f * TOK:(f + 1) * TOK],
                big[:, (f + 8) * TOK:(f + 9) * TOK], ALU.add))
            ctx.op("vector", lambda e, d=d, f=f: e.tensor_tensor(
                d, d, big[:, (f + 16) * TOK:(f + 17) * TOK], ALU.add))
            ctx.op("vector", lambda e, d=d, f=f: e.tensor_tensor(
                d, d, big[:, (f + 24) * TOK:(f + 25) * TOK], ALU.add),
                S["ve"] if f == 7 else None)

        ctx.barrier_from("tensor", S["ev"], S["ve"])
        for mb_, aT, ppx in ((mid1b, a1sb, pp[0]), (mid3b, a3sb, pp[1])):
            for ic in range(NFC):
                for h2 in range(2):
                    t0, t1 = h2 * 512, (h2 + 1) * 512
                    ctx.op("tensor", lambda e, aT=aT, ic=ic, t0=t0, t1=t1, ppx=ppx:
                           e.matmul(ppx[0:2, t0:t1], aT[:, 2 * ic:2 * ic + 2],
                                    big[:, ic * TOK + t0:ic * TOK + t1],
                                    start=(ic == 0), stop=(ic == NFC - 1)),
                           S["peog"] if (ic == NFC - 1 and h2 == 1) else None)
            ctx.wait("vector", S["peog"], S["peog"].count)
            ctx.op("vector", lambda e, mb_=mb_, ppx=ppx: e.tensor_scalar(
                mb_[:, :], ppx[0:2, :], 0.5, None, ALU.mult), S["ve"])
        ctx.wait("tensor", S["ve"], S["ve"].count)

        NOC = 86
        hw_written = set()
        yrel = {}
        ylora = {}
        yw_t = {}
        yb_t = {}
        sil_t = None

        def issue_y(l):
            sl = l % 2
            if l >= 2:
                rs, rv = yrel[l - 2]
                ctx.wait("gpsimd", rs, rv)
            yw_t[l] = [wgo6(w1b[sl][:, :], w1qT[:, l * 1024:(l + 1) * 1024]),
                       wgo6(w3b[sl][:, :], w3qT[:, l * 1024:(l + 1) * 1024])]
            if l >= 2:
                ctx.wait("gpsimd", S["peog"], ylora[l - 2])
            b4ch.go(b1sb[:, sl * 128:(sl + 1) * 128],
                    b1T[:, l * 128:(l + 1) * 128])
            b4ch.go(b3sb[:, sl * 128:(sl + 1) * 128],
                    b3T[:, l * 128:(l + 1) * 128])
            yb_t[l] = b4ch.sem.count

        for oc in range(NOC):
            slot = oc % 2
            if oc == 0:
                issue_y(0)
            if oc + 1 < NOC:
                issue_y(oc + 1)
            for ws, wv in yw_t[oc]:
                ctx.wait("tensor", ws, wv)
            ctx.wait("tensor", b4ch.sem, yb_t[oc])
            if oc >= 1:
                ctx.wait("tensor", S["tr"], sil_t[0])
                ctx.wait("tensor", S["ve"], sil_t[1])
            for wbuf, ppx, midb, bsb in ((w1b[slot], pp[0], mid1b, b1sb),
                                         (w3b[slot], pp[1], mid3b, b3sb)):
                for ic in range(8):
                    for h2 in range(2):
                        t0, t1 = h2 * 512, (h2 + 1) * 512
                        ctx.op("tensor", lambda e, w=wbuf, ic=ic, t0=t0, t1=t1,
                               ppx=ppx:
                               e.matmul(ppx[:, t0:t1],
                                        w[:, ic * 128:(ic + 1) * 128],
                                        xf4[:, ic * TOK + t0:ic * TOK + t1],
                                        start=(ic == 0), stop=False),
                               S["pe"] if (ic == 7 and h2 == 1) else None)
                for h2 in range(2):
                    t0, t1 = h2 * 512, (h2 + 1) * 512
                    ctx.op("tensor", lambda e, b=bsb, m=midb, t0=t0, t1=t1,
                           ppx=ppx, s=slot:
                           e.matmul(ppx[:, t0:t1], b[:, s * 128:(s + 1) * 128],
                                    m[:, t0:t1], start=False, stop=True),
                           S["peog"] if h2 == 1 else None)
            yrel[oc] = (S["pe"], S["pe"].count)
            ylora[oc] = S["peog"].count
            y_done = S["peog"].count
            ctx.wait("scalar", S["peog"], y_done)
            ctx.wait("vector", S["peog"], y_done)
            ctx.op("scalar", lambda e: e.activation(gsb[:, :], pp[0][:, :],
                                                    AF.Silu), S["tr"])
            sil_a = S["tr"].count
            ctx.op("vector", lambda e: e.tensor_copy(usb[:, :], pp[1][:, :]),
                   S["ve"])
            sil_b = S["ve"].count
            sil_t = (sil_a, sil_b)
            ctx.wait("vector", S["tr"], sil_a)
            ctx.op("vector", lambda e: e.tensor_tensor(hsb[:, :], gsb[:, :],
                                                       usb[:, :], ALU.mult),
                   S["ve"])
            ctx.wait("tensor", S["ve"], S["ve"].count)
            for h2 in range(2):
                t0, t1 = h2 * 512, (h2 + 1) * 512
                pm = pm2a if h2 == 0 else pm2b
                ctx.op("tensor", lambda e, pm=pm, t0=t0, t1=t1, oc=oc:
                       e.matmul(pm[:, :], a2sb[:, 2 * oc:2 * oc + 2],
                                hsb[:, t0:t1],
                                start=(oc == 0), stop=(oc == NOC - 1),
                                skip_group_check=True),
                       S["pe2"] if (oc == NOC - 1 and h2 == 1) else None)
            t0g = (128 * oc) % 2752
            pieces = []
            src0 = 0
            rem = 128
            cur = t0g
            while rem > 0:
                tile = cur // 128
                off = cur % 128
                ln = min(rem, 128 - off, 2752 - cur)
                pieces.append((tile, off, src0, ln))
                src0 += ln
                cur = (cur + ln) % 2752
                rem -= ln
            for tile, off, s0, ln in pieces:
                dstap = hf[off:off + ln, tile * TOK:(tile + 1) * TOK]
                srcap = hsb[s0:s0 + ln, :]
                covered = any(k[0] == tile and k[1] <= off and
                              k[1] + k[2] >= off + ln for k in hw_written)
                if covered:
                    if off != s0:
                        ctx.op("vector", lambda e, s=srcap, off=off, ln=ln:
                               e.tensor_copy(hstmp[off:off + ln, :], s))
                        srcap = hstmp[off:off + ln, :]
                    ctx.op("vector", lambda e, d=dstap, s=srcap:
                           e.tensor_tensor(d, d, s, ALU.add), S["ve"])
                else:
                    ctx.op("vector", lambda e, d=dstap, s=srcap:
                           e.tensor_copy(d, s), S["ve"])
                    hw_written.add((tile, off, ln))
        ctx.wait("vector", S["pe2"], S["pe2"].count)
        ctx.op("vector", lambda e: e.tensor_scalar(mid2b[:, 0:512], pm2a[:, :],
                                                   0.5, None, ALU.mult))
        ctx.op("vector", lambda e: e.tensor_scalar(mid2b[:, 512:1024],
                                                   pm2b[:, :], 0.5, None,
                                                   ALU.mult), S["ve"])
        ctx.wait("tensor", S["ve"], S["ve"].count)

        ctx.barrier_from("gpsimd", S["pe"], S["peog"])
        ctx.dma("gpsimd", b2sb[:, :], b2T, S["ix"])
        ctx.wait("tensor", S["ix"], S["ix"].count)
        ctx.barrier_from("vector", *[c.sem for c in acc2])
        dn_acc = {}
        lrel = {}
        dn_t = {}
        ev_prev2 = []
        ogs = [list(range(g, min(g + 3, NFC), 1)) for g in range(0, NFC, 3)]
        NDN = len(ogs) * 22

        def issue_dn(l):
            gi_l, ic_l = divmod(l, 22)
            og_l = ogs[gi_l]
            nc_l = len(og_l) * 128
            c0_l = og_l[0] * 128
            r_l = 128 if ic_l < 21 else 64
            sl = l % 2
            if l >= 2:
                rs, rv = lrel[l - 2]
                ctx.wait("gpsimd", rs, rv)
            dn_t[l] = wgo6(stage[sl][0:r_l, 0:nc_l],
                           w2qT[ic_l * 128:ic_l * 128 + r_l, c0_l:c0_l + nc_l])

        for gi, og in enumerate(ogs):
            ncols = len(og) * 128
            col0 = og[0] * 128
            for ic in range(22):
                rows = 128 if ic < 21 else 64
                lidx = gi * 22 + ic
                slot = lidx % 2
                if lidx == 0:
                    issue_dn(0)
                if lidx + 1 < NDN:
                    issue_dn(lidx + 1)
                ctx.wait("tensor", dn_t[lidx][0], dn_t[lidx][1])
                if gi > 0 and ic == 0:
                    ctx.wait("tensor", S["ev"], ev_prev2[gi - 1])
                for oi, oc in enumerate(og):
                    for h2 in range(2):
                        t0, t1 = h2 * 512, (h2 + 1) * 512
                        sem1 = None
                        if oi == len(og) - 1 and h2 == 1:
                            sem1 = S["pe"]
                        ctx.op("tensor", lambda e, s=slot, oi=oi, o0=oi * 128,
                               r=rows, ic=ic, t0=t0, t1=t1:
                               e.matmul(pp[oi][:, t0:t1],
                                        stage[s][0:r, o0:o0 + 128],
                                        hf[0:r, ic * TOK + t0:ic * TOK + t1],
                                        start=(ic == 0), stop=False), sem1)
                lrel[lidx] = (S["pe"], S["pe"].count)
            for oi, oc in enumerate(og):
                for h2 in range(2):
                    t0, t1 = h2 * 512, (h2 + 1) * 512
                    ctx.op("tensor", lambda e, oi=oi, oc=oc, t0=t0, t1=t1:
                           e.matmul(pp[oi][:, t0:t1],
                                    b2sb[:, oc * 128:oc * 128 + 128],
                                    mid2b[:, t0:t1], start=False, stop=True),
                           S["peog"] if (oi == len(og) - 1 and h2 == 1) else None)
            og_tgt = S["peog"].count
            for oi, oc in enumerate(og):
                slot2 = oc % 4
                ctx.wait("vector", S["peog"], og_tgt)
                if oc >= 4:
                    ctx.wait("vector", acc2[slot2].sem, dn_acc.get(oc - 4, 0))
                ctx.op("vector", lambda e, s=slot2, oi=oi:
                       e.tensor_copy(fst4[s], pp[oi][:, :]), S["ev"])
                ctx.wait("gpsimd", S["ev"], S["ev"].count)
                acc2[slot2].go(xA2[ck][oc * 128:(oc + 1) * 128, :],
                               fst4[slot2], accum_op=ALU.add)
                dn_acc[oc] = acc2[slot2].sem.count
            ev_prev2.append(S["ev"].count)
        for _c in acc2:
            ctx.wait("gpsimd", _c.sem, _c.sem.count)

    q3kv_done = []
    for ck in range(2):
        ctx.dma("gpsimd", x2d[ck], xA2[ck], S["xac"])
        x3copy = S["xac"].count
        ctx.wait("gpsimd", S["xac"], x3copy)
        if FDBG:
            ctx.dma("gpsimd", L["dbgs"][2][:, ck * TOK:(ck + 1) * TOK],
                    x2d[ck], S["st"])
            ctx.wait("gpsimd", S["st"], S["st"].count)
        ctx.barrier_from("tensor", S["ev"], S["ve"])
        emit_ln(ctx, f"l3ln{ck}", x2d[ck], big, aux, sems_ln)
        ctx.wait("gpsimd", S["ve"], S["ve"].count)
        for f in range(4):
            d = xf8[ck][:, f * TOK:(f + 1) * TOK]
            ctx.op("vector", lambda e, d=d, f=f: e.tensor_tensor(
                d, big[:, f * TOK:(f + 1) * TOK],
                big[:, (f + 4) * TOK:(f + 5) * TOK], ALU.add))
            for r in range(2, 8):
                ctx.op("vector", lambda e, d=d, f=f, r=r: e.tensor_tensor(
                    d, d, big[:, (f + 4 * r) * TOK:(f + 4 * r + 1) * TOK],
                    ALU.add), S["ve"] if (f == 3 and r == 7) else None)
        ctx.wait("tensor", S["ve"], S["ve"].count)
        for i3 in range(3):
            for h2 in range(2):
                t0, t1 = h2 * 512, (h2 + 1) * 512
                ctx.op("tensor", lambda e, i3=i3, t0=t0, t1=t1:
                       e.matmul(pp[i3][:, t0:t1], identb[:, :], big[:, 0:512],
                                start=True, stop=True),
                       S["peog"] if (i3 == 2 and h2 == 1) else None)
        ctx.wait("tensor", S["peog"], S["peog"].count)
        q3ev = {}

        def qkv3_evac(oc, ps, og_tgt, ck=ck, q3ev=q3ev):
            slot = oc % 2
            ctx.wait("scalar", S["peog"], og_tgt)
            if oc >= 2:
                ps_, pv = q3ev[oc - 2]
                ctx.wait("scalar", ps_, pv)
            ctx.op("scalar", lambda e, s=slot, p=ps, oc=oc:
                   e.activation(stage[s][:, 0:TOK], p[:, :], AF.Identity,
                                bias=bq3[:, oc:oc + 1], scale=1.0), S["ev"])
            if oc < 8:
                ctx.wait("sync", S["ev"], S["ev"].count)
                stc[slot].go(q3loc[oc * 128:(oc + 1) * 128,
                                   ck * TOK:(ck + 1) * TOK],
                             stage[slot][:, 0:TOK])
                q3ev[oc] = (stc[slot].sem, stc[slot].sem.count)
            else:
                ctx.wait("gpsimd", S["ev"], S["ev"].count)
                j = oc - 8
                ch = scc[slot]
                if ch.sem.count > 0:
                    ctx.wait("gpsimd", ch.sem, ch.sem.count)

                def fsc(e, j=j, s=slot, ck=ck, ch=ch):
                    e.indirect_dma_start(
                        s3kv[ck][0:128, 0:TOK],
                        IndirectOffsetOnAxis(idxs["i3_sc"][:, j:j + 1], 0),
                        stage[s][:, 0:TOK], None).then_inc(ch.sem.h, 16)
                ctx.emit("gpsimd", fsc)
                ch.sem.count += 16
                q3ev[oc] = (ch.sem, ch.sem.count)

        emit_mm(ctx, f"q3c{ck}", wqkv3, NFC, 24,
                lambda ic: big[:, ic * TOK:(ic + 1) * TOK], qkv3_evac,
                S, pp=pp, goc=3, icb=8, wbufs=wbufs, tiled=True,
                wchans=L['wch6'])
    q3kv_done = [(scc[0].sem, scc[0].sem.count),
                 (scc[1].sem, scc[1].sem.count),
                 (stc[0].sem, stc[0].sem.count),
                 (stc[1].sem, stc[1].sem.count)]
    r3 = pcm.publish(q3kv_done)

    for ck in range(2):
        ctx.barrier_from("tensor", S["ev"])
        ctx.barrier_from("vector", *[c.sem for c in acc2])
        f3_acc = {}
        lrel2 = {}
        f3_t = {}
        ev_prev3 = []
        ogs3 = [list(range(g, min(g + 2, NFC))) for g in range(0, NFC, 2)]

        def issue_f3(l):
            og_l = ogs3[l]
            nc_l = len(og_l) * 128
            c0_l = og_l[0] * 128
            sl = l % 2
            if l >= 2:
                rs, rv = lrel2[l - 2]
                ctx.wait("gpsimd", rs, rv)
            f3_t[l] = wgo6(kmb[sl][:, 0:4 * nc_l]
                           .rearrange("p (i c) -> p i c", i=4),
                           kern8[:, c0_l:c0_l + nc_l]
                           .rearrange("(i p) c -> p i c", p=128))

        for gi, og in enumerate(ogs3):
            ncols = len(og) * 128
            slot = gi % 2
            if gi == 0:
                issue_f3(0)
            if gi + 1 < len(ogs3):
                issue_f3(gi + 1)
            ctx.wait("tensor", f3_t[gi][0], f3_t[gi][1])
            if gi > 0:
                ctx.wait("tensor", S["ev"], ev_prev3[gi - 1])
            for ic in range(4):
                for oi, oc in enumerate(og):
                    for h2 in range(2):
                        t0, t1 = h2 * 512, (h2 + 1) * 512
                        sem1 = (S["peog"] if (ic == 3 and oi == len(og) - 1
                                              and h2 == 1) else None)
                        ctx.op("tensor", lambda e, s=slot, ic=ic, oi=oi,
                               o0=ic * ncols + oi * 128, t0=t0, t1=t1,
                               l=(ic == 3), ck=ck:
                               e.matmul(pp[oi][:, t0:t1], kmb[s][:, o0:o0 + 128],
                                        xf8[ck][:, ic * TOK + t0:ic * TOK + t1],
                                        start=(ic == 0), stop=l), sem1)
            lrel2[gi] = (S["peog"], S["peog"].count)
            og_tgt = S["peog"].count
            for oi, oc in enumerate(og):
                slot2 = oc % 4
                ctx.wait("vector", S["peog"], og_tgt)
                if oc >= 4:
                    ctx.wait("vector", acc2[slot2].sem, f3_acc.get(oc - 4, 0))
                ctx.op("vector", lambda e, s=slot2, oi=oi:
                       e.tensor_copy(fst4[s], pp[oi][:, :]), S["ev"])
                ctx.wait("gpsimd", S["ev"], S["ev"].count)
                acc2[slot2].go(xA2[ck][oc * 128:(oc + 1) * 128, :],
                               fst4[slot2], accum_op=ALU.add)
                f3_acc[oc] = acc2[slot2].sem.count
            ev_prev3.append(S["ev"].count)
        for _c in acc2:
            ctx.wait("gpsimd", _c.sem, _c.sem.count)

    pcm.poll(r3)
    pcm.barrier_engines()
    _l3_flash_fused(ctx, L, locals())


def _l3_flash_fused(ctx, L, M):
    S = L["S"]
    stc, lwch, gcc = L["stc"], L["lwch"], L["gcc"]
    acc2 = L["acc2"]
    idxs, s3kv = L["idxs"], L["s3kv"]
    pp, pp3, pp4 = L["pp"], L["pp3"], L["pp4"]
    big, arB = L["big"], L["arB"]
    fstage, stage = L["fstage"], L["stage"]
    cv, identb, onesb, onesf = L["cv"], L["identb"], L["onesb"], L["onesf"]
    ctb4, stb4 = L["ctb4"], L["stb4"]
    xA2, xout = L["xA2"], L["xout"]
    wo3 = L["wo3"]
    wbufs = L["wbufs"]
    q3loc, k3loc, v3tloc = M["q3loc"], M["k3loc"], M["v3tloc"]
    bo3sb = M["bo3sb"]
    fst4 = M["fst4"]

    wch6 = M["wch6"]
    wpi3 = [0]

    def wgo3(dst_ap, src_ap):
        ch = wch6[wpi3[0] % len(wch6)]
        wpi3[0] += 1
        ch.go(dst_ap, src_ap)
        return (ch.sem, ch.sem.count)

    stg = [big[:, 16384 + i * 2048:16384 + (i + 1) * 2048] for i in range(2)]
    asm = [big[:, 20480 + i * 2048:20480 + (i + 1) * 2048] for i in range(2)]
    efull = arB[:, 0:16384]
    esum3 = arB[:, 16384:16896]
    v3b = [stb4[:, 0:4096], arB[:, 20480:24576]]
    kst = [stb4[:, 0:2048], stb4[:, 2048:4096]]
    q3t = ctb4[:, 0:4096]
    bc_sb = fstage[0][:, 0:512]
    rd_row = fstage[1][0:1, 0:512]

    def gather(ch, dst_ap, src, idx_col):
        def f(e, dst=dst_ap, src=src, idxap=idx_col):
            e.indirect_dma_start(dst, None, src[0:128, 0:TOK],
                                 IndirectOffsetOnAxis(idxap, 0)).then_inc(
                ch.sem.h, 16)
        if ch.sem.count > 0:
            ctx.wait(ch.queue, ch.sem, ch.sem.count)
        ctx.emit("gpsimd", f)
        ch.sem.count += 16
        return ch.sem.count

    st_done = {}
    for dc in range(8):
        for w in range(2):
            sl = (dc * 2 + w) % 2
            prev = st_done.get((dc * 2 + w) - 2)
            if prev is not None:
                ctx.wait("gpsimd", prev[0], prev[1])
            for ckk in range(2):
                gather(gcc[sl], stg[sl][:, ckk * TOK:(ckk + 1) * TOK],
                       s3kv[ckk], idxs["i3_k"][:, dc * 2 + w:dc * 2 + w + 1])
            ctx.wait("sync", gcc[sl].sem, gcc[sl].sem.count)
            stc[sl].go(k3loc[dc * 128:(dc + 1) * 128, w * 2048:(w + 1) * 2048],
                       stg[sl])
            st_done[dc * 2 + w] = (stc[sl].sem, stc[sl].sem.count)
    vi = 0
    for dc in range(8):
        for w in range(2):
            sl = vi % 2
            prev = st_done.get(16 + vi - 2)
            if prev is not None:
                ctx.wait("gpsimd", prev[0], prev[1])
            for ckk in range(2):
                gather(gcc[sl], stg[sl][:, ckk * TOK:(ckk + 1) * TOK],
                       s3kv[ckk], idxs["i3_v"][:, dc * 2 + w:dc * 2 + w + 1])
            ctx.wait("tensor", gcc[sl].sem, gcc[sl].sem.count)
            if vi >= 2:
                ctx.wait("tensor", stc[sl].sem, st_done[16 + vi - 2][1])
            for tb in range(2):
                if not (vi == 0 and tb == 0):
                    ctx.wait("tensor", S["ve"], S["ve"].count)
                for t in range(8):
                    ctx.op("tensor", lambda e, sl=sl, tb=tb, t=t: e.transpose(
                        pp4[:, t * 128:(t + 1) * 128],
                        stg[sl][:, (tb * 8 + t) * 128:(tb * 8 + t + 1) * 128],
                        identb[:, :]),
                        S["peog"] if t == 7 else None)
                ctx.wait("vector", S["peog"], S["peog"].count)
                ctx.op("vector", lambda e, sl=sl, tb=tb: e.tensor_copy(
                    asm[sl][:, tb * 1024:(tb + 1) * 1024], pp4[:, 0:1024]),
                    S["ve"])
            ctx.wait("sync", S["ve"], S["ve"].count)
            stc[sl].go(v3tloc[w * 2048:(w + 1) * 2048, dc * 128:(dc + 1) * 128]
                       .rearrange("(j p) c -> p j c", p=128),
                       asm[sl][:, :].rearrange("p (j c) -> p j c", j=16))
            st_done[16 + vi] = (stc[sl].sem, stc[sl].sem.count)
            vi += 1
    for chn in stc:
        ctx.wait("gpsimd", chn.sem, chn.sem.count)
    ctx.wait("tensor", S["ve"], S["ve"].count)

    qch = L["mbch"]
    for s in range(4):
        ctx.barrier_from("gpsimd", S["pe"])
        qch.go(q3t[:, :].rearrange("p (dc c) -> p dc c", c=512),
               q3loc[:, s * 512:(s + 1) * 512]
               .rearrange("(dc p) c -> p dc c", p=128))
        ctx.wait("tensor", qch.sem, qch.sem.count)
        groups = [(g, g + 1) for g in range(0, 32, 2)]
        grp_tr = {}
        grp_ld = {}
        for gi, grp in enumerate(groups):
            scp = pp[gi % 2]
            sl = gi % 2
            if gi >= 2:
                ctx.wait("gpsimd", S["pe"], grp_ld[gi - 2])
            kw = wgo3(kst[sl][:, :].rearrange("p (dc c) -> p dc c", c=256),
                      k3loc[:, grp[0] * 128:grp[0] * 128 + 256]
                      .rearrange("(dc p) c -> p dc c", p=128))
            ctx.wait("tensor", kw[0], kw[1])
            if gi >= 2:
                ctx.wait("tensor", S["tr"], grp_tr[gi - 2])
            for i in range(2):
                for dc in range(8):
                    ctx.op("tensor", lambda e, scp=scp, sl=sl, i=i, dc=dc:
                           e.matmul(scp[:, i * 512:(i + 1) * 512],
                                    kst[sl][:, dc * 256 + i * 128:
                                            dc * 256 + i * 128 + 128],
                                    q3t[:, dc * 512:(dc + 1) * 512],
                                    start=(dc == 0), stop=(dc == 7)),
                           S["pe"] if (i == 1 and dc == 7) else None)
            grp_ld[gi] = S["pe"].count
            ctx.wait("scalar", S["pe"], S["pe"].count)
            ctx.op("scalar", lambda e, scp=scp, c0=grp[0] * 512:
                   e.activation(efull[:, c0:c0 + 1024], scp[:, 0:1024],
                                AF.Exp, bias=cv[:, CV_SHIFT3:CV_SHIFT3 + 1],
                                scale=0.03125), S["tr"])
            grp_tr[gi] = S["tr"].count
            ctx.wait("vector", S["tr"], S["tr"].count)
            for kb in grp:
                esl = efull[:, kb * 512:(kb + 1) * 512]
                if kb == 0:
                    ctx.op("vector", lambda e, esl=esl:
                           e.tensor_copy(esum3[:, :], esl))
                else:
                    ctx.op("vector", lambda e, esl=esl:
                           e.tensor_tensor(esum3[:, :], esum3[:, :], esl,
                                           ALU.add))
            ctx.op("vector", lambda e: e.engine_nop(), S["ve"])
        ctx.wait("tensor", S["ve"], S["ve"].count)
        ctx.op("tensor", lambda e: e.matmul(pp3[0:1, 0:512], onesb[:, :],
                                            esum3[:, :], start=True, stop=True),
               S["pe"])
        ctx.wait("vector", S["pe"], S["pe"].count)
        ctx.op("vector", lambda e: e.reciprocal(rd_row, pp3[0:1, 0:512]),
               S["ve"])
        ctx.wait("tensor", S["ve"], S["ve"].count)
        ctx.op("tensor", lambda e: e.matmul(pp3[:, 0:512], onesf[:, :], rd_row,
                                            start=True, stop=True), S["pe"])
        ctx.wait("vector", S["pe"], S["pe"].count)
        ctx.op("vector", lambda e: e.tensor_copy(bc_sb, pp3[:, 0:512]), S["ve"])
        bc_t = S["ve"].count
        av_ps = [pp[2][:, 0:512], pp3[:, 0:512]]
        av_ev = {}
        for dc in range(8):
            vsl = dc % 2
            gdc = s * 8 + dc
            if dc == 0:
                ctx.wait("gpsimd", S["pe"], grp_ld[len(groups) - 1])
            if gdc >= 2:
                ctx.wait("gpsimd", S["pe"], vload_rel[gdc - 2])
            vw = wgo3(v3b[vsl][:, :].rearrange("p (kb d) -> p kb d", d=128),
                      v3tloc[:, dc * 128:(dc + 1) * 128]
                      .rearrange("(kb p) d -> p kb d", p=128))
            ctx.wait("tensor", vw[0], vw[1])
            if dc >= 2:
                ctx.wait("tensor", S["ve"], av_ev[dc - 2])
            elif dc == 1:
                ctx.wait("tensor", S["ve"], bc_t)
            elif s > 0:
                ctx.wait("tensor", S["ve"], av_prev_tile)
            for kb in range(32):
                ctx.op("tensor", lambda e, vsl=vsl, kb=kb, ps=av_ps[dc % 2]:
                       e.matmul(ps, v3b[vsl][:, kb * 128:(kb + 1) * 128],
                                efull[:, kb * 512:(kb + 1) * 512],
                                start=(kb == 0), stop=(kb == 31)),
                       S["pe"] if kb == 31 else None)
            if gdc == 0:
                vload_rel = {}
            vload_rel[gdc] = S["pe"].count
            ctx.wait("vector", S["pe"], S["pe"].count)
            ctx.wait("vector", S["ve"], bc_t)
            ctx.op("vector", lambda e, dc=dc, s=s, ps=av_ps[dc % 2]:
                   e.tensor_tensor(
                big[:, dc * 2048 + s * 512:dc * 2048 + s * 512 + 512],
                ps, bc_sb, ALU.mult), S["ve"])
            av_ev[dc] = S["ve"].count
        av_prev_tile = S["ve"].count

    ctx.barrier_from("tensor", S["ve"])
    for ck in range(2):
        ctx.barrier_from("vector", *[c.sem for c in acc2])
        w_acc = {}

        def wo3_evac(oc, ps, og_tgt, ck=ck, w_acc=w_acc):
            slot = oc % 4
            ctx.wait("vector", S["peog"], og_tgt)
            if oc >= 4:
                ctx.wait("vector", acc2[slot].sem, w_acc[oc - 4])
            ctx.op("vector", lambda e, s=slot, p=ps:
                   e.tensor_copy(fst4[s], p[:, :]))
            ctx.op("vector", lambda e, s=slot, oc=oc:
                   e.tensor_scalar(fst4[s], fst4[s],
                                   bo3sb[:, oc:oc + 1], None, ALU.add), S["ev"])
            ctx.wait("gpsimd", S["ev"], S["ev"].count)
            acc2[slot].go(xA2[ck][oc * 128:(oc + 1) * 128, :],
                          fst4[slot], accum_op=ALU.add)
            w_acc[oc] = acc2[slot].sem.count

        emit_mm(ctx, f"wo3c{ck}", wo3, 8, NFC,
                lambda ic, ck=ck: big[:, ic * 2048 + ck * TOK:
                                      ic * 2048 + (ck + 1) * TOK],
                wo3_evac, S, pp=pp, goc=3, icb=8, wbufs=wbufs, tiled=True,
                wchans=M['wch6'])
        for _c in acc2:
            ctx.wait("gpsimd", _c.sem, _c.sem.count)
        ctx.dma("gpsimd", xout[:, ck * TOK:(ck + 1) * TOK], xA2[ck], S["st"])
    ctx.wait("gpsimd", S["st"], S["st"].count)


def host_tile_w(w):
    w = np.asarray(w)
    n_in = w.shape[0] // 128
    n_out = w.shape[1] // 128
    return np.ascontiguousarray(
        w.reshape(n_in, 128, n_out, 128).transpose(1, 2, 0, 3)
        .reshape(128, n_out * n_in * 128))


def host_fused_weights(ins):
    def T_(a):
        return np.ascontiguousarray(np.asarray(a).T.astype(np.float32,
                                                           copy=False))
    d = {}
    for l in range(2):
        d[f"wqkvT{l}"] = host_tile_w(np.concatenate(
            [np.asarray(ins[f"L{l}_attn_Wq"]).T,
             np.asarray(ins[f"L{l}_attn_Wk"]).T,
             np.asarray(ins[f"L{l}_attn_Wv"]).T], axis=1)
            .astype(np.float32, copy=False))
        d[f"woT{l}"] = host_tile_w(T_(ins[f"L{l}_attn_Wo"]))
    d["w0T"] = host_tile_w(T_(ins["L0_ffn_weight"]))
    d["bias0"] = np.ascontiguousarray(
        np.asarray(ins["L0_ffn_bias"]).reshape(NFC, 128).T)
    d["gwT"] = T_(ins["L0_ffn_gate_w"])
    d["ics"] = np.ascontiguousarray(np.asarray(ins["L0_ffn_in_channel_scale"]))
    d["ocs"] = np.ascontiguousarray(np.asarray(ins["L0_ffn_out_channel_scale"]))
    pw = np.asarray(ins["L1_ffn_packed_weight"])
    pb = np.ascontiguousarray(pw).view(np.int8)[..., 0::4].astype(np.int8,
                                                                  copy=True)
    pb = pb.view(np.uint8)
    d["pwT"] = np.ascontiguousarray(pb.transpose(1, 2, 0)).reshape(2048, 4096)
    scl = np.asarray(ins["L1_ffn_scales"]).reshape(4096, 32)
    d["sclT"] = np.ascontiguousarray(scl.T.astype(np.float32, copy=False))
    sel2 = np.zeros((2, 128), np.float32)
    sel2[0, 0:64] = 1.0
    sel2[1, 64:128] = 1.0
    d["sel2"] = sel2
    rb = np.arange(128)
    d["indT"] = (np.arange(D)[None, :] // 32 == rb[:, None]).astype(np.float32)
    d["rep4"] = (np.arange(128)[None, :] // 32 ==
                 np.arange(4)[:, None]).astype(np.float32)
    d["g4"] = (np.arange(128)[:, None] // 32 ==
               np.arange(4)[None, :]).astype(np.float32)
    mask = np.ascontiguousarray(np.asarray(ins["L2_attn_mask"]))
    d["mask8"] = np.ascontiguousarray(mask.view(np.int8)[:, 0::4]).view(np.uint8)
    d["w2aT"] = T_(ins["L2_attn_weight"])
    d["bias2"] = np.ascontiguousarray(
        np.asarray(ins["L2_attn_bias"]).reshape(NFC, 128).T)
    a2 = np.asarray(ins["L2_ffn_a2"])
    d["a2Tp"] = np.ascontiguousarray(
        a2.T.reshape(86, 128, 2).transpose(1, 0, 2).reshape(128, 172)
        .astype(np.float32))
    d["w1qT"] = host_tile_w(T_(ins["L2_ffn_w1q"]))
    d["w3qT"] = host_tile_w(T_(ins["L2_ffn_w3q"]))
    d["w2qT"] = T_(ins["L2_ffn_w2q"])
    d["a1T"] = T_(ins["L2_ffn_a1"])
    d["a3T"] = T_(ins["L2_ffn_a3"])
    d["b1T"] = T_(ins["L2_ffn_b1"])
    d["b3T"] = T_(ins["L2_ffn_b3"])
    d["b2T"] = T_(ins["L2_ffn_b2"])
    d["wqkv3"] = host_tile_w(
        np.asarray(ins["L3_attn_Wqkv"]).astype(np.float32))
    d["bqkv3"] = np.ascontiguousarray(
        np.asarray(ins["L3_attn_bqkv"]).reshape(24, 128).T)
    d["kern8"] = np.ascontiguousarray(
        np.asarray(ins["L3_ffn_kernel"]).reshape(512, D))
    d["wo3"] = host_tile_w(
        np.asarray(ins["L3_attn_Wo"]).astype(np.float32))
    d["bo3"] = np.ascontiguousarray(
        np.asarray(ins["L3_attn_bo"]).reshape(NFC, 128).T)
    d["cvec"] = host_cvec()
    d["ident"] = np.eye(128, dtype=np.float32)
    d["trimask"] = host_trimask()
    return d


def host_fused_percore(x0, core, nonces):
    slot = core % 2
    batch = core // 4
    xT2 = np.ascontiguousarray(
        x0[batch, 2048 * slot:2048 * slot + 2048].T.astype(np.float32,
                                                           copy=False))
    inv = 1.0 / ROPE_THETA ** (np.arange(0, 128, 2, dtype=np.float64) / 128.0)
    pos = np.arange(2048 * slot, 2048 * slot + 2048, dtype=np.float64)
    fr = pos[None, :] * np.concatenate([inv, inv])[:, None]
    sgn = np.where(np.arange(128) < 64, -1.0, 1.0)[:, None]
    ct = np.zeros((128, 4096), np.float32)
    st = np.zeros((128, 4096), np.float32)
    ct[:, 0:2048] = np.cos(fr)
    st[:, 0:2048] = np.sin(fr) * sgn
    p = np.arange(128, dtype=np.int32)[:, None]
    iq_sc = (slot * 6144 + np.arange(48, dtype=np.int32)[None, :] * 128 + p)
    iq_q = np.empty((128, 32), np.int32)
    for hl in range(16):
        for w in range(2):
            iq_q[:, hl * 2 + w] = (w * 6144 + (16 * slot + hl) * 128 + p[:, 0])
    iq_kv = np.empty((128, 16), np.int32)
    iq_v = np.empty((128, 16), np.int32)
    for kvh in range(4):
        for w in range(2):
            iq_kv[:, kvh * 2 + w] = (w * 6144 + 4096 +
                                     (4 * slot + kvh) * 128 + p[:, 0])
            iq_v[:, kvh * 2 + w] = (w * 6144 + 5120 +
                                    (4 * slot + kvh) * 128 + p[:, 0])
    iao_sc = np.empty((128, 32), np.int32)
    for hl in range(16):
        for r in range(2):
            iao_sc[:, hl * 2 + r] = (r * 4096 + slot * 2048 +
                                     hl * 128 + p[:, 0])
    iao_g = (slot * 4096 + np.arange(32, dtype=np.int32)[None, :] * 128 + p)
    i3_sc = np.empty((128, 16), np.int32)
    for j in range(16):
        i3_sc[:, j] = (slot * 2048 + (0 if j < 8 else 1024) +
                       (j % 8) * 128 + p[:, 0])
    i3_k = np.empty((128, 16), np.int32)
    i3_v = np.empty((128, 16), np.int32)
    for dc in range(8):
        for w in range(2):
            i3_k[:, dc * 2 + w] = w * 2048 + dc * 128 + p[:, 0]
            i3_v[:, dc * 2 + w] = w * 2048 + 1024 + dc * 128 + p[:, 0]
    fidx = np.array([[slot], [slot + 2]], np.int32)
    fvals = np.empty((2, 64 * PairComm.NROUNDS), np.int32)
    for r in range(PairComm.NROUNDS):
        fvals[:, 64 * r:64 * (r + 1)] = nonces[r]
    return dict(xT2=xT2, ctab4=ct, stab4=st,
                iq_sc=iq_sc.astype(np.int32), iq_q=iq_q, iq_kv=iq_kv,
                iq_v=iq_v, iao_sc=iao_sc, iao_g=iao_g.astype(np.int32),
                i3_sc=i3_sc, i3_k=i3_k, i3_v=i3_v, fidx=fidx, fvals=fvals)


def kernel_fused(ins):
    x0 = np.asarray(ins["L0_attn_hidden_states"], np.float32)
    W = host_fused_weights(ins)
    nc = build_fused().finalize()
    nonces = np.random.randint(1, 2 ** 31 - 1, size=PairComm.NROUNDS)
    in_maps = [dict(W, **host_fused_percore(x0, c, nonces))
               for c in range(NCORES)]
    res = run_bass_kernel_spmd(nc, in_maps, core_ids=list(range(NCORES))).results
    out = np.empty((2, 4096, 4096), np.float32)
    for c in (0, 1, 4, 5):
        out[c // 4, 2048 * (c % 2):2048 * (c % 2) + 2048] = \
            np.asarray(res[c]["xout"]).T
    return out
